# version 17
# baseline (speedup 1.0000x reference)
import numpy as np
import ml_dtypes
from contextlib import ExitStack

import concourse.bass as bass
import concourse.mybir as mybir
from concourse.bass_utils import run_bass_kernel_spmd

F32 = mybir.dt.float32
BF16 = mybir.dt.bfloat16
AF = mybir.ActivationFunctionType
ALU = mybir.AluOpType
AX = mybir.AxisListType

B, T, D = 4, 4096, 2048
TC = 2048
NT = 16
NH = 16
DK = DV = 128
M = 64
EPS = 1e-5
SCALE = DK ** -0.5
E32 = float(np.exp(32.0))
EM32 = float(np.exp(-32.0))
NE = 32

ENGS = ("sync", "gpsimd", "scalar", "vector", "tensor")
SEMS = ("pe", "act", "dve", "cc",
        "hx0", "hx1", "ho", "ho1", "hl0", "hl1", "hs2", "hout", "hg", "hof",
        "sc", "sw0", "sw1", "so", "sl", "sp", "sr", "sq", "sv", "ss", "sdbg")


class Em:

    def __init__(self):
        self.items = {e: [] for e in ENGS}
        self.cnt = {s: 0 for s in SEMS}

    def v(self, sem):
        return self.cnt[sem]

    def add(self, eng, fn=None, wait=(), inc=None, n=1):
        self.items[eng].append((list(wait), fn, inc, n))
        if inc:
            self.cnt[inc] += n
            return self.cnt[inc]
        return None

    def replay(self, block, sems):
        for ename in ENGS:
            items = self.items[ename]

            def body(eng, items=items):
                seen = {}
                for (waits, fn, inc, n) in items:
                    for (s, v) in waits:
                        if v > 0 and seen.get(s, 0) < v:
                            eng.wait_ge(sems[s], v)
                            seen[s] = v
                    ins = fn(eng) if fn is not None else None
                    if inc:
                        ins.then_inc(sems[inc], n)
            getattr(block, ename)(body)


def _bf(x):
    return np.asarray(x, dtype=ml_dtypes.bfloat16)


_DIAG = {}


def build_nc(dbg=False, n_heads=NH):
    nc = bass.Bass("TRN2", target_bir_lowering=False, debug=False)

    def din(name, shape, dt=F32):
        return nc.dram_tensor(name, list(shape), dt, kind="ExternalInput").ap()

    xin = din("xin", (TC, D))
    wqt = din("wqt", (NH, 128, 16, 128))
    wkt = din("wkt", (NH, 128, 16, 128))
    wvt = din("wvt", (NH, 128, 16, 128))
    wgt = din("wgt", (NH, 128, 16, 128))
    wot = din("wot", (NH, 128, D))
    ws1t = din("ws1t", (128, 16, 16))
    ws2t = din("ws2t", (16, NH * M))
    bs2 = din("bs2", (1, NH * M))
    gnw = din("gnw", (1, DV))
    simm = din("simm", (16, 128, NE))
    gates = din("gates", (1, NE))
    wv2 = din("wv2", (16, 128, D))
    wo2 = din("wo2", (16, 128, D))
    cosf = din("cosf", (128, TC))
    sins = din("sins", (128, TC))
    spref = din("spref", (128, 1))

    out = nc.dram_tensor("out", [TC, D], F32, kind="ExternalOutput").ap()

    ident = nc.inline_tensor(_bf(np.eye(128)), name="c_ident").ap()
    triu = nc.inline_tensor(_bf(np.triu(np.ones((128, 128)))), name="c_triu").ap()
    onec_b = nc.inline_tensor(_bf(np.ones((128, 1))), name="c_onec").ap()
    onec_f = nc.inline_tensor(np.ones((128, 1), np.float32), name="c_onecf").ap()
    oner_b = nc.inline_tensor(_bf(np.ones((1, 128))), name="c_oner").ap()
    oner_f = nc.inline_tensor(np.ones((1, 128), np.float32), name="c_onerf").ap()
    eps_c = nc.inline_tensor(np.full((128, 1), EPS, np.float32), name="c_eps").ap()
    istk = nc.inline_tensor(_bf(np.vstack([np.eye(M), np.eye(M)])),
                            name="c_istk").ap()
    sel_np = np.zeros((NE, 16 * 128), np.float32)
    for kc in range(16):
        sel_np[2 * kc, kc * 128: kc * 128 + 64] = 1.0
        sel_np[2 * kc + 1, kc * 128 + 64: kc * 128 + 128] = 1.0
    sel = nc.inline_tensor(_bf(sel_np), name="c_sel").ap()

    og_d = nc.dram_tensor("og_d", [NH, NT, 128, 128], BF16).ap()
    x2_d = nc.dram_tensor("x2_d", [NT, 128, D], F32).ap()
    l2f_d = nc.dram_tensor("l2f_d", [NT, 16, 128, 128], BF16).ap()
    st_in = [nc.dram_tensor(f"st_in{i}", [1, 128 * 256], BF16).ap()
             for i in range(2)]
    st_out = [nc.dram_tensor(f"st_out{i}", [2, 128 * 256], BF16).ap()
              for i in range(2)]
    sq_in = nc.dram_tensor("sq_in", [1, D], F32).ap()
    sq_out = nc.dram_tensor("sq_out", [2, D], F32).ap()

    dbg_outs = {}
    if dbg:
        for nm, shape in [("dbg_hs", (128, 2048)), ("dbg_q", (128, TC)),
                          ("dbg_k", (128, TC)), ("dbg_es", (128, NT * M)),
                          ("dbg_o", (128, TC)), ("dbg_x2", (128, D)),
                          ("dbg_r", (1, NE)), ("dbg_enz", (128, NT * M)),
                          ("dbg_u", (128, NT * M)), ("dbg_weff", (128, 16 * M)),
                          ("dbg_wod", (64, D)), ("dbg_comb", (64, 128)),
                          ("dbg_l2f", (128, D)), ("dbg_ffn", (128, D))]:
            dbg_outs[nm] = nc.dram_tensor(nm, list(shape), F32,
                                          kind="ExternalOutput").ap()

    em = Em()
    A, V = em.add, em.v
    RG = [[0, 1], [2, 3], [4, 5], [6, 7]]

    with ExitStack() as stk:
        e = stk.enter_context

        def sb(name, shape, dt):
            return e(nc.sbuf_tensor(name, list(shape), dt))

        hs_fm = sb("hs_fm", (128, 16 * TC), BF16)
        raw = sb("raw", (128, TC), BF16)
        rope2 = sb("rope2", (128, TC), BF16)
        q_fm = sb("q_fm", (128, TC), BF16)
        k_fm = sb("k_fm", (128, TC), BF16)
        v_tm = sb("v_tm", (128, TC), BF16)
        sg = sb("sg", (128, TC), BF16)
        o_sb = sb("o_sb", (128, TC), BF16)
        ws1s = o_sb[:, 0:16 * 16]
        istks = o_sb[:, 256:256 + M]
        es_tm = sb("es_tm", (128, NT * M), BF16)
        es_fm = sb("es_fm", (128, 8 * 128), BF16)
        u_tm = sb("u_tm", (128, NT * M), BF16)
        u_fm = sb("u_fm", (128, 8 * 128), BF16)
        rsn = sb("rsn", (128, NT * M), BF16)
        psn = sb("psn", (128, 8 * 128), BF16)
        rf32 = sb("rf32", (128, NT * M), F32)
        pf32u = sb("pf32u", (64, NT * 128), F32)
        ktm = sb("ktm", (128, 2 * 128), BF16)
        stm = sb("stm", (128, 2 * 128), BF16)
        sq_bf = sb("sq_bf", (128, TC), BF16)
        sels = sq_bf[0:32, 0:16 * 128]
        carry = sq_bf[0:1, 0:(NT + 1) * M]
        wbuf = [sb(f"wbuf{p}", (128, 4 * TC), BF16) for p in range(2)]
        cosb = sb("cosb", (128, TC), BF16)
        sinb = sb("sinb", (128, TC), BF16)
        xf32 = [sb(f"xf32{p}", (128, D), F32) for p in range(2)]
        f32a = sb("f32a", (128, D), F32)
        f32b = sb("f32b", (128, D), F32)
        hs_tm = sb("hs_tm", (128, D), BF16)
        t1s = sb("t1s", (16, TC), BF16)
        ws2s = sb("ws2s", (16, NH * M), BF16)
        bs2s = sb("bs2s", (1, NH * M), BF16)
        gnws = sb("gnws", (1, DV), BF16)
        stg = sb("stg", (128, 256), BF16)
        rcv = sb("rcv", (128, 256), BF16)
        cons = sb("cons", (128, 128), BF16)
        trius = sb("trius", (128, 128), BF16)
        onecb = sb("onecb", (128, 1), BF16)
        onecf = sb("onecf", (128, 1), F32)
        onerb = sb("onerb", (1, 128), BF16)
        epsc = sb("epsc", (128, 1), F32)
        sprefs = sb("sprefs", (128, 1), F32)
        stat = sb("stat", (128, 42), F32)
        bn6 = stat[:, 16:40]
        sls = sb("sls", (128, 16), F32)
        srep = sb("srep", (128, 16), F32)
        stat2 = srep
        gcol = sls[0:32, :]
        gcbf = sb("gcbf", (32, 2), BF16)
        sim_sb = None
        grow = None
        rwv_t = v_tm
        wefft_t = k_fm
        wodyn_t = q_fm
        comb_t = sg
        l2fs = [wbuf[p][:, 0:D] for p in range(2)]

        smf1 = f32a[:, 0:NT * M]
        smf2 = f32a[:, NT * M:2 * NT * M]
        enza = f32b[:, 0:NT * M]
        enzb = f32b[:, NT * M:2 * NT * M]
        rowa = f32a[0:1, 0:2048]
        rowb = hs_tm[0:1, 0:2048]
        crow = f32a[0:1, 0:(NT + 1) * M]
        onerf = xf32[0][0:1, 0:128]
        sim_sb = xf32[1][:, 0:16 * NE]
        grow = f32a[0:1, 1024:1536]

        pb = [e(nc.psum_tensor(f"pb{i}", (128, 512), F32)) for i in range(4)]
        pt0 = e(nc.psum_tensor("pt0", (128, 1024), BF16))
        pw = e(nc.psum_tensor("pw", (128, 1024), BF16))
        pf = [e(nc.psum_tensor(f"pf{i}", (128, 512), F32)) for i in range(2)]
        ptq = [pt0, pw]

        def tpb(ping):
            return "pt0" if ping == 0 else "pw"

        ptA = [pt0[:, 0:128], pw[:, 0:128]]
        ptV = [pt0[:, 128:256], pw[:, 128:256]]
        ptE = [pt0[:, 256:384], pw[:, 256:384]]
        ptK = [pt0[:, 384:512], pw[:, 384:512]]
        pwRs = [pf[0][:, 0:64], pf[1][:, 0:64]]
        pwPs = [pf[1][0:64, 64:192], pf[0][0:64, 64:192]]
        pwCB = pf[0][:, 192:256]

        sems = {s: e(nc.semaphore(s)) for s in SEMS}

        fresh = {}
        groups_of = {}
        guards = {}

        def MM(eng_wait, out_ap, lhsT, rhs, bank, stop=True, inc=None,
               group=None):
            g_ = group or bank
            groups_of.setdefault(bank, set()).add(g_)
            w = list(eng_wait) + guards.pop(bank, [])
            st_flag = not fresh.get(g_, False)
            fresh[g_] = True
            return A("tensor",
                     lambda t, o=out_ap, l=lhsT, r=rhs, sf=st_flag, sp=stop:
                     t.matmul(o, l, r, start=sf, stop=sp),
                     wait=w, inc=inc)

        def TP(eng_wait, out_ap, in_ap, bank, inc="pe"):
            w = list(eng_wait) + guards.pop(bank, [])
            groups_of.setdefault(bank, set()).add(bank)
            fresh[bank] = True
            return A("tensor",
                     lambda t, o=out_ap, i=in_ap: t.transpose(o, i, cons[:]),
                     wait=w, inc=inc)

        def bank_read(bank, sem, val):
            guards.setdefault(bank, []).append((sem, val))

        def bank_cycle(*banks):
            for b_ in banks:
                for g_ in groups_of.get(b_, {b_}):
                    fresh[g_] = False
                fresh[b_] = False

        def slot(i):
            if i < 8:
                return pb[2][:, 64 * i:64 * i + 64], "pb2", f"pb2:{i}"
            return pb[3][:, 64 * (i - 8):64 * (i - 8) + 64], "pb3", f"pb3:{i}"

        for dst, src in [(cons, ident), (trius, triu), (onecb, onec_b),
                         (onecf, onec_f), (onerb, oner_b),
                         (sprefs, spref),
                         (cosb, cosf), (sinb, sins),
                         (ws2s, ws2t), (bs2s, bs2), (gnws, gnw),
                         (epsc, eps_c)]:

            A("gpsimd", lambda g, d=dst, s=src: g.dma_start(out=d[:], in_=s),
              inc="sc", n=16)
        A("gpsimd", lambda g: g.dma_start(
            out=ws1s[:], in_=ws1t.rearrange("p a b -> p (a b)")),
          inc="sc", n=16)
        sd_const = V("sc")

        def wload(h, p):
            for j, wt in enumerate((wqt, wkt, wvt, wgt)):
                A("gpsimd",
                  lambda g, wt=wt, h=h, p=p, j=j: g.dma_start(
                      out=wbuf[p][:, TC * j:TC * (j + 1)],
                      in_=wt[h].rearrange("p c o -> p (c o)")),
                  inc=f"sw{p}", n=16)
            return (f"sw{p}", V(f"sw{p}"))

        sd_w = [("sw0", 0)] * (NH + 1)
        sd_w[0] = wload(0, 0)

        xld_guard = [0, 0]
        hd_x = [("hx0", 0)] * NT

        def xload(tc, extra=()):
            buf = tc % 2
            A("sync", lambda s, tc=tc, buf=buf: s.dma_start(
                out=xf32[buf][:], in_=xin[128 * tc:128 * (tc + 1), :]),
              wait=[("dve", xld_guard[buf])] + list(extra),
              inc=f"hx{buf}", n=16)
            return (f"hx{buf}", V(f"hx{buf}"))

        hd_x[0] = xload(0)
        hd_x[1] = xload(1)
        tp_ev = [0, 0]
        for tc in range(NT):
            buf = tc % 2
            w_x = [hd_x[tc], ("sc", sd_const)]
            for q in range(4):
                A("vector", lambda v, buf=buf, q=q: v.bn_stats(
                    bn6[:, 6 * q:6 * (q + 1)],
                    xf32[buf][:, 512 * q:512 * (q + 1)]),
                  wait=(w_x + [("dve", V("dve"))]) if q == 0 else [],
                  inc="dve")
            A("vector", lambda v: v.bn_aggr(stat[:, 2:4], bn6[:]),
              wait=[("dve", V("dve"))], inc="dve")
            A("scalar", lambda s: s.activation(
                stat[:, 6:7], stat[:, 3:4], AF.Ln, bias=epsc[:]),
              wait=[("dve", V("dve"))], inc="act")
            A("scalar", lambda s: s.activation(
                stat[:, 7:8], stat[:, 6:7], AF.Exp, scale=-0.5),
              wait=[("act", V("act"))], inc="act")
            act_rstd = V("act")
            A("vector", lambda v, buf=buf: v.tensor_scalar(
                hs_tm[:], xf32[buf][:], stat[:, 2:3], stat[:, 7:8],
                ALU.subtract, ALU.mult),
              wait=[("act", act_rstd)], inc="dve")
            dve_hs = V("dve")
            xld_guard[buf] = dve_hs
            if tc + 2 < NT:
                hd_x[tc + 2] = xload(tc + 2)
            for cc in range(16):
                ping = cc % 2
                TP([("dve", dve_hs)], ptA[ping],
                   hs_tm[:, 128 * cc:128 * (cc + 1)], tpb(ping))
                pe_tp = V("pe")
                A("vector", lambda v, tc=tc, cc=cc, ping=ping: v.tensor_copy(
                    hs_fm[:, TC * cc + 128 * tc: TC * cc + 128 * (tc + 1)],
                    ptA[ping]),
                  wait=[("pe", pe_tp)], inc="dve")
                tp_ev[ping] = V("dve")
                bank_read(tpb(ping), "dve", tp_ev[ping])
                bank_cycle(tpb(ping))
        dve_hs_all = V("dve")
        act_t1p = [0, 0]
        for tch in range(4):
            ping = tch % 2
            for cc in range(16):
                MM([("dve", dve_hs_all)] if cc == 0 else [],
                   pf[ping][0:16, :], ws1s[:, 16 * cc:16 * (cc + 1)],
                   hs_fm[:, TC * cc + 512 * tch: TC * cc + 512 * (tch + 1)],
                   f"pf{ping}", stop=(cc == 15),
                   inc="pe" if cc == 15 else None)
            pe_t1 = V("pe")
            A("scalar", lambda s, tch=tch, ping=ping: s.copy(
                t1s[:, 512 * tch:512 * (tch + 1)], pf[ping][0:16, :]),
              wait=[("pe", pe_t1)], inc="act")
            act_t1p[ping] = V("act")
            bank_read(f"pf{ping}", "act", act_t1p[ping])
            bank_cycle(f"pf{ping}")
        act_t1 = V("act")

        if dbg:
            A("gpsimd", lambda g: g.dma_start(out=dbg_outs["dbg_hs"],
                                            in_=hs_fm[:, 0:2048]),
              wait=[("dve", dve_hs_all)], inc="sdbg", n=16)

        pe_wread = [0, 0]
        raw_guard = [("dve", 0)]
        rope_guard = [0]
        og_sd = 0
        stg_rd = 0
        rcv_rd = 0
        cc_done = [0] * (NH + 2)
        rowb_rd = [0]
        vt_ev = [0, 0]
        esf_ev = [0, 0]
        ktm_ev = [0, 0]
        uf_ev = [0, 0]
        stm_rd = [0, 0]
        pe_okp = [0, 0]
        pe_Cp = [0, 0]
        pe_Cglob = 0
        dve_hs_all_ = dve_hs_all

        xb0 = xf32[0][:].bitcast(BF16)
        xb1 = xf32[1][:].bitcast(BF16)
        kfm = [k_fm, xb0[:, 0:2048]]
        vtm = [v_tm, xb0[:, 2048:4096]]
        estm = [es_tm, xb1[:, 0:1024]]
        esfm = [es_fm, xb1[:, 1024:2048]]
        carry = xb1[0:1, 2048:2048 + (NT + 1) * M]
        SPb = [[pb[2], pb[3]], [pb[0], pb[1]]]
        SPn = [["pb2", "pb3"], ["pb0", "pb1"]]

        def slot(i, p):
            b = SPn[p][i // 8]
            return (SPb[p][i // 8][:, 64 * (i % 8):64 * (i % 8) + 64], b,
                    f"{b}:s{i % 8}")

        HS = [dict() for _ in range(NH)]

        def projection(kind, h):
            hp = h % 2
            base = {"q": 0, "k": 1, "v": 2, "g": 3}[kind] * TC
            for tch in range(4):
                ping = tch % 2
                w0 = [sd_w[h], ("dve", dve_hs_all_)] if tch == 0 else []
                for cc in range(16):
                    MM(w0 if cc == 0 else [],
                       pf[ping][:, :],
                       wbuf[hp][:, base + 128 * cc:base + 128 * (cc + 1)],
                       hs_fm[:, TC * cc + 512 * tch:
                             TC * cc + 512 * (tch + 1)],
                       f"pf{ping}", stop=(cc == 15),
                       inc="pe" if cc == 15 else None)
                    w0 = []
                pe_pr = V("pe")
                A("scalar", lambda s, tch=tch, ping=ping: s.copy(
                    raw[:, 512 * tch:512 * (tch + 1)], pf[ping][:, :]),
                  wait=[("pe", pe_pr)] + list(raw_guard), inc="act")
                bank_read(f"pf{ping}", "act", V("act"))
                bank_cycle(f"pf{ping}")
            return V("act")

        def rope_into(dst, pe_war):
            act_raw = V("act")
            w = [("act", act_raw), ("sc", sd_const), ("pe", pe_war)]
            A("vector", lambda v, dst=dst: v.tensor_mul(
                dst[:], raw[:], cosb[:]),
              wait=w + [("dve", rope_guard[0])], inc="dve")
            A("vector", lambda v: v.tensor_mul(
                rope2[0:64, :], raw[64:128, :], sinb[64:128, :]),
              inc="dve")
            A("vector", lambda v: v.tensor_mul(
                rope2[64:128, :], raw[0:64, :], sinb[0:64, :]),
              inc="dve")
            dve_m = V("dve")
            A("vector", lambda v, dst=dst: v.tensor_add(
                dst[:], dst[:], rope2[:]),
              wait=[("dve", dve_m)], inc="dve")
            rope_guard[0] = V("dve")
            raw_guard[0] = ("dve", V("dve"))

        def eA1(h):
            st = HS[h]
            p = h % 2
            bank_cycle(*SPn[p])
            for i in range(NT):
                sl, bank, grp = slot(i, p)
                MM([("act", act_t1), ("sc", sd_const)] if i == 0 else [],
                   sl, t1s[:, 128 * i:128 * (i + 1)],
                   ws2s[:, M * h:M * (h + 1)], bank, stop=False, group=grp)
                MM([], sl, onerb[:], bs2s[:, M * h:M * (h + 1)], bank,
                   stop=True, inc="pe" if i == NT - 1 else None, group=grp)
            pe_s = V("pe")
            A("scalar", lambda s, p=p: s.activation(
                estm[p][:, 0:512], SPb[p][0][:, 0:512], AF.Exp),
              wait=[("pe", pe_s), ("pe", pe_okp[p])], inc="act")
            A("scalar", lambda s, p=p: s.activation(
                estm[p][:, 512:1024], SPb[p][1][:, 0:512], AF.Exp),
              inc="act")
            act_es0 = V("act")
            bank_read(SPn[p][0], "act", act_es0)
            bank_read(SPn[p][1], "act", act_es0)
            bank_cycle(*SPn[p])
            A("vector", lambda v, p=p: v.tensor_scalar(
                estm[p][:], estm[p][:], E32, None, ALU.min),
              wait=[("act", act_es0)], inc="dve")
            A("vector", lambda v, p=p: v.tensor_scalar(
                estm[p][:], estm[p][:], EM32, None, ALU.max),
              wait=[("dve", V("dve"))], inc="dve")
            st["dve_es"] = V("dve")

        def eCS(h, dve_f32a_free):
            st = HS[h]
            p = h % 2
            MM([("dve", st["dve_es"])], SPb[p][0][0:1, 0:512], onecb[:],
               estm[p][:, 0:512], SPn[p][0], stop=True, inc="pe",
               group=f"{SPn[p][0]}:cs{h}")
            pe_cs1 = V("pe")
            A("scalar", lambda s, p=p: s.copy(crow[0:1, M:M + 512],
                                              SPb[p][0][0:1, 0:512]),
              wait=[("pe", pe_cs1), ("dve", dve_f32a_free)], inc="act")
            cs_ev1 = V("act")
            bank_read(SPn[p][0], "act", cs_ev1)
            MM([("dve", st["dve_es"])], SPb[p][1][0:1, 0:512], onecb[:],
               estm[p][:, 512:1024], SPn[p][1], stop=True, inc="pe",
               group=f"{SPn[p][1]}:cs{h}")
            pe_cs2 = V("pe")
            A("scalar", lambda s, p=p: s.copy(crow[0:1, M + 512:M + 1024],
                                              SPb[p][1][0:1, 0:512]),
              wait=[("pe", pe_cs2)], inc="act")
            cs_ev2 = V("act")
            bank_read(SPn[p][1], "act", cs_ev2)
            A("vector", lambda v: v.memset(crow[0:1, 0:M], 0.0),
              wait=[("dve", dve_f32a_free)], inc="dve")
            for ci in range(1, NT + 1):
                A("vector", lambda v, ci=ci: v.tensor_add(
                    crow[0:1, M * ci:M * (ci + 1)],
                    crow[0:1, M * ci:M * (ci + 1)],
                    crow[0:1, M * (ci - 1):M * ci]),
                  wait=[("act", cs_ev1 if ci <= 8 else cs_ev2),
                        ("dve", V("dve"))], inc="dve")
            A("vector", lambda v: v.tensor_copy(carry[:], crow[:]),
              wait=[("dve", V("dve"))], inc="dve")
            st["dve_carry"] = V("dve")

        def eKV(h):
            st = HS[h]
            p = h % 2
            projection("k", h)
            rope_into(kfm[p], pe_okp[p])
            st["dve_rope"] = V("dve")
            projection("v", h)
            act_vraw = V("act")
            for i in range(NT):
                ping = i % 2
                TP([("act", act_vraw), ("pe", pe_Cp[p])],
                   ptV[ping], raw[:, 128 * i:128 * (i + 1)], tpb(ping))
                pe_v = V("pe")
                A("vector", lambda v, i=i, ping=ping, p=p: v.tensor_copy(
                    vtm[p][:, 128 * i:128 * (i + 1)], ptV[ping]),
                  wait=[("pe", pe_v)], inc="dve")
                vt_ev[ping] = V("dve")
                bank_read(tpb(ping), "dve", vt_ev[ping])
                bank_cycle(tpb(ping))
            st["dve_vtm"] = V("dve")
            raw_guard[0] = ("pe", V("pe"))

        def eES(h):
            st = HS[h]
            p = h % 2
            for j in range(8):
                ping = j % 2
                TP([("dve", st["dve_es"]), ("pe", pe_Cp[p])],
                   ptE[ping], estm[p][:, 128 * j:128 * (j + 1)], tpb(ping))
                pe_e = V("pe")
                A("vector", lambda v, j=j, ping=ping, p=p: v.tensor_copy(
                    esfm[p][:, 128 * j:128 * (j + 1)], ptE[ping]),
                  wait=[("pe", pe_e)], inc="dve")
                esf_ev[ping] = V("dve")
                bank_read(tpb(ping), "dve", esf_ev[ping])
                bank_cycle(tpb(ping))
            st["dve_esf"] = V("dve")

        def eA3(h):
            st = HS[h]
            p = h % 2
            A("vector", lambda v: v.memset(rf32[:, 0:M], 0.0), inc="dve")
            A("vector", lambda v: v.memset(pf32u[:, 0:128], 0.0), inc="dve")
            chain = V("dve")
            for i in range(NT):
                ping = i % 2
                TP([("dve", st["dve_rope"]), ("dve", ktm_ev[ping])],
                   ptK[ping], kfm[p][:, 128 * i:128 * (i + 1)], tpb(ping))
                pe_kt = V("pe")
                A("vector", lambda v, ping=ping: v.tensor_copy(
                    ktm[:, 128 * ping:128 * (ping + 1)], ptK[ping]),
                  wait=[("pe", pe_kt)], inc="dve")
                ktm_ev[ping] = V("dve")
                bank_read(tpb(ping), "dve", ktm_ev[ping])
                bank_cycle(tpb(ping))
                wR = [("dve", ktm_ev[ping])]
                if i == 0:
                    wR += [("dve", st["dve_es"]), ("dve", st["dve_vtm"])]
                bkR, bkP = ("pf0", "pf1") if ping == 0 else ("pf1", "pf0")
                MM(wR, pwRs[ping], ktm[:, 128 * ping:128 * (ping + 1)],
                   estm[p][:, M * i:M * (i + 1)], bkR, stop=True,
                   inc="pe", group=f"pfR:{h}:{i}")
                pe_R = V("pe")
                MM([], pwPs[ping], estm[p][:, M * i:M * (i + 1)],
                   vtm[p][:, 128 * i:128 * (i + 1)], bkP, stop=True,
                   inc="pe", group=f"pfP:{h}:{i}")
                pe_P = V("pe")
                if i < NT - 1:
                    A("vector", lambda v, i=i, ping=ping: v.tensor_add(
                        rf32[:, M * (i + 1):M * (i + 2)],
                        rf32[:, M * i:M * (i + 1)], pwRs[ping]),
                      wait=[("pe", pe_R), ("dve", chain)], inc="dve")
                    bank_read(bkR, "dve", V("dve"))
                    A("vector", lambda v, i=i, ping=ping: v.tensor_add(
                        pf32u[:, 128 * (i + 1):128 * (i + 2)],
                        pf32u[:, 128 * i:128 * (i + 1)], pwPs[ping]),
                      wait=[("pe", pe_P), ("dve", V("dve"))], inc="dve")
                    chain = V("dve")
                    bank_read(bkP, "dve", chain)
                else:
                    st["pe_R15"], st["pe_P15"] = pe_R, pe_P
                sl, bank, grp = slot(i, p)
                MM([("dve", st["dve_carry"])] if i == 0 else [], sl, trius[:],
                   estm[p][:, M * i:M * (i + 1)], bank, stop=False, group=grp)
                MM([], sl, onerb[:], carry[0:1, M * i:M * (i + 1)], bank,
                   stop=True, inc="pe" if i == NT - 1 else None, group=grp)
            st["pe_state"] = V("pe")
            st["dve_chain"] = chain
            A("vector", lambda v: v.tensor_add(
                stg[:, 0:64], rf32[:, (NT - 1) * M:NT * M], pwRs[1]),
              wait=[("dve", st["dve_chain"]), ("pe", st["pe_R15"]),
                    ("sp", stg_rd)],
              inc="dve")
            bank_read("pf1", "dve", V("dve"))
            A("vector", lambda v: v.tensor_add(
                stg[0:64, 64:192], pf32u[:, (NT - 1) * 128:NT * 128],
                pwPs[1]),
              wait=[("pe", st["pe_P15"])], inc="dve")
            A("vector", lambda v: v.tensor_add(
                stg[64:128, 64:192], pf32u[:, (NT - 1) * 128:NT * 128],
                pwPs[1]),
              inc="dve")
            st["dve_stg"] = V("dve")
            bank_read("pf0", "dve", st["dve_stg"])
            MM([("dve", st["dve_carry"])], pwCB, onerb[:],
               carry[0:1, NT * M:(NT + 1) * M], "pf0", stop=True, inc="pe",
               group=f"pfCB:{h}")
            pe_cb = V("pe")
            A("vector", lambda v: v.tensor_copy(stg[:, 192:256], pwCB),
              wait=[("pe", pe_cb), ("sp", stg_rd)], inc="dve")
            st["dve_stg2"] = V("dve")
            bank_read("pf0", "dve", st["dve_stg2"])

        def ePush(h):
            st = HS[h]
            dve_stg = st["dve_stg"]
            dve_stg2 = st["dve_stg2"]
            A("gpsimd", lambda g, h=h: g.dma_start(
                out=st_in[h % 2].rearrange("o (p f) -> (o p) f", p=128),
                in_=stg[:]),
              wait=[("dve", dve_stg2), ("dve", dve_stg),
                    ("cc", cc_done[h - 2] if h >= 2 else 0)],
              inc="sp", n=16)
            sd_push = V("sp")
            A("gpsimd", lambda g, h=h: g.collective_compute(
                "AllGather", ALU.bypass, replica_groups=RG,
                ins=[st_in[h % 2].opt()], outs=[st_out[h % 2].opt()]),
              wait=[("sp", sd_push)], inc="cc")
            cc_done[h] = V("cc")
            A("gpsimd", lambda g, h=h: g.dma_start(
                out=rcv[:], in_=st_out[h % 2][0:1, :].rearrange(
                    "o (p f) -> (o p) f", p=128)),
              wait=[("cc", cc_done[h]), ("dve", rcv_rd)], inc="sr", n=16)
            st["sd_rcv"] = V("sr")
            return sd_push

        def eQG(h, pe_q_war, dve_sg_war):
            st = HS[h]
            projection("q", h)
            rope_into(q_fm, pe_q_war)
            st["dve_qrope"] = V("dve")
            projection("g", h)
            act_graw = V("act")
            A("scalar", lambda s: s.activation(sg[:], raw[:], AF.Silu),
              wait=[("act", act_graw), ("dve", dve_sg_war)], inc="act")
            st["act_silu"] = V("act")
            raw_guard[0] = ("act", st["act_silu"])
            pe_wread[h % 2] = V("pe")

        def eFold(h):
            st = HS[h]
            p = h % 2
            A("vector", lambda v: v.scalar_tensor_tensor(
                rsn[:].rearrange("p (i m) -> p i m", m=M),
                rcv[:, 0:64].unsqueeze(1).broadcast_to([128, NT, M]),
                sprefs[:],
                rf32[:, 0:NT * M].rearrange("p (i m) -> p i m", m=M),
                ALU.mult, ALU.add),
              wait=[("sr", st["sd_rcv"]), ("dve", st["dve_chain"]),
                    ("pe", pe_okp[1 - p])],
              inc="dve")
            pf32v = pf32u[:, 0:NT * 128].rearrange(
                "p (a two f) -> p a two f", two=2, f=128)
            A("vector", lambda v: v.scalar_tensor_tensor(
                psn[0:64, :].rearrange("p (j one f) -> p j one f",
                                       one=1, f=128),
                rcv[0:64, 64:192].unsqueeze(1).unsqueeze(2)
                .broadcast_to([64, 8, 1, 128]),
                sprefs[0:64, :],
                pf32v[:, :, 0:1, :],
                ALU.mult, ALU.add),
              wait=[("pe", pe_Cglob)], inc="dve")
            A("vector", lambda v: v.scalar_tensor_tensor(
                psn[64:128, :].rearrange("p (j one f) -> p j one f",
                                         one=1, f=128),
                rcv[0:64, 64:192].unsqueeze(1).unsqueeze(2)
                .broadcast_to([64, 8, 1, 128]),
                sprefs[0:64, :],
                pf32v[:, :, 1:2, :],
                ALU.mult, ALU.add),
              inc="dve")
            st["dve_snaps"] = V("dve")
            A("vector", lambda v, p=p: v.scalar_tensor_tensor(
                enza[:, 0:512].rearrange("p (i m) -> p i m", m=M),
                rcv[:, 192:256].unsqueeze(1).broadcast_to([128, 8, M]),
                sprefs[:],
                SPb[p][0][:, 0:512].rearrange("p (i m) -> p i m", m=M),
                ALU.mult, ALU.add),
              wait=[("sr", st["sd_rcv"]), ("pe", st["pe_state"])],
              inc="dve")
            A("vector", lambda v, p=p: v.scalar_tensor_tensor(
                enza[:, 512:1024].rearrange("p (i m) -> p i m", m=M),
                rcv[:, 192:256].unsqueeze(1).broadcast_to([128, 8, M]),
                sprefs[:],
                SPb[p][1][:, 0:512].rearrange("p (i m) -> p i m", m=M),
                ALU.mult, ALU.add),
              inc="dve")
            dve_c = V("dve")
            st["dve_c"] = dve_c
            bank_read(SPn[p][0], "dve", dve_c)
            bank_read(SPn[p][1], "dve", dve_c)
            bank_cycle(*SPn[p])
            A("scalar", lambda s: s.activation(enzb[:], enza[:], AF.Ln),
              wait=[("dve", dve_c)], inc="act")
            A("scalar", lambda s: s.activation(enzb[:], enzb[:], AF.Exp,
                                               scale=-1.0),
              wait=[("act", V("act"))], inc="act")
            st["act_enz"] = V("act")

        def eB2(h):
            st = HS[h]
            p = h % 2
            for i in range(NT):
                ping = i % 2
                sl, bank, grp = slot(i, p)
                MM([("dve", st["dve_snaps"]), ("act", st["act_enz"]),
                    ("dve", st["dve_qrope"])] if i == 0 else [],
                   sl, q_fm[:, 128 * i:128 * (i + 1)],
                   rsn[:, M * i:M * (i + 1)], bank, stop=False, group=grp)
                MM([], pf[ping][:, 0:128], kfm[p][:, 128 * i:128 * (i + 1)],
                   q_fm[:, 128 * i:128 * (i + 1)], f"pf{ping}", stop=True,
                   inc="pe", group=f"kq:{h}:{i}")
                pe_st = V("pe")
                A("vector", lambda v, ping=ping: v.tensor_mul(
                    stm[:, 128 * ping:128 * (ping + 1)], pf[ping][:, 0:128],
                    trius[:]),
                  wait=[("pe", pe_st), ("pe", stm_rd[ping])], inc="dve")
                dve_st = V("dve")
                bank_read(f"pf{ping}", "dve", dve_st)
                MM([("dve", dve_st)], sl,
                   stm[:, 128 * ping:128 * (ping + 1)],
                   estm[p][:, M * i:M * (i + 1)], bank, stop=True,
                   inc="pe", group=grp)
                stm_rd[ping] = V("pe")
            st["pe_ok"] = V("pe")
            pe_okp[p] = st["pe_ok"]

        def eB3(h):
            st = HS[h]
            p = h % 2
            A("vector", lambda v, p=p: v.scalar_tensor_tensor(
                smf1[:, 0:512], SPb[p][0][:, 0:512], SCALE, enzb[:, 0:512],
                ALU.mult, ALU.mult),
              wait=[("pe", st["pe_ok"]), ("act", st["act_enz"])], inc="dve")
            A("vector", lambda v, p=p: v.scalar_tensor_tensor(
                smf1[:, 512:1024], SPb[p][1][:, 0:512],
                SCALE, enzb[:, 512:1024],
                ALU.mult, ALU.mult),
              inc="dve")
            dve_okz = V("dve")
            bank_read(SPn[p][0], "dve", dve_okz)
            bank_read(SPn[p][1], "dve", dve_okz)
            bank_cycle(*SPn[p])
            A("vector", lambda v: v.reduce_max(
                stat[:, 0:NT], smf1.rearrange("p (i m) -> p i m", m=M),
                axis=AX.X, negate=True),
              wait=[("dve", dve_okz)], inc="dve")
            A("vector", lambda v: v.tensor_tensor(
                smf1.rearrange("p (i m) -> p i m", m=M),
                smf1.rearrange("p (i m) -> p i m", m=M),
                stat[:, 0:NT].unsqueeze(2).broadcast_to([128, NT, M]),
                ALU.add),
              wait=[("dve", V("dve"))], inc="dve")
            dve_xm = V("dve")
            A("scalar", lambda s: s.activation(smf2, smf1, AF.Exp),
              wait=[("dve", dve_xm)], inc="act")
            act_ex = V("act")
            A("vector", lambda v: v.reduce_sum(
                stat2[:, 0:NT], smf2.rearrange("p (i m) -> p i m", m=M),
                axis=AX.X),
              wait=[("act", act_ex)], inc="dve")
            A("vector", lambda v: v.reciprocal(stat[:, 0:NT], stat2[:, 0:NT]),
              wait=[("dve", V("dve"))], inc="dve")
            A("vector", lambda v: v.tensor_tensor(
                smf2.rearrange("p (i m) -> p i m", m=M),
                smf2.rearrange("p (i m) -> p i m", m=M),
                stat[:, 0:NT].unsqueeze(2).broadcast_to([128, NT, M]),
                ALU.mult),
              wait=[("dve", V("dve"))], inc="dve")
            A("vector", lambda v: v.tensor_mul(u_tm[:], smf2, enzb),
              wait=[("dve", V("dve")), ("pe", pe_Cglob)], inc="dve")
            st["dve_u"] = V("dve")
            for j in range(8):
                ping = j % 2
                TP([("dve", st["dve_u"]), ("pe", pe_Cglob)],
                   ptE[ping], u_tm[:, 128 * j:128 * (j + 1)], tpb(ping))
                pe_u = V("pe")
                A("vector", lambda v, j=j, ping=ping: v.tensor_copy(
                    u_fm[:, 128 * j:128 * (j + 1)], ptE[ping]),
                  wait=[("pe", pe_u)], inc="dve")
                uf_ev[ping] = V("dve")
                bank_read(tpb(ping), "dve", uf_ev[ping])
                bank_cycle(tpb(ping))
            st["dve_uf"] = V("dve")

        def eC(h):
            st = HS[h]
            p = h % 2
            obank = SPn[p][0]
            opp = SPb[p][0]
            for i in range(NT):
                ping = i % 2
                pslc = psn[64 * (i % 2):64 * (i % 2) + 64,
                           128 * (i // 2):128 * (i // 2) + 128]
                eslc = esfm[p][64 * (i % 2):64 * (i % 2) + 64,
                              128 * (i // 2):128 * (i // 2) + 128]
                uslc = u_fm[64 * (i % 2):64 * (i % 2) + 64,
                            128 * (i // 2):128 * (i // 2) + 128]
                MM([("dve", st["dve_uf"]), ("dve", st["dve_esf"])]
                   if i == 0 else [],
                   pf[ping][:, 0:128], eslc, uslc, f"pf{ping}", stop=True,
                   inc="pe", group=f"w2:{h}:{i}")
                pe_w2 = V("pe")
                A("vector", lambda v, ping=ping: v.tensor_mul(
                    stm[:, 128 * ping:128 * (ping + 1)], pf[ping][:, 0:128],
                    trius[:]),
                  wait=[("pe", pe_w2), ("pe", stm_rd[ping])], inc="dve")
                dve_w2 = V("dve")
                bank_read(f"pf{ping}", "dve", dve_w2)
                MM([("dve", st["dve_snaps"])], opp[:, 0:128], pslc, uslc,
                   obank, stop=False, group=f"o:{h}:{i}")
                MM([("dve", dve_w2)], opp[:, 0:128],
                   vtm[p][:, 128 * i:128 * (i + 1)],
                   stm[:, 128 * ping:128 * (ping + 1)], obank, stop=True,
                   inc="pe", group=f"o:{h}:{i}")
                stm_rd[ping] = V("pe")
                pe_o = V("pe")
                A("scalar", lambda s, i=i, p=p: s.copy(
                    o_sb[:, 128 * i:128 * (i + 1)], opp[:, 0:128]),
                  wait=[("pe", pe_o), ("so", og_sd)], inc="act")
                o_ev = V("act")
                bank_read(obank, "act", o_ev)
            st["act_o"] = V("act")
            st["pe_C"] = V("pe")
            pe_Cp[p] = st["pe_C"]
            return st["pe_C"]

        def eE(h):
            st = HS[h]
            p = h % 2
            A("scalar", lambda s: s.activation(sq_bf[:], o_sb[:], AF.Square),
              wait=[("act", st["act_o"])], inc="act")
            act_sq = V("act")
            sum_banks = [(SPn[p][0], SPb[p][0], 0), (SPn[p][1], SPb[p][1], 2)]
            bank_cycle(*SPn[p])
            for (bk, pp_, j0) in sum_banks:
                MM([("act", act_sq)] if j0 == 0 else [],
                   pp_[0:1, 0:512], onecb[:],
                   sq_bf[:, 512 * j0:512 * (j0 + 1)],
                   bk, stop=True, inc="pe", group=f"E:{h}:{j0}")
            pe_sq = V("pe")
            for (bk, pp_, j0) in sum_banks:
                A("scalar", lambda s, j0=j0, pp_=pp_: s.activation(
                    rowa[0:1, 512 * j0:512 * (j0 + 1)], pp_[0:1, 0:512],
                    AF.Ln, bias=epsc[0:1, 0:1], scale=1.0 / DV),
                  wait=[("pe", pe_sq), ("dve", st["dve_u"]),
                        ("dve", HS[h + 1].get("dve_carry", 0)
                         if h + 1 < len(HS) else 0)] if j0 == 0 else [],
                  inc="act")
                bank_read(bk, "act", V("act"))
            for (bk, pp_, j0) in sum_banks:
                j1 = j0 + 1
                MM([], pp_[0:1, 0:512], onecb[:],
                   sq_bf[:, 512 * j1:512 * (j1 + 1)],
                   bk, stop=True, inc="pe", group=f"E:{h}:{j1}")
            pe_sq2 = V("pe")
            for (bk, pp_, j0) in sum_banks:
                j1 = j0 + 1
                A("scalar", lambda s, j1=j1, pp_=pp_: s.activation(
                    rowa[0:1, 512 * j1:512 * (j1 + 1)], pp_[0:1, 0:512],
                    AF.Ln, bias=epsc[0:1, 0:1], scale=1.0 / DV),
                  wait=[("pe", pe_sq2)] if j1 == 1 else [],
                  inc="act")
                bank_read(bk, "act", V("act"))
            A("scalar", lambda s: s.activation(
                rowb[0:1, :], rowa[0:1, :], AF.Exp, scale=-0.5),
              wait=[("act", V("act")), ("pe", rowb_rd[0])], inc="act")
            act_rstd = V("act")
            bank_cycle(*SPn[p])
            for j in range(4):
                gb = SPn[p][j % 2]
                gp = SPb[p][j % 2]
                MM([("act", act_rstd)] if j == 0 else [],
                   gp[:, :], gnws[:], rowb[0:1, 512 * j:512 * (j + 1)],
                   gb, stop=True, inc="pe", group=f"g:{h}:{j}")
                pe_g = V("pe")
                rowb_rd[0] = pe_g
                A("scalar", lambda s, j=j, gp=gp: s.copy(
                    sq_bf[:, 512 * j:512 * (j + 1)], gp[:, :]),
                  wait=[("pe", pe_g)], inc="act")
                bank_read(gb, "act", V("act"))
            act_grbc = V("act")
            A("vector", lambda v: v.tensor_mul(o_sb[:], o_sb[:], sg[:]),
              wait=[("act", act_grbc)], inc="dve")
            A("vector", lambda v: v.tensor_mul(o_sb[:], o_sb[:], sq_bf[:]),
              wait=[("dve", V("dve"))], inc="dve")
            dve_og = V("dve")
            st["dve_og"] = dve_og
            A("gpsimd", lambda g, h=h: g.dma_start(
                out=og_d[h].transpose([1, 0, 2]),
                in_=o_sb[:].rearrange("p (c t) -> p c t", t=128)),
              wait=[("dve", dve_og)], inc="so", n=16)
            return V("so")

        for h in range(n_heads + 1):
            live = h < n_heads
            if live:
                if h + 1 < n_heads:
                    A("gpsimd", lambda g, h=h: g.nop(),
                      wait=[("pe", pe_wread[(h + 1) % 2])])
                    sd_w[h + 1] = wload(h + 1, (h + 1) % 2)
                eA1(h)
            if h >= 1:
                eB2(h - 1)
            if live:
                eKV(h)
            if h >= 1:
                eB3(h - 1)
            if live:
                eCS(h, HS[h - 1]["dve_u"] if h >= 1 else 0)
                eES(h)
                eA3(h)
                stg_rd = ePush(h)
            if h >= 1:
                eC(h - 1)
                pe_Cglob = HS[h - 1]["pe_C"]
                og_sd = eE(h - 1)
            if live:
                eQG(h, HS[h - 1]["pe_ok"] if h >= 1 else 0,
                    HS[h - 1]["dve_og"] if h >= 1 else 0)
                eFold(h)
                rcv_rd = HS[h]["dve_c"]

        act_headend = V("act")
        dve_headend = V("dve")
        pe_headend = V("pe")
        sd_og = V("so")
        pe_headloop = V("pe")

        A("gpsimd", lambda g: g.nop(),
          wait=[("pe", pe_headloop), ("dve", V("dve")), ("act", V("act"))])
        for hh in range(NH):
            A("gpsimd", lambda g, hh=hh: g.dma_start(
                out=hs_fm[:, TC * hh:TC * (hh + 1)], in_=wot[hh]),
              inc="sv", n=16)
        sd_wo = V("sv")
        bank_cycle("pf0")
        A("vector", lambda v: v.memset(sls[:], 0.0),
          wait=[("dve", V("dve"))], inc="dve")
        sls_dve = V("dve")
        attn_ev = 0
        l2t_ev = [0, 0]
        pe_sln = 0
        hstm_sd = 0
        sqbf_rd = pe_headloop
        f32a_rd = dve_headend
        f32b_rd = [dve_headend]
        ogb = [raw, rope2]
        ogb_rd = [pe_headloop, pe_headloop]
        hd_ogs = [0, 0]
        hd_xs = [0, 0]

        def p3_load(tc):
            b = tc % 2
            hosem = "ho" if b == 0 else "ho1"
            A("sync", lambda s, tc=tc, b=b: s.dma_start(
                out=ogb[b][:].rearrange("p (h t) -> p h t", h=NH),
                in_=og_d[:, tc].transpose([1, 0, 2])),
              wait=[("so", sd_og), ("pe", ogb_rd[b]),
                    ("act", act_headend), ("dve", dve_headend)],
              inc=hosem, n=16)
            hd_ogs[b] = (hosem, V(hosem))
            hd_xs[b] = xload(tc, [("pe", pe_headend), ("act", act_headend),
                                  ("hs2", V("hs2"))])

        def p3_wo(tc):
            b = tc % 2
            bank_cycle("pb0", "pb1", "pb2", "pb3")
            for j in range(4):
                for hh in range(NH):
                    MM([hd_ogs[b], ("sv", sd_wo)]
                       if (j == 0 and hh == 0) else [],
                       pb[j][:, :], ogb[b][:, 128 * hh:128 * (hh + 1)],
                       hs_fm[:, TC * hh + 512 * j:TC * hh + 512 * (j + 1)],
                       f"pb{j}", stop=(hh == NH - 1),
                       inc="pe" if (j == 3 and hh == NH - 1) else None)
            ogb_rd[b] = V("pe")
            return ogb_rd[b]

        p3_load(0)
        pe_attns = {0: p3_wo(0)}
        for tc in range(NT):
            buf = tc % 2
            pe_attn = pe_attns[tc]
            hd_xr = hd_xs[buf]
            for j in range(4):
                A("scalar", lambda s, j=j: s.copy(
                    f32a[:, 512 * j:512 * (j + 1)], pb[j][:, :]),
                  wait=[("pe", pe_attn), ("dve", f32a_rd)] if j == 0 else [],
                  inc="act")
                bank_read(f"pb{j}", "act", V("act"))
            attn_ev = V("act")
            if tc + 1 < NT:
                p3_load(tc + 1)
                pe_attns[tc + 1] = p3_wo(tc + 1)

            def ln_stats(src_ap, wait0):
                for q in range(4):
                    A("vector", lambda v, q=q: v.bn_stats(
                        bn6[:, 6 * q:6 * (q + 1)],
                        src_ap[:, 512 * q:512 * (q + 1)]),
                      wait=(wait0 + [("dve", V("dve"))]) if q == 0 else [],
                      inc="dve")
                A("vector", lambda v: v.bn_aggr(stat[:, 2:4], bn6[:]),
                  wait=[("dve", V("dve"))], inc="dve")
                A("scalar", lambda s: s.activation(
                    stat[:, 6:7], stat[:, 3:4], AF.Ln, bias=epsc[:]),
                  wait=[("dve", V("dve"))], inc="act")
                A("scalar", lambda s: s.activation(
                    stat[:, 7:8], stat[:, 6:7], AF.Exp, scale=-0.5),
                  wait=[("act", V("act"))], inc="act")
                return V("act")

            act_r1 = ln_stats(f32a[:], [("act", attn_ev)])
            A("vector", lambda v: v.tensor_scalar(
                f32a[:], f32a[:], stat[:, 2:3], stat[:, 7:8],
                ALU.subtract, ALU.mult),
              wait=[("act", act_r1)], inc="dve")
            A("vector", lambda v, buf=buf: v.tensor_add(
                xf32[buf][:], xf32[buf][:], f32a[:]),
              wait=[("dve", V("dve")), hd_xr], inc="dve")
            dve_x2 = V("dve")
            f32a_rd = dve_x2
            A("sync", lambda s, tc=tc, buf=buf: s.dma_start(
                out=x2_d[tc], in_=xf32[buf][:]),
              wait=[("dve", dve_x2)], inc="hs2", n=16)
            hd_x2st = V("hs2")
            act_r2 = ln_stats(xf32[buf][:], [("dve", dve_x2)])
            A("vector", lambda v, buf=buf: v.tensor_scalar(
                sq_bf[:], xf32[buf][:], stat[:, 2:3], stat[:, 7:8],
                ALU.subtract, ALU.mult),
              wait=[("act", act_r2), ("pe", sqbf_rd)], inc="dve")
            dve_l2 = V("dve")
            xld_guard[buf] = dve_l2
            bank_cycle("pf0")
            for cc in range(16):
                MM([("dve", dve_l2), ("dve", sls_dve)] if cc == 0 else [],
                   pf[0][:, cc:cc + 1], sq_bf[:, 128 * cc:128 * (cc + 1)],
                   onecb[:], "pf0", stop=True,
                   inc="pe" if cc == 15 else None, group=f"pf0:c{cc}")
            pe_sln = V("pe")
            A("vector", lambda v: v.tensor_add(sls[:], sls[:],
                                               pf[0][:, 0:16]),
              wait=[("pe", pe_sln), ("dve", sls_dve)], inc="dve")
            sls_dve = V("dve")
            bank_read("pf0", "dve", sls_dve)
            for cc in range(16):
                ping = cc % 2
                TP([("dve", dve_l2)],
                   ptA[ping], sq_bf[:, 128 * cc:128 * (cc + 1)],
                   tpb(ping))
                pe_t = V("pe")
                A("vector", lambda v, cc=cc, ping=ping: v.tensor_copy(
                    hs_tm[:, 128 * cc:128 * (cc + 1)], ptA[ping]),
                  wait=[("pe", pe_t), ("sl", hstm_sd)], inc="dve")
                l2t_ev[ping] = V("dve")
                bank_read(tpb(ping), "dve", l2t_ev[ping])
                bank_cycle(tpb(ping))
            sqbf_rd = V("pe")
            hstm_act = V("dve")
            A("gpsimd", lambda g, tc=tc: g.dma_start(
                out=l2f_d[tc].transpose([1, 0, 2]),
                in_=hs_tm[:].rearrange("p (c t) -> p c t", t=128)),
              wait=[("dve", hstm_act)], inc="sl", n=16)
            hstm_sd = V("sl")
            if dbg and tc == 0:
                A("sync", lambda s, buf=buf: s.dma_start(
                    out=dbg_outs["dbg_x2"], in_=xf32[buf][:]),
                  wait=[("dve", dve_x2)], inc="sdbg", n=16)
        hd_wo = V("hs2")
        sd_l2f = V("sl")

        dve_sls = sls_dve
        bank_cycle("pf0")
        A("gpsimd", lambda g: g.dma_start(out=sels, in_=sel),
          wait=[("pe", V("pe")), ("act", V("act")), ("dve", V("dve"))],
          inc="ss", n=16)
        A("gpsimd", lambda g: g.dma_start(out=istks, in_=istk),
          inc="ss", n=16)
        sd_sels = V("ss")
        A("gpsimd", lambda g: g.dma_start(out=onerf, in_=oner_f),
          wait=[("dve", xld_guard[0]), ("hs2", hd_wo)], inc="hof", n=16)
        hd_onerf = V("hof")
        A("gpsimd", lambda g: g.dma_start(
            out=sq_in.rearrange("o (p f) -> (o p) f", p=128), in_=sls[:]),
          wait=[("dve", dve_sls)], inc="sq", n=16)
        A("gpsimd", lambda g: g.collective_compute(
            "AllGather", ALU.bypass, replica_groups=RG,
            ins=[sq_in.opt()], outs=[sq_out.opt()]),
          wait=[("sq", V("sq"))], inc="cc")
        cc_sq = V("cc")
        A("gpsimd", lambda g: g.dma_start(
            out=srep[:], in_=sq_out[0:1, :].rearrange("o (p f) -> (o p) f",
                                                      p=128)),
          wait=[("cc", cc_sq)], inc="sq", n=16)
        A("gpsimd", lambda g: g.dma_start(
            out=sls[:], in_=sq_out[1:2, :].rearrange("o (p f) -> (o p) f",
                                                     p=128)),
          wait=[("dve", dve_sls)], inc="sq", n=16)
        sd_sq = V("sq")
        A("vector", lambda v: v.tensor_add(srep[:], srep[:], sls[:]),
          wait=[("sq", sd_sq)], inc="dve")
        A("vector", lambda v: v.tensor_scalar(
            srep[:], srep[:], 1.0 / T, None, ALU.mult),
          wait=[("dve", V("dve"))], inc="dve")
        dve_srep = V("dve")
        A("gpsimd", lambda g: g.dma_start(
            out=sim_sb.rearrange("p (c e) -> p c e", e=NE),
            in_=simm.transpose([1, 0, 2])),
          wait=[("dve", xld_guard[1])], inc="ss", n=16)
        sd_sim = V("ss")
        A("scalar", lambda s: s.activation(f32a[:, 0:16 * NE], sim_sb,
                                           AF.Square),
          wait=[("ss", sd_sim), ("dve", f32a_rd)], inc="act")
        act_simsq = V("act")
        for cc in range(16):
            A("vector", lambda v, cc=cc: v.tensor_scalar(
                f32b[:, NE * cc:NE * (cc + 1)],
                sim_sb[:, NE * cc:NE * (cc + 1)],
                srep[:, cc:cc + 1], None, ALU.mult),
              wait=[("dve", dve_srep), ("ss", sd_sim),
                    ("dve", f32b_rd[0])] if cc == 0 else [],
              inc="dve")
        dve_ws = V("dve")
        bank_cycle("pf0", "pf1", "pb0", "pb1")
        for cc in range(16):
            MM([("dve", dve_ws)] if cc == 0 else [],
               pf[0][0:1, 0:NE], onecf[:], f32b[:, NE * cc:NE * (cc + 1)],
               "pf0", stop=(cc == 15), inc="pe" if cc == 15 else None)
            MM([("act", act_simsq)] if cc == 0 else [],
               pf[1][0:1, 0:NE], onecf[:], f32a[:, NE * cc:NE * (cc + 1)],
               "pf1", stop=(cc == 15), inc="pe" if cc == 15 else None)
            MM([("dve", dve_ws)] if cc == 0 else [],
               pb[0][0:NE, 0:1], f32b[:, NE * cc:NE * (cc + 1)], onecf[:],
               "pb0", stop=(cc == 15), inc="pe" if cc == 15 else None)
            MM([("act", act_simsq)] if cc == 0 else [],
               pb[1][0:NE, 0:1], f32a[:, NE * cc:NE * (cc + 1)], onecf[:],
               "pb1", stop=(cc == 15), inc="pe" if cc == 15 else None)
        pe_aff = V("pe")
        A("scalar", lambda s: s.activation(stat[:, 0:16], srep[:], AF.Square),
          wait=[("dve", dve_srep)], inc="act")
        act_ss = V("act")
        bank_cycle("pb2")
        MM([("act", act_ss)], pb[2][0:1, 0:16], onecf[:], stat[:, 0:16],
           "pb2", stop=True, inc="pe")
        pe_sn = V("pe")
        A("vector", lambda v: v.tensor_copy(grow[0:1, 0:16], pb[2][0:1, 0:16]),
          wait=[("pe", pe_sn)], inc="dve")
        bank_read("pb2", "dve", V("dve"))
        A("vector", lambda v: v.reduce_sum(gcol[0:1, 0:1], grow[0:1, 0:16],
                                           axis=AX.X),
          wait=[("dve", V("dve"))], inc="dve")
        A("scalar", lambda s: s.activation(gcol[0:1, 1:2], gcol[0:1, 0:1],
                                           AF.Ln),
          wait=[("dve", V("dve"))], inc="act")
        A("scalar", lambda s: s.activation(gcol[0:1, 2:3], gcol[0:1, 1:2],
                                           AF.Exp, scale=-0.5),
          wait=[("act", V("act"))], inc="act")
        act_rn = V("act")
        def gr(j):
            return grow[0:1, NE * j + 32:NE * (j + 1) + 32]
        A("scalar", lambda s: s.activation(grow[0:1, 0:NE], pf[1][0:1, 0:NE],
                                           AF.Ln),
          wait=[("pe", pe_aff)], inc="act")
        A("scalar", lambda s: s.activation(grow[0:1, 0:NE], grow[0:1, 0:NE],
                                           AF.Exp, scale=-0.5),
          wait=[("act", V("act"))], inc="act")
        act_cnr = V("act")
        bank_read("pf1", "act", act_cnr)
        A("scalar", lambda s: s.activation(gcol[:, 3:4], pb[1][0:NE, 0:1],
                                           AF.Ln),
          wait=[("pe", pe_aff)], inc="act")
        A("scalar", lambda s: s.activation(gcol[:, 3:4], gcol[:, 3:4],
                                           AF.Exp, scale=-0.5),
          wait=[("act", V("act"))], inc="act")
        act_cnc = V("act")
        bank_read("pb1", "act", act_cnc)
        A("vector", lambda v: v.tensor_mul(gr(0), pf[0][0:1, 0:NE],
                                           grow[0:1, 0:NE]),
          wait=[("act", act_cnr)], inc="dve")
        bank_read("pf0", "dve", V("dve"))
        A("vector", lambda v: v.tensor_scalar(
            gr(0), gr(0), gcol[0:1, 2:3], None, ALU.mult),
          wait=[("dve", V("dve")), ("act", act_rn)], inc="dve")
        dve_affr = V("dve")
        A("vector", lambda v: v.tensor_mul(gcol[:, 0:1], pb[0][0:NE, 0:1],
                                           gcol[:, 3:4]),
          wait=[("act", act_cnc)], inc="dve")
        bank_read("pb0", "dve", V("dve"))
        A("vector", lambda v: v.tensor_copy(gcbf[:, 0:1], gcol[:, 0:1]),
          wait=[("dve", V("dve"))], inc="dve")
        dve_acbf = V("dve")
        bank_cycle("pb3")
        MM([("dve", dve_acbf), ("act", act_rn), ("hof", hd_onerf)],
           pb[3][0:NE, 0:1],
           onerf[0:1, 0:NE], gcol[0:1, 2:3], "pb3", stop=True, inc="pe")
        pe_rnb = V("pe")
        A("vector", lambda v: v.tensor_mul(gcol[:, 0:1], gcol[:, 0:1],
                                           pb[3][0:NE, 0:1]),
          wait=[("pe", pe_rnb)], inc="dve")
        dve_affc = V("dve")
        bank_read("pb3", "dve", dve_affc)
        A("sync", lambda s: s.dma_start(out=gr(1), in_=gates),
          inc="hg", n=16)
        hd_g = V("hg")
        A("scalar", lambda s: s.activation(gr(1), gr(1), AF.Tanh, scale=0.5),
          wait=[("hg", hd_g)], inc="act")
        A("vector", lambda v: v.tensor_scalar(
            gr(1), gr(1), 0.5, 0.5, ALU.mult, ALU.add),
          wait=[("act", V("act"))], inc="dve")
        dve_sigr = V("dve")
        bank_cycle("pb3")
        MM([("dve", dve_sigr)], pb[3][0:NE, 0:1], gr(1),
           onecf[0:1, 0:1], "pb3", stop=True, inc="pe")
        pe_sigc = V("pe")
        A("vector", lambda v: v.tensor_copy(gcol[:, 1:2], pb[3][0:NE, 0:1]),
          wait=[("pe", pe_sigc)], inc="dve")
        bank_read("pb3", "dve", V("dve"))
        A("vector", lambda v: v.tensor_sub(gr(2), gr(0), gr(1)),
          wait=[("dve", dve_affr), ("dve", V("dve"))], inc="dve")
        A("vector", lambda v: v.tensor_relu(gr(3), gr(2)),
          wait=[("dve", V("dve"))], inc="dve")
        A("vector", lambda v: v.tensor_single_scalar(
            gr(4), gr(3), 0.0, ALU.is_gt),
          wait=[("dve", V("dve"))], inc="dve")
        dve_maskr = V("dve")
        A("vector", lambda v: v.tensor_sub(gcol[:, 2:3], gcol[:, 0:1],
                                           gcol[:, 1:2]),
          wait=[("dve", dve_affc)], inc="dve")
        dve_logc = V("dve")
        A("vector", lambda v: v.memset(sim_sb[0:32, 0:NE], 1.0),
          wait=[("act", act_simsq), ("dve", dve_ws)], inc="dve")
        A("vector", lambda v: v.tensor_scalar(
            sim_sb[0:32, 0:NE], sim_sb[0:32, 0:NE], gcol[0:32, 2:3], None,
            ALU.mult),
          wait=[("dve", V("dve")), ("dve", dve_logc)], inc="dve")
        dve_cbc = V("dve")
        bank_cycle("pb3")
        MM([("dve", dve_logc)], pb[3][0:NE, 0:NE], onerf[0:1, 0:NE],
           gr(2), "pb3", stop=True, inc="pe")
        pe_rbc = V("pe")
        A("vector", lambda v: v.tensor_tensor(
            sim_sb[0:32, NE:2 * NE], pb[3][0:NE, 0:NE], sim_sb[0:32, 0:NE],
            ALU.is_gt),
          wait=[("pe", pe_rbc), ("dve", dve_cbc)], inc="dve")
        bank_read("pb3", "dve", V("dve"))
        A("vector", lambda v: v.reduce_sum(
            gcol[:, 4:5], sim_sb[0:32, NE:2 * NE], axis=AX.X),
          wait=[("dve", V("dve"))], inc="dve")
        xld_guard[1] = V("dve")
        A("vector", lambda v: v.tensor_single_scalar(
            gcol[:, 5:6], gcol[:, 4:5], float(NE // 2) - 0.5, ALU.is_le),
          wait=[("dve", V("dve"))], inc="dve")
        dve_fbc = V("dve")
        A("vector", lambda v: v.reduce_sum(gcol[0:1, 6:7], gr(4), axis=AX.X),
          wait=[("dve", dve_maskr)], inc="dve")
        A("vector", lambda v: v.tensor_single_scalar(
            gcol[0:1, 7:8], gcol[0:1, 6:7], 0.5, ALU.is_lt),
          wait=[("dve", V("dve"))], inc="dve")
        dve_flag = V("dve")
        bank_cycle("pb3")
        MM([("dve", dve_flag)], pb[3][0:NE, 0:1], onerf[0:1, 0:NE],
           gcol[0:1, 7:8], "pb3", stop=True, inc="pe")
        pe_flb = V("pe")
        A("vector", lambda v: v.tensor_mul(gcol[:, 5:6], gcol[:, 5:6],
                                           pb[3][0:NE, 0:1]),
          wait=[("pe", pe_flb), ("dve", dve_fbc)], inc="dve")
        bank_read("pb3", "dve", V("dve"))
        xld_guard[0] = V("dve")
        A("vector", lambda v: v.tensor_single_scalar(
            gcol[:, 8:9], gcol[:, 2:3], 0.0, ALU.is_gt),
          wait=[("dve", dve_logc)], inc="dve")
        A("vector", lambda v: v.tensor_max(gcol[:, 8:9], gcol[:, 8:9],
                                           gcol[:, 5:6]),
          wait=[("dve", V("dve"))], inc="dve")
        dve_mfc = V("dve")
        A("vector", lambda v: v.tensor_copy(gcbf[:, 1:2], gcol[:, 8:9]),
          wait=[("dve", dve_mfc)], inc="dve")
        dve_mfb = V("dve")
        bank_cycle("pb3")
        MM([("dve", dve_mfb)], pb[3][0:1, 0:NE], gcbf[:, 1:2],
           cons[0:32, 0:NE], "pb3", stop=True, inc="pe")
        pe_mfr = V("pe")
        A("vector", lambda v: v.tensor_copy(gr(5), pb[3][0:1, 0:NE]),
          wait=[("pe", pe_mfr)], inc="dve")
        bank_read("pb3", "dve", V("dve"))
        A("vector", lambda v: v.tensor_mul(gr(6), gr(3), gr(5)),
          wait=[("dve", V("dve"))], inc="dve")
        A("vector", lambda v: v.tensor_scalar(
            gr(7), gr(5), -1.0, 1e30, ALU.add, ALU.mult),
          wait=[("dve", V("dve"))], inc="dve")
        A("vector", lambda v: v.tensor_add(gr(6), gr(6), gr(7)),
          wait=[("dve", V("dve"))], inc="dve")
        A("vector", lambda v: v.reduce_max(gcol[0:1, 9:10], gr(6),
                                           axis=AX.X, negate=True),
          wait=[("dve", V("dve"))], inc="dve")
        dve_ml = V("dve")
        A("scalar", lambda s: s.activation(gr(7), gr(6), AF.Exp,
                                           bias=gcol[0:1, 9:10]),
          wait=[("dve", dve_ml)], inc="act")
        A("vector", lambda v: v.reduce_sum(gcol[0:1, 10:11], gr(7), axis=AX.X),
          wait=[("act", V("act"))], inc="dve")
        A("vector", lambda v: v.reciprocal(gcol[0:1, 11:12],
                                           gcol[0:1, 10:11]),
          wait=[("dve", V("dve"))], inc="dve")
        A("vector", lambda v: v.tensor_scalar(
            gr(7), gr(7), gcol[0:1, 11:12], None, ALU.mult),
          wait=[("dve", V("dve"))], inc="dve")
        A("vector", lambda v: v.tensor_mul(gr(7), gr(7), gr(5)),
          wait=[("dve", V("dve"))], inc="dve")
        dve_r = V("dve")
        if dbg:
            A("gpsimd", lambda g: g.dma_start(out=dbg_outs["dbg_r"],
                                              in_=gr(7)),
              wait=[("dve", dve_r)], inc="sdbg", n=16)
        A("vector", lambda v: v.tensor_copy(rowb[0:1, 0:NE], gr(7)),
          wait=[("dve", dve_r), ("act", V("act")), ("sl", sd_l2f)], inc="dve")
        dve_rbf = V("dve")
        f32a_rd = dve_rbf
        bank_cycle("pb3")
        MM([("dve", dve_rbf)], pb[3][0:NE, 0:1], rowb[0:1, 0:NE],
           onecb[0:1, 0:1], "pb3", stop=True, inc="pe")
        pe_rc = V("pe")
        A("vector", lambda v: v.tensor_copy(gcbf[:, 0:1], pb[3][0:NE, 0:1]),
          wait=[("pe", pe_rc)], inc="dve")
        dve_rcol = V("dve")
        bank_read("pb3", "dve", dve_rcol)
        bank_cycle("pb1")
        for kc in range(16):
            MM([("dve", dve_rcol), ("ss", sd_sim)] if kc == 0 else [],
               pb[1][:, kc:kc + 1], sels[:, 128 * kc:128 * (kc + 1)],
               gcbf[:, 0:1], "pb1", stop=True,
               inc="pe" if kc == 15 else None, group=f"pb1:rt{kc}")
        pe_rt = V("pe")
        A("vector", lambda v: v.tensor_copy(stat[:, 0:16], pb[1][:, 0:16]),
          wait=[("pe", pe_rt)], inc="dve")
        dve_rt = V("dve")
        bank_read("pb1", "dve", dve_rt)
        for kc in range(16):
            A("vector", lambda v, kc=kc: v.tensor_scalar(
                rwv_t[:, M * kc:M * (kc + 1)], istks[:], stat[:, kc:kc + 1],
                None, ALU.mult),
              wait=[("dve", dve_rt), ("pe", pe_headloop),
                    ("ss", sd_sim)] if kc == 0 else [],
              inc="dve")
        dve_rwv = V("dve")
        pe_wo_done = V("pe")
        for kc in range(16):
            A("gpsimd", lambda g, kc=kc: g.dma_start(
                out=hs_fm[:, TC * kc:TC * (kc + 1)], in_=wv2[kc]),
              wait=[("pe", pe_wo_done)] if kc == 0 else [], inc="sv", n=16)
        sd_wv = V("sv")
        bank_cycle("pf0", "pf1")
        wef_pv = [0, 0]
        for cc in range(16):
            ping = cc % 2
            for kc in range(16):
                MM([("sv", sd_wv), ("dve", dve_rwv)] if kc == 0 else [],
                   pf[ping][:, 0:64],
                   hs_fm[:, TC * kc + 128 * cc:TC * kc + 128 * (cc + 1)],
                   rwv_t[:, M * kc:M * (kc + 1)], f"pf{ping}", stop=(kc == 15),
                   inc="pe" if kc == 15 else None)
            pe_wef = V("pe")
            A("scalar", lambda s, cc=cc, ping=ping: s.copy(
                wefft_t[:, M * cc:M * (cc + 1)], pf[ping][:, 0:64]),
              wait=[("pe", pe_wef)], inc="act")
            wef_pv[ping] = V("act")
            bank_read(f"pf{ping}", "act", wef_pv[ping])
            bank_cycle(f"pf{ping}")
        act_wef = V("act")
        pe_wef_all = V("pe")
        if dbg:
            A("gpsimd", lambda g: g.dma_start(out=dbg_outs["dbg_weff"],
                                              in_=wefft_t[:, 0:16 * M]),
              wait=[("act", act_wef)], inc="sdbg", n=16)
        for kc in range(16):
            A("gpsimd", lambda g, kc=kc: g.dma_start(
                out=hs_fm[:, TC * kc:TC * (kc + 1)], in_=wo2[kc]),
              wait=[("pe", pe_wef_all)] if kc == 0 else [], inc="sv", n=16)
        sd_wo2 = V("sv")
        for dc in range(4):
            ping = dc % 2
            for kc in range(16):
                MM([("sv", sd_wo2)] if kc == 0 else [],
                   pf[ping][0:64, :], rwv_t[:, M * kc:M * (kc + 1)],
                   hs_fm[:, TC * kc + 512 * dc:TC * kc + 512 * (dc + 1)],
                   f"pf{ping}", stop=(kc == 15),
                   inc="pe" if kc == 15 else None)
            pe_wod = V("pe")
            A("scalar", lambda s, dc=dc, ping=ping: s.copy(
                wodyn_t[0:64, 512 * dc:512 * (dc + 1)], pf[ping][0:64, :]),
              wait=[("pe", pe_wod)], inc="act")
            bank_read(f"pf{ping}", "act", V("act"))
            bank_cycle(f"pf{ping}")
        act_wod = V("act")
        if dbg:
            A("gpsimd", lambda g: g.dma_start(out=dbg_outs["dbg_wod"],
                                              in_=wodyn_t[0:64, 0:D]),
              wait=[("act", act_wod)], inc="sdbg", n=16)

        l2l_guard = [pe_headloop, pe_headloop]
        f32ab_rd = [f32a_rd, f32a_rd]
        comb_rd = 0
        fin_ev = 0
        out_dma = 0
        for tc in range(NT):
            buf = tc % 2
            A("sync", lambda s, tc=tc, buf=buf: s.dma_start(
                out=l2fs[buf][:].rearrange("p (c t) -> p c t", t=128),
                in_=l2f_d[tc].transpose([1, 0, 2])),
              wait=[("sl", sd_l2f), ("pe", l2l_guard[buf])],
              inc=f"hl{buf}", n=16)
            hd_l2 = (f"hl{buf}", V(f"hl{buf}"))
            A("sync", lambda s, tc=tc, buf=buf: s.dma_start(
                out=xf32[buf][:], in_=x2_d[tc]),
              wait=[("hs2", hd_wo), ("dve", xld_guard[buf]),
                    ("hout", V("hout"))],
              inc=f"hx{buf}", n=16)
            hd_x2t = (f"hx{buf}", V(f"hx{buf}"))
            bank_cycle("pb0")
            for cc in range(16):
                MM([hd_l2, ("act", act_wef), ("pe", comb_rd)]
                   if cc == 0 else [],
                   pb[0][0:64, 0:128], wefft_t[:, M * cc:M * (cc + 1)],
                   l2fs[buf][:, 128 * cc:128 * (cc + 1)], "pb0",
                   stop=(cc == 15), inc="pe" if cc == 15 else None)
            pe_comb = V("pe")
            l2l_guard[buf] = pe_comb
            A("scalar", lambda s: s.copy(comb_t[0:64, 0:128],
                                         pb[0][0:64, 0:128]),
              wait=[("pe", pe_comb), ("dve", dve_headend)], inc="act")
            act_comb = V("act")
            if dbg and tc == 0:
                A("gpsimd", lambda g: g.dma_start(out=dbg_outs["dbg_comb"],
                                                  in_=comb_t[0:64, 0:128]),
                  wait=[("act", act_comb)], inc="sdbg", n=16)
                A("gpsimd", lambda g, buf=buf: g.dma_start(
                    out=dbg_outs["dbg_l2f"], in_=l2fs[buf][:]),
                  wait=[hd_l2], inc="sdbg", n=16)
            bank_read("pb0", "act", act_comb)
            bank_cycle("pb0", "pb1", "pb2", "pb3")
            for j in range(4):
                MM([("act", act_comb), ("act", act_wod)] if j == 0 else [],
                   pb[j][:, :], comb_t[0:64, 0:128],
                   wodyn_t[0:64, 512 * j:512 * (j + 1)], f"pb{j}", stop=True,
                   inc="pe" if j == 3 else None)
            pe_fin = V("pe")
            comb_rd = pe_fin
            fbuf = [f32a, f32b][buf]
            for j in range(4):
                A("scalar", lambda s, j=j, fbuf=fbuf: s.copy(
                    fbuf[:, 512 * j:512 * (j + 1)], pb[j][:, :]),
                  wait=[("pe", pe_fin), ("dve", f32ab_rd[buf]),
                        ("pe", pe_aff)] if j == 0 else [],
                  inc="act")
                bank_read(f"pb{j}", "act", V("act"))
            fin_ev = V("act")
            if dbg and tc == 0:
                A("gpsimd", lambda g, fbuf=fbuf: g.dma_start(
                    out=dbg_outs["dbg_ffn"], in_=fbuf[:]),
                  wait=[("act", fin_ev)], inc="sdbg", n=16)

            act_r3 = ln_stats(fbuf[:], [("act", fin_ev)])
            A("vector", lambda v, fbuf=fbuf: v.tensor_scalar(
                fbuf[:], fbuf[:], stat[:, 2:3], stat[:, 7:8],
                ALU.subtract, ALU.mult),
              wait=[("act", act_r3)], inc="dve")
            A("vector", lambda v, buf=buf, fbuf=fbuf: v.tensor_add(
                xf32[buf][:], xf32[buf][:], fbuf[:]),
              wait=[("dve", V("dve")), hd_x2t], inc="dve")
            dve_out = V("dve")
            xld_guard[buf] = dve_out
            f32ab_rd[buf] = dve_out
            A("sync", lambda s, tc=tc, buf=buf: s.dma_start(
                out=out[128 * tc:128 * (tc + 1), :], in_=xf32[buf][:]),
              wait=[("dve", dve_out)], inc="hout", n=16)
            out_dma = V("hout")

        A("sync", lambda s: s.nop(),
          wait=[("hout", out_dma), ("cc", V("cc"))] +
               [(s_, V(s_)) for s_ in SEMS if s_ not in
                ("pe", "act", "dve", "cc", "hout")])

        _DIAG["em"] = em
        block = e(nc.Block())
        em.replay(block, sems)

    return nc


_CACHE = {}


def _host_inputs(inputs):
    def wt(W):
        W = np.ascontiguousarray(np.asarray(W, np.float32))
        return np.ascontiguousarray(
            W.T.reshape(16, 128, 16, 128).transpose(2, 1, 0, 3))

    base = {
        "wqt": wt(inputs["L0_attn_Wq"]),
        "wkt": wt(inputs["L0_attn_Wk"]),
        "wvt": wt(inputs["L0_attn_Wv"]),
        "wgt": wt(inputs["L0_attn_Wg"]),
        "wot": np.ascontiguousarray(
            np.asarray(inputs["L0_attn_Wo"], np.float32).T).reshape(16, 128, D),
        "ws1t": np.ascontiguousarray(
            np.asarray(inputs["L0_attn_Ws1"], np.float32)
            .T.reshape(16, 128, 16).transpose(1, 0, 2)),
        "ws2t": np.ascontiguousarray(np.asarray(inputs["L0_attn_Ws2"],
                                                np.float32).T),
        "bs2": np.asarray(inputs["L0_attn_bs2"], np.float32).reshape(1, -1),
        "gnw": np.asarray(inputs["L0_attn_g_norm_weight"],
                          np.float32).reshape(1, -1),
        "simm": np.ascontiguousarray(
            np.asarray(inputs["L0_ffn_sim_matrix"],
                       np.float32).reshape(16, 128, NE)),
        "gates": np.asarray(inputs["L0_ffn_gates"], np.float32).reshape(1, -1),
        "wv2": np.ascontiguousarray(
            np.asarray(inputs["L0_ffn_w_v"], np.float32).reshape(16, 128, D)),
        "wo2": np.ascontiguousarray(
            np.asarray(inputs["L0_ffn_o_weights"],
                       np.float32).reshape(16, 128, D)),
    }

    x = np.asarray(inputs["L0_attn_hidden_states"], np.float32)
    inv = 1.0 / 10000.0 ** (np.arange(0, DK, 2, dtype=np.float64) / DK)
    maps = []
    for c in range(8):
        b, half = c // 2, c % 2
        pos = np.arange(half * TC, (half + 1) * TC, dtype=np.float64)
        ang = pos[None, :] * inv[:, None]
        cos = np.cos(ang).astype(np.float32)
        sin = np.sin(ang).astype(np.float32)
        m = dict(base)
        m["xin"] = np.ascontiguousarray(x[b, half * TC:(half + 1) * TC])
        m["cosf"] = np.concatenate([cos, cos], 0)
        m["sins"] = np.concatenate([sin, -sin], 0)
        m["spref"] = np.full((128, 1), float(half), np.float32)
        maps.append(m)
    return maps


def kernel(**inputs):
    _ = inputs["L0_ffn_w_q"], inputs["L0_ffn_w_k"]
    if "nc" not in _CACHE:
        _CACHE["nc"] = build_nc()
    nc = _CACHE["nc"]
    maps = _host_inputs(inputs)
    res = run_bass_kernel_spmd(nc, maps, core_ids=list(range(8)))
    out = np.empty((B, T, D), np.float32)
    for c in range(8):
        b, half = c // 2, c % 2
        out[b, half * TC:(half + 1) * TC] = res.results[c]["out"]
    return out


# version 18 (final)
# speedup vs baseline: 1.0187x; 1.0187x over previous
import numpy as np
import ml_dtypes
from contextlib import ExitStack

import concourse.bass as bass
import concourse.mybir as mybir
from concourse.bass_utils import run_bass_kernel_spmd

F32 = mybir.dt.float32
BF16 = mybir.dt.bfloat16
AF = mybir.ActivationFunctionType
ALU = mybir.AluOpType
AX = mybir.AxisListType

B, T, D = 4, 4096, 2048
TC = 2048
NT = 16
NH = 16
DK = DV = 128
M = 64
EPS = 1e-5
SCALE = DK ** -0.5
E32 = float(np.exp(32.0))
EM32 = float(np.exp(-32.0))
NE = 32

ENGS = ("sync", "gpsimd", "scalar", "vector", "tensor")
SEMS = ("pe", "act", "dve", "cc",
        "hx0", "hx1", "ho", "ho1", "hl0", "hl1", "hs2", "hout", "hg", "hof",
        "sc", "sw0", "sw1", "so", "sl", "sp", "sr", "sq", "sv", "ss", "sdbg")


class Em:

    def __init__(self):
        self.items = {e: [] for e in ENGS}
        self.cnt = {s: 0 for s in SEMS}

    def v(self, sem):
        return self.cnt[sem]

    def add(self, eng, fn=None, wait=(), inc=None, n=1):
        self.items[eng].append((list(wait), fn, inc, n))
        if inc:
            self.cnt[inc] += n
            return self.cnt[inc]
        return None

    def replay(self, block, sems):
        for ename in ENGS:
            items = self.items[ename]

            def body(eng, items=items):
                seen = {}
                for (waits, fn, inc, n) in items:
                    for (s, v) in waits:
                        if v > 0 and seen.get(s, 0) < v:
                            eng.wait_ge(sems[s], v)
                            seen[s] = v
                    ins = fn(eng) if fn is not None else None
                    if inc:
                        ins.then_inc(sems[inc], n)
            getattr(block, ename)(body)


def _bf(x):
    return np.asarray(x, dtype=ml_dtypes.bfloat16)


_DIAG = {}


def build_nc(dbg=False, n_heads=NH):
    nc = bass.Bass("TRN2", target_bir_lowering=False, debug=False)

    def din(name, shape, dt=F32):
        return nc.dram_tensor(name, list(shape), dt, kind="ExternalInput").ap()

    xin = din("xin", (TC, D))
    wqt = din("wqt", (NH, 128, 16, 128))
    wkt = din("wkt", (NH, 128, 16, 128))
    wvt = din("wvt", (NH, 128, 16, 128))
    wgt = din("wgt", (NH, 128, 16, 128))
    wot = din("wot", (NH, 128, D))
    ws1t = din("ws1t", (128, 16, 16))
    ws2t = din("ws2t", (16, NH * M))
    bs2 = din("bs2", (1, NH * M))
    gnw = din("gnw", (1, DV))
    simm = din("simm", (16, 128, NE))
    gates = din("gates", (1, NE))
    wv2 = din("wv2", (16, 128, D))
    wo2 = din("wo2", (16, 128, D))
    cosf = din("cosf", (128, TC))
    sins = din("sins", (128, TC))
    spref = din("spref", (128, 1))

    out = nc.dram_tensor("out", [TC, D], F32, kind="ExternalOutput").ap()

    ident = nc.inline_tensor(_bf(np.eye(128)), name="c_ident").ap()
    triu = nc.inline_tensor(_bf(np.triu(np.ones((128, 128)))), name="c_triu").ap()
    onec_b = nc.inline_tensor(_bf(np.ones((128, 1))), name="c_onec").ap()
    onec_f = nc.inline_tensor(np.ones((128, 1), np.float32), name="c_onecf").ap()
    oner_b = nc.inline_tensor(_bf(np.ones((1, 128))), name="c_oner").ap()
    oner_f = nc.inline_tensor(np.ones((1, 128), np.float32), name="c_onerf").ap()
    eps_c = nc.inline_tensor(np.full((128, 1), EPS, np.float32), name="c_eps").ap()
    istk = nc.inline_tensor(_bf(np.vstack([np.eye(M), np.eye(M)])),
                            name="c_istk").ap()
    sel_np = np.zeros((NE, 16 * 128), np.float32)
    for kc in range(16):
        sel_np[2 * kc, kc * 128: kc * 128 + 64] = 1.0
        sel_np[2 * kc + 1, kc * 128 + 64: kc * 128 + 128] = 1.0
    sel = nc.inline_tensor(_bf(sel_np), name="c_sel").ap()

    og_d = nc.dram_tensor("og_d", [NH, NT, 128, 128], BF16).ap()
    x2_d = nc.dram_tensor("x2_d", [NT, 128, D], F32).ap()
    l2f_d = nc.dram_tensor("l2f_d", [NT, 16, 128, 128], BF16).ap()
    st_in = [nc.dram_tensor(f"st_in{i}", [1, 128 * 256], BF16).ap()
             for i in range(2)]
    st_out = [nc.dram_tensor(f"st_out{i}", [2, 128 * 256], BF16).ap()
              for i in range(2)]
    sq_in = nc.dram_tensor("sq_in", [1, D], F32).ap()
    sq_out = nc.dram_tensor("sq_out", [2, D], F32).ap()

    dbg_outs = {}
    if dbg:
        for nm, shape in [("dbg_hs", (128, 2048)), ("dbg_q", (128, TC)),
                          ("dbg_k", (128, TC)), ("dbg_es", (128, NT * M)),
                          ("dbg_o", (128, TC)), ("dbg_x2", (128, D)),
                          ("dbg_r", (1, NE)), ("dbg_enz", (128, NT * M)),
                          ("dbg_u", (128, NT * M)), ("dbg_weff", (128, 16 * M)),
                          ("dbg_wod", (64, D)), ("dbg_comb", (64, 128)),
                          ("dbg_l2f", (128, D)), ("dbg_ffn", (128, D))]:
            dbg_outs[nm] = nc.dram_tensor(nm, list(shape), F32,
                                          kind="ExternalOutput").ap()

    em = Em()
    A, V = em.add, em.v
    RG = [[0, 1], [2, 3], [4, 5], [6, 7]]

    with ExitStack() as stk:
        e = stk.enter_context

        def sb(name, shape, dt):
            return e(nc.sbuf_tensor(name, list(shape), dt))

        hs_fm = sb("hs_fm", (128, 16 * TC), BF16)
        raw = sb("raw", (128, TC), BF16)
        rope2 = sb("rope2", (128, TC), BF16)
        q_fm = sb("q_fm", (128, TC), BF16)
        k_fm = sb("k_fm", (128, TC), BF16)
        v_tm = sb("v_tm", (128, TC), BF16)
        sg = sb("sg", (128, TC), BF16)
        o_sb = sb("o_sb", (128, TC), BF16)
        ws1s = o_sb[:, 0:16 * 16]
        istks = o_sb[:, 256:256 + M]
        es_tm = sb("es_tm", (128, NT * M), BF16)
        es_fm = sb("es_fm", (128, 8 * 128), BF16)
        u_tm = sb("u_tm", (128, NT * M), BF16)
        u_fm = sb("u_fm", (128, 8 * 128), BF16)
        rsn = sb("rsn", (128, NT * M), BF16)
        psn = sb("psn", (128, 8 * 128), BF16)
        rf32 = sb("rf32", (128, NT * M), F32)
        pf32u = sb("pf32u", (64, NT * 128), F32)
        ktm = sb("ktm", (128, 2 * 128), BF16)
        stm = sb("stm", (128, 2 * 128), BF16)
        sq_bf = sb("sq_bf", (128, TC), BF16)
        sels = sq_bf[0:32, 0:16 * 128]
        carry = sq_bf[0:1, 0:(NT + 1) * M]
        wbuf = [sb(f"wbuf{p}", (128, 4 * TC), BF16) for p in range(2)]
        cosb = sb("cosb", (128, TC), BF16)
        sinb = sb("sinb", (128, TC), BF16)
        xf32 = [sb(f"xf32{p}", (128, D), F32) for p in range(2)]
        f32a = sb("f32a", (128, D), F32)
        f32b = sb("f32b", (128, D), F32)
        hs_tm = sb("hs_tm", (128, D), BF16)
        t1s = sb("t1s", (16, TC), BF16)
        ws2s = sb("ws2s", (16, NH * M), BF16)
        bs2s = sb("bs2s", (1, NH * M), BF16)
        gnws = sb("gnws", (1, DV), BF16)
        stg = sb("stg", (128, 256), BF16)
        rcv = sb("rcv", (128, 256), BF16)
        cons = sb("cons", (128, 128), BF16)
        trius = sb("trius", (128, 128), BF16)
        onecb = sb("onecb", (128, 1), BF16)
        onecf = sb("onecf", (128, 1), F32)
        onerb = sb("onerb", (1, 128), BF16)
        epsc = sb("epsc", (128, 1), F32)
        sprefs = sb("sprefs", (128, 1), F32)
        stat = sb("stat", (128, 42), F32)
        bn6 = stat[:, 16:40]
        sls = sb("sls", (128, 16), F32)
        srep = sb("srep", (128, 16), F32)
        stat2 = srep
        gcol = sls[0:32, :]
        gcbf = sb("gcbf", (32, 2), BF16)
        sim_sb = None
        grow = None
        rwv_t = v_tm
        wefft_t = k_fm
        wodyn_t = q_fm
        comb_t = sg
        l2fs = [wbuf[p][:, 0:D] for p in range(2)]

        smf1 = f32a[:, 0:NT * M]
        smf2 = f32a[:, NT * M:2 * NT * M]
        enza = f32b[:, 0:NT * M]
        enzb = f32b[:, NT * M:2 * NT * M]
        rowa = f32a[0:1, 0:2048]
        rowb = hs_tm[0:1, 0:2048]
        crow = f32a[0:1, 0:(NT + 1) * M]
        onerf = xf32[0][0:1, 0:128]
        sim_sb = xf32[1][:, 0:16 * NE]
        grow = f32a[0:1, 1024:1536]

        pb = [e(nc.psum_tensor(f"pb{i}", (128, 512), F32)) for i in range(4)]
        pt0 = e(nc.psum_tensor("pt0", (128, 1024), BF16))
        pw = e(nc.psum_tensor("pw", (128, 1024), BF16))
        pf = [e(nc.psum_tensor(f"pf{i}", (128, 512), F32)) for i in range(2)]
        ptq = [pt0, pw]

        def tpb(ping):
            return "pt0" if ping == 0 else "pw"

        ptA = [pt0[:, 0:128], pw[:, 0:128]]
        ptV = [pt0[:, 128:256], pw[:, 128:256]]
        ptE = [pt0[:, 256:384], pw[:, 256:384]]
        ptK = [pt0[:, 384:512], pw[:, 384:512]]
        pwRs = [pf[0][:, 0:64], pf[1][:, 0:64]]
        pwPs = [pf[1][0:64, 64:192], pf[0][0:64, 64:192]]
        pwCB = pf[0][:, 192:256]

        sems = {s: e(nc.semaphore(s)) for s in SEMS}

        fresh = {}
        groups_of = {}
        guards = {}

        def MM(eng_wait, out_ap, lhsT, rhs, bank, stop=True, inc=None,
               group=None):
            g_ = group or bank
            groups_of.setdefault(bank, set()).add(g_)
            w = list(eng_wait) + guards.pop(bank, [])
            st_flag = not fresh.get(g_, False)
            fresh[g_] = True
            return A("tensor",
                     lambda t, o=out_ap, l=lhsT, r=rhs, sf=st_flag, sp=stop:
                     t.matmul(o, l, r, start=sf, stop=sp),
                     wait=w, inc=inc)

        def TP(eng_wait, out_ap, in_ap, bank, inc="pe"):
            w = list(eng_wait) + guards.pop(bank, [])
            groups_of.setdefault(bank, set()).add(bank)
            fresh[bank] = True
            return A("tensor",
                     lambda t, o=out_ap, i=in_ap: t.transpose(o, i, cons[:]),
                     wait=w, inc=inc)

        def bank_read(bank, sem, val):
            guards.setdefault(bank, []).append((sem, val))

        def bank_cycle(*banks):
            for b_ in banks:
                for g_ in groups_of.get(b_, {b_}):
                    fresh[g_] = False
                fresh[b_] = False

        def slot(i):
            if i < 8:
                return pb[2][:, 64 * i:64 * i + 64], "pb2", f"pb2:{i}"
            return pb[3][:, 64 * (i - 8):64 * (i - 8) + 64], "pb3", f"pb3:{i}"

        for dst, src in [(cons, ident), (trius, triu), (onecb, onec_b),
                         (onecf, onec_f), (onerb, oner_b),
                         (sprefs, spref),
                         (cosb, cosf), (sinb, sins),
                         (ws2s, ws2t), (bs2s, bs2), (gnws, gnw),
                         (epsc, eps_c)]:

            A("gpsimd", lambda g, d=dst, s=src: g.dma_start(out=d[:], in_=s),
              inc="sc", n=16)
        A("gpsimd", lambda g: g.dma_start(
            out=ws1s[:], in_=ws1t.rearrange("p a b -> p (a b)")),
          inc="sc", n=16)
        sd_const = V("sc")

        def wload(h, p):
            for j, wt in enumerate((wqt, wkt, wvt, wgt)):
                A("gpsimd",
                  lambda g, wt=wt, h=h, p=p, j=j: g.dma_start(
                      out=wbuf[p][:, TC * j:TC * (j + 1)],
                      in_=wt[h].rearrange("p c o -> p (c o)")),
                  inc=f"sw{p}", n=16)
            return (f"sw{p}", V(f"sw{p}"))

        sd_w = [("sw0", 0)] * (NH + 1)
        sd_w[0] = wload(0, 0)

        xld_guard = [0, 0]
        hd_x = [("hx0", 0)] * NT

        def xload(tc, extra=()):
            buf = tc % 2
            A("sync", lambda s, tc=tc, buf=buf: s.dma_start(
                out=xf32[buf][:], in_=xin[128 * tc:128 * (tc + 1), :]),
              wait=[("dve", xld_guard[buf])] + list(extra),
              inc=f"hx{buf}", n=16)
            return (f"hx{buf}", V(f"hx{buf}"))

        hd_x[0] = xload(0)
        hd_x[1] = xload(1)
        tp_ev = [0, 0]
        for tc in range(NT):
            buf = tc % 2
            w_x = [hd_x[tc], ("sc", sd_const)]
            for q in range(4):
                A("vector", lambda v, buf=buf, q=q: v.bn_stats(
                    bn6[:, 6 * q:6 * (q + 1)],
                    xf32[buf][:, 512 * q:512 * (q + 1)]),
                  wait=(w_x + [("dve", V("dve"))]) if q == 0 else [],
                  inc="dve")
            A("vector", lambda v: v.bn_aggr(stat[:, 2:4], bn6[:]),
              wait=[("dve", V("dve"))], inc="dve")
            A("scalar", lambda s: s.activation(
                stat[:, 6:7], stat[:, 3:4], AF.Ln, bias=epsc[:]),
              wait=[("dve", V("dve"))], inc="act")
            A("scalar", lambda s: s.activation(
                stat[:, 7:8], stat[:, 6:7], AF.Exp, scale=-0.5),
              wait=[("act", V("act"))], inc="act")
            act_rstd = V("act")
            A("vector", lambda v, buf=buf: v.tensor_scalar(
                hs_tm[:], xf32[buf][:], stat[:, 2:3], stat[:, 7:8],
                ALU.subtract, ALU.mult),
              wait=[("act", act_rstd)], inc="dve")
            dve_hs = V("dve")
            xld_guard[buf] = dve_hs
            if tc + 2 < NT:
                hd_x[tc + 2] = xload(tc + 2)
            for cc in range(16):
                ping = cc % 2
                TP([("dve", dve_hs)], ptA[ping],
                   hs_tm[:, 128 * cc:128 * (cc + 1)], tpb(ping))
                pe_tp = V("pe")
                A("vector", lambda v, tc=tc, cc=cc, ping=ping: v.tensor_copy(
                    hs_fm[:, TC * cc + 128 * tc: TC * cc + 128 * (tc + 1)],
                    ptA[ping]),
                  wait=[("pe", pe_tp)], inc="dve")
                tp_ev[ping] = V("dve")
                bank_read(tpb(ping), "dve", tp_ev[ping])
                bank_cycle(tpb(ping))
        dve_hs_all = V("dve")
        act_t1p = [0, 0]
        for tch in range(4):
            ping = tch % 2
            for cc in range(16):
                MM([("dve", dve_hs_all)] if cc == 0 else [],
                   pf[ping][0:16, :], ws1s[:, 16 * cc:16 * (cc + 1)],
                   hs_fm[:, TC * cc + 512 * tch: TC * cc + 512 * (tch + 1)],
                   f"pf{ping}", stop=(cc == 15),
                   inc="pe" if cc == 15 else None)
            pe_t1 = V("pe")
            A("scalar", lambda s, tch=tch, ping=ping: s.copy(
                t1s[:, 512 * tch:512 * (tch + 1)], pf[ping][0:16, :]),
              wait=[("pe", pe_t1)], inc="act")
            act_t1p[ping] = V("act")
            bank_read(f"pf{ping}", "act", act_t1p[ping])
            bank_cycle(f"pf{ping}")
        act_t1 = V("act")

        if dbg:
            A("gpsimd", lambda g: g.dma_start(out=dbg_outs["dbg_hs"],
                                            in_=hs_fm[:, 0:2048]),
              wait=[("dve", dve_hs_all)], inc="sdbg", n=16)

        pe_wread = [0, 0]
        raw_guard = [("dve", 0)]
        rope_guard = [0]
        og_sd = 0
        stg_rd = 0
        rcv_rd = 0
        cc_done = [0] * (NH + 2)
        rowb_rd = [0]
        vt_ev = [0, 0]
        esf_ev = [0, 0]
        ktm_ev = [0, 0]
        uf_ev = [0, 0]
        stm_rd = [0, 0]
        pe_okp = [0, 0]
        pe_Cp = [0, 0]
        pe_Cglob = 0
        dve_hs_all_ = dve_hs_all

        xb0 = xf32[0][:].bitcast(BF16)
        xb1 = xf32[1][:].bitcast(BF16)
        kfm = [k_fm, xb0[:, 0:2048]]
        vtm = [v_tm, xb0[:, 2048:4096]]
        estm = [es_tm, xb1[:, 0:1024]]
        esfm = [es_fm, xb1[:, 1024:2048]]
        carry = xb1[0:1, 2048:2048 + (NT + 1) * M]
        SPb = [[pb[2], pb[3]], [pb[0], pb[1]]]
        SPn = [["pb2", "pb3"], ["pb0", "pb1"]]

        def slot(i, p):
            b = SPn[p][i // 8]
            return (SPb[p][i // 8][:, 64 * (i % 8):64 * (i % 8) + 64], b,
                    f"{b}:s{i % 8}")

        HS = [dict() for _ in range(NH)]

        def projection(kind, h):
            hp = h % 2
            base = {"q": 0, "k": 1, "v": 2, "g": 3}[kind] * TC
            for tch in range(4):
                ping = tch % 2
                w0 = [sd_w[h], ("dve", dve_hs_all_)] if tch == 0 else []
                for cc in range(16):
                    MM(w0 if cc == 0 else [],
                       pf[ping][:, :],
                       wbuf[hp][:, base + 128 * cc:base + 128 * (cc + 1)],
                       hs_fm[:, TC * cc + 512 * tch:
                             TC * cc + 512 * (tch + 1)],
                       f"pf{ping}", stop=(cc == 15),
                       inc="pe" if cc == 15 else None)
                    w0 = []
                pe_pr = V("pe")
                A("scalar", lambda s, tch=tch, ping=ping: s.copy(
                    raw[:, 512 * tch:512 * (tch + 1)], pf[ping][:, :]),
                  wait=[("pe", pe_pr)] + list(raw_guard), inc="act")
                bank_read(f"pf{ping}", "act", V("act"))
                bank_cycle(f"pf{ping}")
            return V("act")

        def rope_into(dst, pe_war):
            act_raw = V("act")
            w = [("act", act_raw), ("sc", sd_const), ("pe", pe_war)]
            A("vector", lambda v, dst=dst: v.tensor_mul(
                dst[:], raw[:], cosb[:]),
              wait=w + [("dve", rope_guard[0])], inc="dve")
            A("vector", lambda v: v.tensor_mul(
                rope2[0:64, :], raw[64:128, :], sinb[64:128, :]),
              inc="dve")
            A("vector", lambda v: v.tensor_mul(
                rope2[64:128, :], raw[0:64, :], sinb[0:64, :]),
              inc="dve")
            dve_m = V("dve")
            A("vector", lambda v, dst=dst: v.tensor_add(
                dst[:], dst[:], rope2[:]),
              wait=[("dve", dve_m)], inc="dve")
            rope_guard[0] = V("dve")
            raw_guard[0] = ("dve", V("dve"))

        def eA1(h):
            st = HS[h]
            p = h % 2
            bank_cycle(*SPn[p])
            for i in range(NT):
                sl, bank, grp = slot(i, p)
                MM([("act", act_t1), ("sc", sd_const)] if i == 0 else [],
                   sl, t1s[:, 128 * i:128 * (i + 1)],
                   ws2s[:, M * h:M * (h + 1)], bank, stop=False, group=grp)
                MM([], sl, onerb[:], bs2s[:, M * h:M * (h + 1)], bank,
                   stop=True, inc="pe" if i == NT - 1 else None, group=grp)
            pe_s = V("pe")
            A("scalar", lambda s, p=p: s.activation(
                estm[p][:, 0:512], SPb[p][0][:, 0:512], AF.Exp),
              wait=[("pe", pe_s), ("pe", pe_okp[p])], inc="act")
            A("scalar", lambda s, p=p: s.activation(
                estm[p][:, 512:1024], SPb[p][1][:, 0:512], AF.Exp),
              inc="act")
            act_es0 = V("act")
            bank_read(SPn[p][0], "act", act_es0)
            bank_read(SPn[p][1], "act", act_es0)
            bank_cycle(*SPn[p])
            A("vector", lambda v, p=p: v.tensor_scalar(
                estm[p][:], estm[p][:], E32, None, ALU.min),
              wait=[("act", act_es0)], inc="dve")
            A("vector", lambda v, p=p: v.tensor_scalar(
                estm[p][:], estm[p][:], EM32, None, ALU.max),
              wait=[("dve", V("dve"))], inc="dve")
            st["dve_es"] = V("dve")

        def eCS(h, dve_f32a_free):
            st = HS[h]
            p = h % 2
            MM([("dve", st["dve_es"])], SPb[p][0][0:1, 0:512], onecb[:],
               estm[p][:, 0:512], SPn[p][0], stop=True, inc="pe",
               group=f"{SPn[p][0]}:cs{h}")
            pe_cs1 = V("pe")
            A("scalar", lambda s, p=p: s.copy(crow[0:1, M:M + 512],
                                              SPb[p][0][0:1, 0:512]),
              wait=[("pe", pe_cs1), ("dve", dve_f32a_free)], inc="act")
            cs_ev1 = V("act")
            bank_read(SPn[p][0], "act", cs_ev1)
            MM([("dve", st["dve_es"])], SPb[p][1][0:1, 0:512], onecb[:],
               estm[p][:, 512:1024], SPn[p][1], stop=True, inc="pe",
               group=f"{SPn[p][1]}:cs{h}")
            pe_cs2 = V("pe")
            A("scalar", lambda s, p=p: s.copy(crow[0:1, M + 512:M + 1024],
                                              SPb[p][1][0:1, 0:512]),
              wait=[("pe", pe_cs2)], inc="act")
            cs_ev2 = V("act")
            bank_read(SPn[p][1], "act", cs_ev2)
            A("vector", lambda v: v.memset(crow[0:1, 0:M], 0.0),
              wait=[("dve", dve_f32a_free)], inc="dve")
            for ci in range(1, NT + 1):
                A("vector", lambda v, ci=ci: v.tensor_add(
                    crow[0:1, M * ci:M * (ci + 1)],
                    crow[0:1, M * ci:M * (ci + 1)],
                    crow[0:1, M * (ci - 1):M * ci]),
                  wait=[("act", cs_ev1 if ci <= 8 else cs_ev2),
                        ("dve", V("dve"))], inc="dve")
            A("vector", lambda v: v.tensor_copy(carry[:], crow[:]),
              wait=[("dve", V("dve"))], inc="dve")
            st["dve_carry"] = V("dve")

        def eKV(h):
            st = HS[h]
            p = h % 2
            projection("k", h)
            rope_into(kfm[p], pe_okp[p])
            st["dve_rope"] = V("dve")
            projection("v", h)
            act_vraw = V("act")
            for i in range(NT):
                ping = i % 2
                TP([("act", act_vraw), ("pe", pe_Cp[p])],
                   ptV[ping], raw[:, 128 * i:128 * (i + 1)], tpb(ping))
                pe_v = V("pe")
                A("vector", lambda v, i=i, ping=ping, p=p: v.tensor_copy(
                    vtm[p][:, 128 * i:128 * (i + 1)], ptV[ping]),
                  wait=[("pe", pe_v)], inc="dve")
                vt_ev[ping] = V("dve")
                bank_read(tpb(ping), "dve", vt_ev[ping])
                bank_cycle(tpb(ping))
            st["dve_vtm"] = V("dve")
            raw_guard[0] = ("pe", V("pe"))

        def eES(h):
            st = HS[h]
            p = h % 2
            for j in range(8):
                ping = j % 2
                TP([("dve", st["dve_es"]), ("pe", pe_Cp[p])],
                   ptE[ping], estm[p][:, 128 * j:128 * (j + 1)], tpb(ping))
                pe_e = V("pe")
                A("vector", lambda v, j=j, ping=ping, p=p: v.tensor_copy(
                    esfm[p][:, 128 * j:128 * (j + 1)], ptE[ping]),
                  wait=[("pe", pe_e)], inc="dve")
                esf_ev[ping] = V("dve")
                bank_read(tpb(ping), "dve", esf_ev[ping])
                bank_cycle(tpb(ping))
            st["dve_esf"] = V("dve")

        def eA3(h):
            st = HS[h]
            p = h % 2
            A("vector", lambda v: v.memset(rf32[:, 0:M], 0.0), inc="dve")
            A("vector", lambda v: v.memset(pf32u[:, 0:128], 0.0), inc="dve")
            chain = V("dve")
            for i in range(NT):
                ping = i % 2
                TP([("dve", st["dve_rope"]), ("dve", ktm_ev[ping])],
                   ptK[ping], kfm[p][:, 128 * i:128 * (i + 1)], tpb(ping))
                pe_kt = V("pe")
                A("vector", lambda v, ping=ping: v.tensor_copy(
                    ktm[:, 128 * ping:128 * (ping + 1)], ptK[ping]),
                  wait=[("pe", pe_kt)], inc="dve")
                ktm_ev[ping] = V("dve")
                bank_read(tpb(ping), "dve", ktm_ev[ping])
                bank_cycle(tpb(ping))
                wR = [("dve", ktm_ev[ping])]
                if i == 0:
                    wR += [("dve", st["dve_es"]), ("dve", st["dve_vtm"])]
                bkR, bkP = ("pf0", "pf1") if ping == 0 else ("pf1", "pf0")
                MM(wR, pwRs[ping], ktm[:, 128 * ping:128 * (ping + 1)],
                   estm[p][:, M * i:M * (i + 1)], bkR, stop=True,
                   inc="pe", group=f"pfR:{h}:{i}")
                pe_R = V("pe")
                MM([], pwPs[ping], estm[p][:, M * i:M * (i + 1)],
                   vtm[p][:, 128 * i:128 * (i + 1)], bkP, stop=True,
                   inc="pe", group=f"pfP:{h}:{i}")
                pe_P = V("pe")
                if i < NT - 1:
                    A("vector", lambda v, i=i, ping=ping: v.tensor_add(
                        rf32[:, M * (i + 1):M * (i + 2)],
                        rf32[:, M * i:M * (i + 1)], pwRs[ping]),
                      wait=[("pe", pe_R), ("dve", chain)], inc="dve")
                    bank_read(bkR, "dve", V("dve"))
                    A("vector", lambda v, i=i, ping=ping: v.tensor_add(
                        pf32u[:, 128 * (i + 1):128 * (i + 2)],
                        pf32u[:, 128 * i:128 * (i + 1)], pwPs[ping]),
                      wait=[("pe", pe_P), ("dve", V("dve"))], inc="dve")
                    chain = V("dve")
                    bank_read(bkP, "dve", chain)
                else:
                    st["pe_R15"], st["pe_P15"] = pe_R, pe_P
                sl, bank, grp = slot(i, p)
                MM([("dve", st["dve_carry"])] if i == 0 else [], sl, trius[:],
                   estm[p][:, M * i:M * (i + 1)], bank, stop=False, group=grp)
                MM([], sl, onerb[:], carry[0:1, M * i:M * (i + 1)], bank,
                   stop=True, inc="pe" if i == NT - 1 else None, group=grp)
            st["pe_state"] = V("pe")
            st["dve_chain"] = chain
            A("vector", lambda v: v.tensor_add(
                stg[:, 0:64], rf32[:, (NT - 1) * M:NT * M], pwRs[1]),
              wait=[("dve", st["dve_chain"]), ("pe", st["pe_R15"]),
                    ("sp", stg_rd)],
              inc="dve")
            bank_read("pf1", "dve", V("dve"))
            A("vector", lambda v: v.tensor_add(
                stg[0:64, 64:192], pf32u[:, (NT - 1) * 128:NT * 128],
                pwPs[1]),
              wait=[("pe", st["pe_P15"])], inc="dve")
            A("vector", lambda v: v.tensor_add(
                stg[64:128, 64:192], pf32u[:, (NT - 1) * 128:NT * 128],
                pwPs[1]),
              inc="dve")
            st["dve_stg"] = V("dve")
            bank_read("pf0", "dve", st["dve_stg"])
            MM([("dve", st["dve_carry"])], pwCB, onerb[:],
               carry[0:1, NT * M:(NT + 1) * M], "pf0", stop=True, inc="pe",
               group=f"pfCB:{h}")
            pe_cb = V("pe")
            A("vector", lambda v: v.tensor_copy(stg[:, 192:256], pwCB),
              wait=[("pe", pe_cb), ("sp", stg_rd)], inc="dve")
            st["dve_stg2"] = V("dve")
            bank_read("pf0", "dve", st["dve_stg2"])

        def ePush(h):
            st = HS[h]
            dve_stg = st["dve_stg"]
            dve_stg2 = st["dve_stg2"]
            A("gpsimd", lambda g, h=h: g.dma_start(
                out=st_in[h % 2].rearrange("o (p f) -> (o p) f", p=128),
                in_=stg[:]),
              wait=[("dve", dve_stg2), ("dve", dve_stg),
                    ("cc", cc_done[h - 2] if h >= 2 else 0)],
              inc="sp", n=16)
            sd_push = V("sp")
            A("gpsimd", lambda g, h=h: g.collective_compute(
                "AllGather", ALU.bypass, replica_groups=RG,
                ins=[st_in[h % 2].opt()], outs=[st_out[h % 2].opt()]),
              wait=[("sp", sd_push)], inc="cc")
            cc_done[h] = V("cc")
            A("gpsimd", lambda g, h=h: g.dma_start(
                out=rcv[:], in_=st_out[h % 2][0:1, :].rearrange(
                    "o (p f) -> (o p) f", p=128)),
              wait=[("cc", cc_done[h]), ("dve", rcv_rd)], inc="sr", n=16)
            st["sd_rcv"] = V("sr")
            return sd_push

        def eQG(h, pe_q_war, dve_sg_war):
            st = HS[h]
            projection("q", h)
            rope_into(q_fm, pe_q_war)
            st["dve_qrope"] = V("dve")
            projection("g", h)
            act_graw = V("act")
            A("scalar", lambda s: s.activation(sg[:], raw[:], AF.Silu),
              wait=[("act", act_graw), ("dve", dve_sg_war)], inc="act")
            st["act_silu"] = V("act")
            raw_guard[0] = ("act", st["act_silu"])
            pe_wread[h % 2] = V("pe")

        def eFold(h):
            st = HS[h]
            p = h % 2
            A("vector", lambda v: v.scalar_tensor_tensor(
                rsn[:].rearrange("p (i m) -> p i m", m=M),
                rcv[:, 0:64].unsqueeze(1).broadcast_to([128, NT, M]),
                sprefs[:],
                rf32[:, 0:NT * M].rearrange("p (i m) -> p i m", m=M),
                ALU.mult, ALU.add),
              wait=[("sr", st["sd_rcv"]), ("dve", st["dve_chain"]),
                    ("pe", pe_okp[1 - p])],
              inc="dve")
            pf32v = pf32u[:, 0:NT * 128].rearrange(
                "p (a two f) -> p a two f", two=2, f=128)
            A("vector", lambda v: v.scalar_tensor_tensor(
                psn[0:64, :].rearrange("p (j one f) -> p j one f",
                                       one=1, f=128),
                rcv[0:64, 64:192].unsqueeze(1).unsqueeze(2)
                .broadcast_to([64, 8, 1, 128]),
                sprefs[0:64, :],
                pf32v[:, :, 0:1, :],
                ALU.mult, ALU.add),
              wait=[("pe", pe_Cglob)], inc="dve")
            A("vector", lambda v: v.scalar_tensor_tensor(
                psn[64:128, :].rearrange("p (j one f) -> p j one f",
                                         one=1, f=128),
                rcv[0:64, 64:192].unsqueeze(1).unsqueeze(2)
                .broadcast_to([64, 8, 1, 128]),
                sprefs[0:64, :],
                pf32v[:, :, 1:2, :],
                ALU.mult, ALU.add),
              inc="dve")
            st["dve_snaps"] = V("dve")
            A("vector", lambda v, p=p: v.scalar_tensor_tensor(
                enza[:, 0:512].rearrange("p (i m) -> p i m", m=M),
                rcv[:, 192:256].unsqueeze(1).broadcast_to([128, 8, M]),
                sprefs[:],
                SPb[p][0][:, 0:512].rearrange("p (i m) -> p i m", m=M),
                ALU.mult, ALU.add),
              wait=[("sr", st["sd_rcv"]), ("pe", st["pe_state"])],
              inc="dve")
            A("vector", lambda v, p=p: v.scalar_tensor_tensor(
                enza[:, 512:1024].rearrange("p (i m) -> p i m", m=M),
                rcv[:, 192:256].unsqueeze(1).broadcast_to([128, 8, M]),
                sprefs[:],
                SPb[p][1][:, 0:512].rearrange("p (i m) -> p i m", m=M),
                ALU.mult, ALU.add),
              inc="dve")
            dve_c = V("dve")
            st["dve_c"] = dve_c
            bank_read(SPn[p][0], "dve", dve_c)
            bank_read(SPn[p][1], "dve", dve_c)
            bank_cycle(*SPn[p])
            A("scalar", lambda s: s.activation(enzb[:], enza[:], AF.Ln),
              wait=[("dve", dve_c)], inc="act")
            A("scalar", lambda s: s.activation(enzb[:], enzb[:], AF.Exp,
                                               scale=-1.0),
              wait=[("act", V("act"))], inc="act")
            st["act_enz"] = V("act")

        def eB2(h):
            st = HS[h]
            p = h % 2
            for i in range(NT):
                ping = i % 2
                sl, bank, grp = slot(i, p)
                MM([("dve", st["dve_snaps"]), ("act", st["act_enz"]),
                    ("dve", st["dve_qrope"])] if i == 0 else [],
                   sl, q_fm[:, 128 * i:128 * (i + 1)],
                   rsn[:, M * i:M * (i + 1)], bank, stop=False, group=grp)
                MM([], pf[ping][:, 0:128], kfm[p][:, 128 * i:128 * (i + 1)],
                   q_fm[:, 128 * i:128 * (i + 1)], f"pf{ping}", stop=True,
                   inc="pe", group=f"kq:{h}:{i}")
                pe_st = V("pe")
                A("vector", lambda v, ping=ping: v.tensor_mul(
                    stm[:, 128 * ping:128 * (ping + 1)], pf[ping][:, 0:128],
                    trius[:]),
                  wait=[("pe", pe_st), ("pe", stm_rd[ping])], inc="dve")
                dve_st = V("dve")
                bank_read(f"pf{ping}", "dve", dve_st)
                MM([("dve", dve_st)], sl,
                   stm[:, 128 * ping:128 * (ping + 1)],
                   estm[p][:, M * i:M * (i + 1)], bank, stop=True,
                   inc="pe", group=grp)
                stm_rd[ping] = V("pe")
            st["pe_ok"] = V("pe")
            pe_okp[p] = st["pe_ok"]

        def eB3(h):
            st = HS[h]
            p = h % 2
            A("vector", lambda v, p=p: v.scalar_tensor_tensor(
                smf1[:, 0:512], SPb[p][0][:, 0:512], SCALE, enzb[:, 0:512],
                ALU.mult, ALU.mult),
              wait=[("pe", st["pe_ok"]), ("act", st["act_enz"])], inc="dve")
            A("vector", lambda v, p=p: v.scalar_tensor_tensor(
                smf1[:, 512:1024], SPb[p][1][:, 0:512],
                SCALE, enzb[:, 512:1024],
                ALU.mult, ALU.mult),
              inc="dve")
            dve_okz = V("dve")
            bank_read(SPn[p][0], "dve", dve_okz)
            bank_read(SPn[p][1], "dve", dve_okz)
            bank_cycle(*SPn[p])
            A("vector", lambda v: v.reduce_max(
                stat[:, 0:NT], smf1.rearrange("p (i m) -> p i m", m=M),
                axis=AX.X, negate=True),
              wait=[("dve", dve_okz)], inc="dve")
            A("vector", lambda v: v.tensor_tensor(
                smf1.rearrange("p (i m) -> p i m", m=M),
                smf1.rearrange("p (i m) -> p i m", m=M),
                stat[:, 0:NT].unsqueeze(2).broadcast_to([128, NT, M]),
                ALU.add),
              wait=[("dve", V("dve"))], inc="dve")
            dve_xm = V("dve")
            A("scalar", lambda s: s.activation(smf2, smf1, AF.Exp),
              wait=[("dve", dve_xm)], inc="act")
            act_ex = V("act")
            A("vector", lambda v: v.reduce_sum(
                stat2[:, 0:NT], smf2.rearrange("p (i m) -> p i m", m=M),
                axis=AX.X),
              wait=[("act", act_ex)], inc="dve")
            A("vector", lambda v: v.reciprocal(stat[:, 0:NT], stat2[:, 0:NT]),
              wait=[("dve", V("dve"))], inc="dve")
            A("vector", lambda v: v.tensor_tensor(
                smf2.rearrange("p (i m) -> p i m", m=M),
                smf2.rearrange("p (i m) -> p i m", m=M),
                stat[:, 0:NT].unsqueeze(2).broadcast_to([128, NT, M]),
                ALU.mult),
              wait=[("dve", V("dve"))], inc="dve")
            A("vector", lambda v: v.tensor_mul(u_tm[:], smf2, enzb),
              wait=[("dve", V("dve")), ("pe", pe_Cglob)], inc="dve")
            st["dve_u"] = V("dve")
            for j in range(8):
                ping = j % 2
                TP([("dve", st["dve_u"]), ("pe", pe_Cglob)],
                   ptE[ping], u_tm[:, 128 * j:128 * (j + 1)], tpb(ping))
                pe_u = V("pe")
                A("vector", lambda v, j=j, ping=ping: v.tensor_copy(
                    u_fm[:, 128 * j:128 * (j + 1)], ptE[ping]),
                  wait=[("pe", pe_u)], inc="dve")
                uf_ev[ping] = V("dve")
                bank_read(tpb(ping), "dve", uf_ev[ping])
                bank_cycle(tpb(ping))
            st["dve_uf"] = V("dve")

        def eC(h):
            st = HS[h]
            p = h % 2
            obank = SPn[p][0]
            opp = SPb[p][0]
            for i in range(NT):
                ping = i % 2
                pslc = psn[64 * (i % 2):64 * (i % 2) + 64,
                           128 * (i // 2):128 * (i // 2) + 128]
                eslc = esfm[p][64 * (i % 2):64 * (i % 2) + 64,
                              128 * (i // 2):128 * (i // 2) + 128]
                uslc = u_fm[64 * (i % 2):64 * (i % 2) + 64,
                            128 * (i // 2):128 * (i // 2) + 128]
                MM([("dve", st["dve_uf"]), ("dve", st["dve_esf"])]
                   if i == 0 else [],
                   pf[ping][:, 0:128], eslc, uslc, f"pf{ping}", stop=True,
                   inc="pe", group=f"w2:{h}:{i}")
                pe_w2 = V("pe")
                A("vector", lambda v, ping=ping: v.tensor_mul(
                    stm[:, 128 * ping:128 * (ping + 1)], pf[ping][:, 0:128],
                    trius[:]),
                  wait=[("pe", pe_w2), ("pe", stm_rd[ping])], inc="dve")
                dve_w2 = V("dve")
                bank_read(f"pf{ping}", "dve", dve_w2)
                MM([("dve", st["dve_snaps"])], opp[:, 0:128], pslc, uslc,
                   obank, stop=False, group=f"o:{h}:{i}")
                MM([("dve", dve_w2)], opp[:, 0:128],
                   vtm[p][:, 128 * i:128 * (i + 1)],
                   stm[:, 128 * ping:128 * (ping + 1)], obank, stop=True,
                   inc="pe", group=f"o:{h}:{i}")
                stm_rd[ping] = V("pe")
                pe_o = V("pe")
                A("scalar", lambda s, i=i, p=p: s.copy(
                    o_sb[:, 128 * i:128 * (i + 1)], opp[:, 0:128]),
                  wait=[("pe", pe_o), ("so", og_sd)], inc="act")
                o_ev = V("act")
                bank_read(obank, "act", o_ev)
            st["act_o"] = V("act")
            st["pe_C"] = V("pe")
            pe_Cp[p] = st["pe_C"]
            return st["pe_C"]

        def eE(h):
            st = HS[h]
            p = h % 2
            A("scalar", lambda s: s.activation(sq_bf[:], o_sb[:], AF.Square),
              wait=[("act", st["act_o"])], inc="act")
            act_sq = V("act")
            sum_banks = [(SPn[p][0], SPb[p][0], 0), (SPn[p][1], SPb[p][1], 2)]
            bank_cycle(*SPn[p])
            for (bk, pp_, j0) in sum_banks:
                MM([("act", act_sq)] if j0 == 0 else [],
                   pp_[0:1, 0:512], onecb[:],
                   sq_bf[:, 512 * j0:512 * (j0 + 1)],
                   bk, stop=True, inc="pe", group=f"E:{h}:{j0}")
            pe_sq = V("pe")
            for (bk, pp_, j0) in sum_banks:
                A("scalar", lambda s, j0=j0, pp_=pp_: s.activation(
                    rowa[0:1, 512 * j0:512 * (j0 + 1)], pp_[0:1, 0:512],
                    AF.Ln, bias=epsc[0:1, 0:1], scale=1.0 / DV),
                  wait=[("pe", pe_sq), ("dve", st["dve_u"]),
                        ("dve", HS[h + 1].get("dve_carry", 0)
                         if h + 1 < len(HS) else 0)] if j0 == 0 else [],
                  inc="act")
                bank_read(bk, "act", V("act"))
            for (bk, pp_, j0) in sum_banks:
                j1 = j0 + 1
                MM([], pp_[0:1, 0:512], onecb[:],
                   sq_bf[:, 512 * j1:512 * (j1 + 1)],
                   bk, stop=True, inc="pe", group=f"E:{h}:{j1}")
            pe_sq2 = V("pe")
            for (bk, pp_, j0) in sum_banks:
                j1 = j0 + 1
                A("scalar", lambda s, j1=j1, pp_=pp_: s.activation(
                    rowa[0:1, 512 * j1:512 * (j1 + 1)], pp_[0:1, 0:512],
                    AF.Ln, bias=epsc[0:1, 0:1], scale=1.0 / DV),
                  wait=[("pe", pe_sq2)] if j1 == 1 else [],
                  inc="act")
                bank_read(bk, "act", V("act"))
            A("scalar", lambda s: s.activation(
                rowb[0:1, :], rowa[0:1, :], AF.Exp, scale=-0.5),
              wait=[("act", V("act")), ("pe", rowb_rd[0])], inc="act")
            act_rstd = V("act")
            bank_cycle(*SPn[p])
            for j in range(4):
                gb = SPn[p][j % 2]
                gp = SPb[p][j % 2]
                MM([("act", act_rstd)] if j == 0 else [],
                   gp[:, :], gnws[:], rowb[0:1, 512 * j:512 * (j + 1)],
                   gb, stop=True, inc="pe", group=f"g:{h}:{j}")
                pe_g = V("pe")
                rowb_rd[0] = pe_g
                A("scalar", lambda s, j=j, gp=gp: s.copy(
                    sq_bf[:, 512 * j:512 * (j + 1)], gp[:, :]),
                  wait=[("pe", pe_g)], inc="act")
                bank_read(gb, "act", V("act"))
            act_grbc = V("act")
            A("vector", lambda v: v.tensor_mul(o_sb[:], o_sb[:], sg[:]),
              wait=[("act", act_grbc)], inc="dve")
            A("vector", lambda v: v.tensor_mul(o_sb[:], o_sb[:], sq_bf[:]),
              wait=[("dve", V("dve"))], inc="dve")
            dve_og = V("dve")
            st["dve_og"] = dve_og
            A("gpsimd", lambda g, h=h: g.dma_start(
                out=og_d[h].transpose([1, 0, 2]),
                in_=o_sb[:].rearrange("p (c t) -> p c t", t=128)),
              wait=[("dve", dve_og)], inc="so", n=16)
            return V("so")

        for h in range(n_heads + 1):
            live = h < n_heads
            if live:
                if h + 1 < n_heads:
                    A("gpsimd", lambda g, h=h: g.nop(),
                      wait=[("pe", pe_wread[(h + 1) % 2])])
                    sd_w[h + 1] = wload(h + 1, (h + 1) % 2)
                eA1(h)
            if h >= 1:
                eB2(h - 1)
            if live:
                eKV(h)
            if h >= 1:
                eB3(h - 1)
            if live:
                eCS(h, HS[h - 1]["dve_u"] if h >= 1 else 0)
                eES(h)
                eA3(h)
                stg_rd = ePush(h)
            if h >= 1:
                eC(h - 1)
                pe_Cglob = HS[h - 1]["pe_C"]
                og_sd = eE(h - 1)
            if live:
                eQG(h, HS[h - 1]["pe_ok"] if h >= 1 else 0,
                    HS[h - 1]["dve_og"] if h >= 1 else 0)
                eFold(h)
                rcv_rd = HS[h]["dve_c"]

        act_headend = V("act")
        dve_headend = V("dve")
        pe_headend = V("pe")
        sd_og = V("so")
        pe_headloop = V("pe")

        A("gpsimd", lambda g: g.nop(),
          wait=[("pe", pe_headloop), ("dve", V("dve")), ("act", V("act"))])
        for hh in range(NH):
            A("gpsimd", lambda g, hh=hh: g.dma_start(
                out=hs_fm[:, TC * hh:TC * (hh + 1)], in_=wot[hh]),
              inc="sv", n=16)
        sd_wo = V("sv")
        bank_cycle("pf0")
        A("vector", lambda v: v.memset(sls[:], 0.0),
          wait=[("dve", V("dve"))], inc="dve")
        sls_dve = V("dve")
        attn_ev = 0
        l2t_ev = [0, 0]
        pe_sln = 0
        hstm_sd = 0
        sqbf_rd = pe_headloop
        f32a_rd = dve_headend
        f32b_rd = [dve_headend]
        ogb = [raw, rope2]
        ogb_rd = [pe_headloop, pe_headloop]
        hd_ogs = [0, 0]
        hd_xs = [0, 0]

        def p3_load(tc):
            b = tc % 2
            hosem = "ho" if b == 0 else "ho1"
            A("sync", lambda s, tc=tc, b=b: s.dma_start(
                out=ogb[b][:].rearrange("p (h t) -> p h t", h=NH),
                in_=og_d[:, tc].transpose([1, 0, 2])),
              wait=[("so", sd_og), ("pe", ogb_rd[b]),
                    ("act", act_headend), ("dve", dve_headend)],
              inc=hosem, n=16)
            hd_ogs[b] = (hosem, V(hosem))
            hd_xs[b] = xload(tc, [("pe", pe_headend), ("act", act_headend),
                                  ("hs2", V("hs2"))])

        def p3_wo(tc):
            b = tc % 2
            bank_cycle("pb0", "pb1", "pb2", "pb3")
            for j in range(4):
                for hh in range(NH):
                    MM([hd_ogs[b], ("sv", sd_wo)]
                       if (j == 0 and hh == 0) else [],
                       pb[j][:, :], ogb[b][:, 128 * hh:128 * (hh + 1)],
                       hs_fm[:, TC * hh + 512 * j:TC * hh + 512 * (j + 1)],
                       f"pb{j}", stop=(hh == NH - 1),
                       inc="pe" if (j == 3 and hh == NH - 1) else None)
            ogb_rd[b] = V("pe")
            return ogb_rd[b]

        p3_load(0)
        pe_attns = {0: p3_wo(0)}
        for tc in range(NT):
            buf = tc % 2
            pe_attn = pe_attns[tc]
            hd_xr = hd_xs[buf]
            for j in range(4):
                A("scalar", lambda s, j=j: s.copy(
                    f32a[:, 512 * j:512 * (j + 1)], pb[j][:, :]),
                  wait=[("pe", pe_attn), ("dve", f32a_rd)] if j == 0 else [],
                  inc="act")
                bank_read(f"pb{j}", "act", V("act"))
            attn_ev = V("act")
            if tc + 1 < NT:
                p3_load(tc + 1)
                pe_attns[tc + 1] = p3_wo(tc + 1)

            def ln_stats(src_ap, wait0):
                for q in range(4):
                    A("vector", lambda v, q=q: v.bn_stats(
                        bn6[:, 6 * q:6 * (q + 1)],
                        src_ap[:, 512 * q:512 * (q + 1)]),
                      wait=(wait0 + [("dve", V("dve"))]) if q == 0 else [],
                      inc="dve")
                A("vector", lambda v: v.bn_aggr(stat[:, 2:4], bn6[:]),
                  wait=[("dve", V("dve"))], inc="dve")
                A("scalar", lambda s: s.activation(
                    stat[:, 6:7], stat[:, 3:4], AF.Ln, bias=epsc[:]),
                  wait=[("dve", V("dve"))], inc="act")
                A("scalar", lambda s: s.activation(
                    stat[:, 7:8], stat[:, 6:7], AF.Exp, scale=-0.5),
                  wait=[("act", V("act"))], inc="act")
                return V("act")

            act_r1 = ln_stats(f32a[:], [("act", attn_ev)])
            A("vector", lambda v: v.tensor_scalar(
                f32a[:], f32a[:], stat[:, 2:3], stat[:, 7:8],
                ALU.subtract, ALU.mult),
              wait=[("act", act_r1)], inc="dve")
            A("vector", lambda v, buf=buf: v.tensor_add(
                xf32[buf][:], xf32[buf][:], f32a[:]),
              wait=[("dve", V("dve")), hd_xr], inc="dve")
            dve_x2 = V("dve")
            f32a_rd = dve_x2
            A("sync", lambda s, tc=tc, buf=buf: s.dma_start(
                out=x2_d[tc], in_=xf32[buf][:]),
              wait=[("dve", dve_x2)], inc="hs2", n=16)
            hd_x2st = V("hs2")
            act_r2 = ln_stats(xf32[buf][:], [("dve", dve_x2)])
            A("vector", lambda v, buf=buf: v.tensor_scalar(
                sq_bf[:], xf32[buf][:], stat[:, 2:3], stat[:, 7:8],
                ALU.subtract, ALU.mult),
              wait=[("act", act_r2), ("pe", sqbf_rd)], inc="dve")
            dve_l2 = V("dve")
            xld_guard[buf] = dve_l2
            bank_cycle("pf0")
            for cc in range(16):
                MM([("dve", dve_l2), ("dve", sls_dve)] if cc == 0 else [],
                   pf[0][:, cc:cc + 1], sq_bf[:, 128 * cc:128 * (cc + 1)],
                   onecb[:], "pf0", stop=True,
                   inc="pe" if cc == 15 else None, group=f"pf0:c{cc}")
            pe_sln = V("pe")
            A("vector", lambda v: v.tensor_add(sls[:], sls[:],
                                               pf[0][:, 0:16]),
              wait=[("pe", pe_sln), ("dve", sls_dve)], inc="dve")
            sls_dve = V("dve")
            bank_read("pf0", "dve", sls_dve)
            for cc in range(16):
                ping = cc % 2
                TP([("dve", dve_l2)],
                   ptA[ping], sq_bf[:, 128 * cc:128 * (cc + 1)],
                   tpb(ping))
                pe_t = V("pe")
                A("vector", lambda v, cc=cc, ping=ping: v.tensor_copy(
                    hs_tm[:, 128 * cc:128 * (cc + 1)], ptA[ping]),
                  wait=[("pe", pe_t), ("sl", hstm_sd)], inc="dve")
                l2t_ev[ping] = V("dve")
                bank_read(tpb(ping), "dve", l2t_ev[ping])
                bank_cycle(tpb(ping))
            sqbf_rd = V("pe")
            hstm_act = V("dve")
            A("gpsimd", lambda g, tc=tc: g.dma_start(
                out=l2f_d[tc].transpose([1, 0, 2]),
                in_=hs_tm[:].rearrange("p (c t) -> p c t", t=128)),
              wait=[("dve", hstm_act)], inc="sl", n=16)
            hstm_sd = V("sl")
            if dbg and tc == 0:
                A("sync", lambda s, buf=buf: s.dma_start(
                    out=dbg_outs["dbg_x2"], in_=xf32[buf][:]),
                  wait=[("dve", dve_x2)], inc="sdbg", n=16)
        hd_wo = V("hs2")
        sd_l2f = V("sl")

        dve_sls = sls_dve
        bank_cycle("pf0")
        A("gpsimd", lambda g: g.dma_start(out=sels, in_=sel),
          wait=[("pe", V("pe")), ("act", V("act")), ("dve", V("dve"))],
          inc="ss", n=16)
        A("gpsimd", lambda g: g.dma_start(out=istks, in_=istk),
          inc="ss", n=16)
        sd_sels = V("ss")
        A("gpsimd", lambda g: g.dma_start(out=onerf, in_=oner_f),
          wait=[("dve", xld_guard[0]), ("hs2", hd_wo)], inc="hof", n=16)
        hd_onerf = V("hof")
        A("gpsimd", lambda g: g.dma_start(
            out=sq_in.rearrange("o (p f) -> (o p) f", p=128), in_=sls[:]),
          wait=[("dve", dve_sls)], inc="sq", n=16)
        A("gpsimd", lambda g: g.collective_compute(
            "AllGather", ALU.bypass, replica_groups=RG,
            ins=[sq_in.opt()], outs=[sq_out.opt()]),
          wait=[("sq", V("sq"))], inc="cc")
        cc_sq = V("cc")
        A("gpsimd", lambda g: g.dma_start(
            out=srep[:], in_=sq_out[0:1, :].rearrange("o (p f) -> (o p) f",
                                                      p=128)),
          wait=[("cc", cc_sq)], inc="sq", n=16)
        A("gpsimd", lambda g: g.dma_start(
            out=sls[:], in_=sq_out[1:2, :].rearrange("o (p f) -> (o p) f",
                                                     p=128)),
          wait=[("dve", dve_sls)], inc="sq", n=16)
        sd_sq = V("sq")
        A("vector", lambda v: v.tensor_add(srep[:], srep[:], sls[:]),
          wait=[("sq", sd_sq)], inc="dve")
        A("vector", lambda v: v.tensor_scalar(
            srep[:], srep[:], 1.0 / T, None, ALU.mult),
          wait=[("dve", V("dve"))], inc="dve")
        dve_srep = V("dve")
        A("gpsimd", lambda g: g.dma_start(
            out=sim_sb.rearrange("p (c e) -> p c e", e=NE),
            in_=simm.transpose([1, 0, 2])),
          wait=[("dve", xld_guard[1])], inc="ss", n=16)
        sd_sim = V("ss")
        A("scalar", lambda s: s.activation(f32a[:, 0:16 * NE], sim_sb,
                                           AF.Square),
          wait=[("ss", sd_sim), ("dve", f32a_rd)], inc="act")
        act_simsq = V("act")
        for cc in range(16):
            A("vector", lambda v, cc=cc: v.tensor_scalar(
                f32b[:, NE * cc:NE * (cc + 1)],
                sim_sb[:, NE * cc:NE * (cc + 1)],
                srep[:, cc:cc + 1], None, ALU.mult),
              wait=[("dve", dve_srep), ("ss", sd_sim),
                    ("dve", f32b_rd[0])] if cc == 0 else [],
              inc="dve")
        dve_ws = V("dve")
        bank_cycle("pf0", "pf1", "pb0", "pb1")
        for cc in range(16):
            MM([("dve", dve_ws)] if cc == 0 else [],
               pf[0][0:1, 0:NE], onecf[:], f32b[:, NE * cc:NE * (cc + 1)],
               "pf0", stop=(cc == 15), inc="pe" if cc == 15 else None)
            MM([("act", act_simsq)] if cc == 0 else [],
               pf[1][0:1, 0:NE], onecf[:], f32a[:, NE * cc:NE * (cc + 1)],
               "pf1", stop=(cc == 15), inc="pe" if cc == 15 else None)
            MM([("dve", dve_ws)] if cc == 0 else [],
               pb[0][0:NE, 0:1], f32b[:, NE * cc:NE * (cc + 1)], onecf[:],
               "pb0", stop=(cc == 15), inc="pe" if cc == 15 else None)
            MM([("act", act_simsq)] if cc == 0 else [],
               pb[1][0:NE, 0:1], f32a[:, NE * cc:NE * (cc + 1)], onecf[:],
               "pb1", stop=(cc == 15), inc="pe" if cc == 15 else None)
        pe_aff = V("pe")
        A("scalar", lambda s: s.activation(stat[:, 0:16], srep[:], AF.Square),
          wait=[("dve", dve_srep)], inc="act")
        act_ss = V("act")
        bank_cycle("pb2")
        MM([("act", act_ss)], pb[2][0:1, 0:16], onecf[:], stat[:, 0:16],
           "pb2", stop=True, inc="pe")
        pe_sn = V("pe")
        A("vector", lambda v: v.tensor_copy(grow[0:1, 0:16], pb[2][0:1, 0:16]),
          wait=[("pe", pe_sn)], inc="dve")
        bank_read("pb2", "dve", V("dve"))
        A("vector", lambda v: v.reduce_sum(gcol[0:1, 0:1], grow[0:1, 0:16],
                                           axis=AX.X),
          wait=[("dve", V("dve"))], inc="dve")
        A("scalar", lambda s: s.activation(gcol[0:1, 1:2], gcol[0:1, 0:1],
                                           AF.Ln),
          wait=[("dve", V("dve"))], inc="act")
        A("scalar", lambda s: s.activation(gcol[0:1, 2:3], gcol[0:1, 1:2],
                                           AF.Exp, scale=-0.5),
          wait=[("act", V("act"))], inc="act")
        act_rn = V("act")
        def gr(j):
            return grow[0:1, NE * j + 32:NE * (j + 1) + 32]
        A("scalar", lambda s: s.activation(grow[0:1, 0:NE], pf[1][0:1, 0:NE],
                                           AF.Ln),
          wait=[("pe", pe_aff)], inc="act")
        A("scalar", lambda s: s.activation(grow[0:1, 0:NE], grow[0:1, 0:NE],
                                           AF.Exp, scale=-0.5),
          wait=[("act", V("act"))], inc="act")
        act_cnr = V("act")
        bank_read("pf1", "act", act_cnr)
        A("scalar", lambda s: s.activation(gcol[:, 3:4], pb[1][0:NE, 0:1],
                                           AF.Ln),
          wait=[("pe", pe_aff)], inc="act")
        A("scalar", lambda s: s.activation(gcol[:, 3:4], gcol[:, 3:4],
                                           AF.Exp, scale=-0.5),
          wait=[("act", V("act"))], inc="act")
        act_cnc = V("act")
        bank_read("pb1", "act", act_cnc)
        A("vector", lambda v: v.tensor_mul(gr(0), pf[0][0:1, 0:NE],
                                           grow[0:1, 0:NE]),
          wait=[("act", act_cnr)], inc="dve")
        bank_read("pf0", "dve", V("dve"))
        A("vector", lambda v: v.tensor_scalar(
            gr(0), gr(0), gcol[0:1, 2:3], None, ALU.mult),
          wait=[("dve", V("dve")), ("act", act_rn)], inc="dve")
        dve_affr = V("dve")
        A("vector", lambda v: v.tensor_mul(gcol[:, 0:1], pb[0][0:NE, 0:1],
                                           gcol[:, 3:4]),
          wait=[("act", act_cnc)], inc="dve")
        bank_read("pb0", "dve", V("dve"))
        A("vector", lambda v: v.tensor_copy(gcbf[:, 0:1], gcol[:, 0:1]),
          wait=[("dve", V("dve"))], inc="dve")
        dve_acbf = V("dve")
        bank_cycle("pb3")
        MM([("dve", dve_acbf), ("act", act_rn), ("hof", hd_onerf)],
           pb[3][0:NE, 0:1],
           onerf[0:1, 0:NE], gcol[0:1, 2:3], "pb3", stop=True, inc="pe")
        pe_rnb = V("pe")
        A("vector", lambda v: v.tensor_mul(gcol[:, 0:1], gcol[:, 0:1],
                                           pb[3][0:NE, 0:1]),
          wait=[("pe", pe_rnb)], inc="dve")
        dve_affc = V("dve")
        bank_read("pb3", "dve", dve_affc)
        A("sync", lambda s: s.dma_start(out=gr(1), in_=gates),
          inc="hg", n=16)
        hd_g = V("hg")
        A("scalar", lambda s: s.activation(gr(1), gr(1), AF.Tanh, scale=0.5),
          wait=[("hg", hd_g)], inc="act")
        A("vector", lambda v: v.tensor_scalar(
            gr(1), gr(1), 0.5, 0.5, ALU.mult, ALU.add),
          wait=[("act", V("act"))], inc="dve")
        dve_sigr = V("dve")
        bank_cycle("pb3")
        MM([("dve", dve_sigr)], pb[3][0:NE, 0:1], gr(1),
           onecf[0:1, 0:1], "pb3", stop=True, inc="pe")
        pe_sigc = V("pe")
        A("vector", lambda v: v.tensor_copy(gcol[:, 1:2], pb[3][0:NE, 0:1]),
          wait=[("pe", pe_sigc)], inc="dve")
        bank_read("pb3", "dve", V("dve"))
        A("vector", lambda v: v.tensor_sub(gr(2), gr(0), gr(1)),
          wait=[("dve", dve_affr), ("dve", V("dve"))], inc="dve")
        A("vector", lambda v: v.tensor_relu(gr(3), gr(2)),
          wait=[("dve", V("dve"))], inc="dve")
        A("vector", lambda v: v.tensor_single_scalar(
            gr(4), gr(3), 0.0, ALU.is_gt),
          wait=[("dve", V("dve"))], inc="dve")
        dve_maskr = V("dve")
        A("vector", lambda v: v.tensor_sub(gcol[:, 2:3], gcol[:, 0:1],
                                           gcol[:, 1:2]),
          wait=[("dve", dve_affc)], inc="dve")
        dve_logc = V("dve")
        A("vector", lambda v: v.memset(sim_sb[0:32, 0:NE], 1.0),
          wait=[("act", act_simsq), ("dve", dve_ws)], inc="dve")
        A("vector", lambda v: v.tensor_scalar(
            sim_sb[0:32, 0:NE], sim_sb[0:32, 0:NE], gcol[0:32, 2:3], None,
            ALU.mult),
          wait=[("dve", V("dve")), ("dve", dve_logc)], inc="dve")
        dve_cbc = V("dve")
        bank_cycle("pb3")
        MM([("dve", dve_logc)], pb[3][0:NE, 0:NE], onerf[0:1, 0:NE],
           gr(2), "pb3", stop=True, inc="pe")
        pe_rbc = V("pe")
        A("vector", lambda v: v.tensor_tensor(
            sim_sb[0:32, NE:2 * NE], pb[3][0:NE, 0:NE], sim_sb[0:32, 0:NE],
            ALU.is_gt),
          wait=[("pe", pe_rbc), ("dve", dve_cbc)], inc="dve")
        bank_read("pb3", "dve", V("dve"))
        A("vector", lambda v: v.reduce_sum(
            gcol[:, 4:5], sim_sb[0:32, NE:2 * NE], axis=AX.X),
          wait=[("dve", V("dve"))], inc="dve")
        xld_guard[1] = V("dve")
        A("vector", lambda v: v.tensor_single_scalar(
            gcol[:, 5:6], gcol[:, 4:5], float(NE // 2) - 0.5, ALU.is_le),
          wait=[("dve", V("dve"))], inc="dve")
        dve_fbc = V("dve")
        A("vector", lambda v: v.reduce_sum(gcol[0:1, 6:7], gr(4), axis=AX.X),
          wait=[("dve", dve_maskr)], inc="dve")
        A("vector", lambda v: v.tensor_single_scalar(
            gcol[0:1, 7:8], gcol[0:1, 6:7], 0.5, ALU.is_lt),
          wait=[("dve", V("dve"))], inc="dve")
        dve_flag = V("dve")
        bank_cycle("pb3")
        MM([("dve", dve_flag)], pb[3][0:NE, 0:1], onerf[0:1, 0:NE],
           gcol[0:1, 7:8], "pb3", stop=True, inc="pe")
        pe_flb = V("pe")
        A("vector", lambda v: v.tensor_mul(gcol[:, 5:6], gcol[:, 5:6],
                                           pb[3][0:NE, 0:1]),
          wait=[("pe", pe_flb), ("dve", dve_fbc)], inc="dve")
        bank_read("pb3", "dve", V("dve"))
        xld_guard[0] = V("dve")
        A("vector", lambda v: v.tensor_single_scalar(
            gcol[:, 8:9], gcol[:, 2:3], 0.0, ALU.is_gt),
          wait=[("dve", dve_logc)], inc="dve")
        A("vector", lambda v: v.tensor_max(gcol[:, 8:9], gcol[:, 8:9],
                                           gcol[:, 5:6]),
          wait=[("dve", V("dve"))], inc="dve")
        dve_mfc = V("dve")
        A("vector", lambda v: v.tensor_copy(gcbf[:, 1:2], gcol[:, 8:9]),
          wait=[("dve", dve_mfc)], inc="dve")
        dve_mfb = V("dve")
        bank_cycle("pb3")
        MM([("dve", dve_mfb)], pb[3][0:1, 0:NE], gcbf[:, 1:2],
           cons[0:32, 0:NE], "pb3", stop=True, inc="pe")
        pe_mfr = V("pe")
        A("vector", lambda v: v.tensor_copy(gr(5), pb[3][0:1, 0:NE]),
          wait=[("pe", pe_mfr)], inc="dve")
        bank_read("pb3", "dve", V("dve"))
        A("vector", lambda v: v.tensor_mul(gr(6), gr(3), gr(5)),
          wait=[("dve", V("dve"))], inc="dve")
        A("vector", lambda v: v.tensor_scalar(
            gr(7), gr(5), -1.0, 1e30, ALU.add, ALU.mult),
          wait=[("dve", V("dve"))], inc="dve")
        A("vector", lambda v: v.tensor_add(gr(6), gr(6), gr(7)),
          wait=[("dve", V("dve"))], inc="dve")
        A("vector", lambda v: v.reduce_max(gcol[0:1, 9:10], gr(6),
                                           axis=AX.X, negate=True),
          wait=[("dve", V("dve"))], inc="dve")
        dve_ml = V("dve")
        A("scalar", lambda s: s.activation(gr(7), gr(6), AF.Exp,
                                           bias=gcol[0:1, 9:10]),
          wait=[("dve", dve_ml)], inc="act")
        A("vector", lambda v: v.reduce_sum(gcol[0:1, 10:11], gr(7), axis=AX.X),
          wait=[("act", V("act"))], inc="dve")
        A("vector", lambda v: v.reciprocal(gcol[0:1, 11:12],
                                           gcol[0:1, 10:11]),
          wait=[("dve", V("dve"))], inc="dve")
        A("vector", lambda v: v.tensor_scalar(
            gr(7), gr(7), gcol[0:1, 11:12], None, ALU.mult),
          wait=[("dve", V("dve"))], inc="dve")
        A("vector", lambda v: v.tensor_mul(gr(7), gr(7), gr(5)),
          wait=[("dve", V("dve"))], inc="dve")
        dve_r = V("dve")
        if dbg:
            A("gpsimd", lambda g: g.dma_start(out=dbg_outs["dbg_r"],
                                              in_=gr(7)),
              wait=[("dve", dve_r)], inc="sdbg", n=16)
        A("vector", lambda v: v.tensor_copy(rowb[0:1, 0:NE], gr(7)),
          wait=[("dve", dve_r), ("act", V("act")), ("sl", sd_l2f)], inc="dve")
        dve_rbf = V("dve")
        f32a_rd = dve_rbf
        bank_cycle("pb3")
        MM([("dve", dve_rbf)], pb[3][0:NE, 0:1], rowb[0:1, 0:NE],
           onecb[0:1, 0:1], "pb3", stop=True, inc="pe")
        pe_rc = V("pe")
        A("vector", lambda v: v.tensor_copy(gcbf[:, 0:1], pb[3][0:NE, 0:1]),
          wait=[("pe", pe_rc)], inc="dve")
        dve_rcol = V("dve")
        bank_read("pb3", "dve", dve_rcol)
        bank_cycle("pb1")
        for kc in range(16):
            MM([("dve", dve_rcol), ("ss", sd_sim)] if kc == 0 else [],
               pb[1][:, kc:kc + 1], sels[:, 128 * kc:128 * (kc + 1)],
               gcbf[:, 0:1], "pb1", stop=True,
               inc="pe" if kc == 15 else None, group=f"pb1:rt{kc}")
        pe_rt = V("pe")
        A("vector", lambda v: v.tensor_copy(stat[:, 0:16], pb[1][:, 0:16]),
          wait=[("pe", pe_rt)], inc="dve")
        dve_rt = V("dve")
        bank_read("pb1", "dve", dve_rt)
        for kc in range(16):
            A("vector", lambda v, kc=kc: v.tensor_scalar(
                rwv_t[:, M * kc:M * (kc + 1)], istks[:], stat[:, kc:kc + 1],
                None, ALU.mult),
              wait=[("dve", dve_rt), ("pe", pe_headloop),
                    ("ss", sd_sim)] if kc == 0 else [],
              inc="dve")
        dve_rwv = V("dve")
        pe_wo_done = V("pe")
        for kc in range(16):
            A("gpsimd", lambda g, kc=kc: g.dma_start(
                out=hs_fm[:, TC * kc:TC * (kc + 1)], in_=wv2[kc]),
              wait=[("pe", pe_wo_done)] if kc == 0 else [], inc="sv", n=16)
        sd_wv = V("sv")
        bank_cycle("pf0", "pf1")
        wef_pv = [0, 0]
        for cc in range(16):
            ping = cc % 2
            for kc in range(16):
                MM([("sv", sd_wv), ("dve", dve_rwv)] if kc == 0 else [],
                   pf[ping][:, 0:64],
                   hs_fm[:, TC * kc + 128 * cc:TC * kc + 128 * (cc + 1)],
                   rwv_t[:, M * kc:M * (kc + 1)], f"pf{ping}", stop=(kc == 15),
                   inc="pe" if kc == 15 else None)
            pe_wef = V("pe")
            A("scalar", lambda s, cc=cc, ping=ping: s.copy(
                wefft_t[:, M * cc:M * (cc + 1)], pf[ping][:, 0:64]),
              wait=[("pe", pe_wef)], inc="act")
            wef_pv[ping] = V("act")
            bank_read(f"pf{ping}", "act", wef_pv[ping])
            bank_cycle(f"pf{ping}")
        act_wef = V("act")
        pe_wef_all = V("pe")
        if dbg:
            A("gpsimd", lambda g: g.dma_start(out=dbg_outs["dbg_weff"],
                                              in_=wefft_t[:, 0:16 * M]),
              wait=[("act", act_wef)], inc="sdbg", n=16)
        for kc in range(16):
            A("gpsimd", lambda g, kc=kc: g.dma_start(
                out=hs_fm[:, TC * kc:TC * (kc + 1)], in_=wo2[kc]),
              wait=[("pe", pe_wef_all)] if kc == 0 else [], inc="sv", n=16)
        sd_wo2 = V("sv")
        for dc in range(4):
            ping = dc % 2
            for kc in range(16):
                MM([("sv", sd_wo2)] if kc == 0 else [],
                   pf[ping][0:64, :], rwv_t[:, M * kc:M * (kc + 1)],
                   hs_fm[:, TC * kc + 512 * dc:TC * kc + 512 * (dc + 1)],
                   f"pf{ping}", stop=(kc == 15),
                   inc="pe" if kc == 15 else None)
            pe_wod = V("pe")
            A("scalar", lambda s, dc=dc, ping=ping: s.copy(
                wodyn_t[0:64, 512 * dc:512 * (dc + 1)], pf[ping][0:64, :]),
              wait=[("pe", pe_wod)], inc="act")
            bank_read(f"pf{ping}", "act", V("act"))
            bank_cycle(f"pf{ping}")
        act_wod = V("act")
        if dbg:
            A("gpsimd", lambda g: g.dma_start(out=dbg_outs["dbg_wod"],
                                              in_=wodyn_t[0:64, 0:D]),
              wait=[("act", act_wod)], inc="sdbg", n=16)

        l2l_guard = [pe_headloop, pe_headloop]
        f32ab_rd = [f32a_rd, f32a_rd]
        comb_fin_rd = [0, 0]
        fin_ev = 0
        out_dma = 0
        for tc in range(NT):
            buf = tc % 2
            A("sync", lambda s, tc=tc, buf=buf: s.dma_start(
                out=l2fs[buf][:].rearrange("p (c t) -> p c t", t=128),
                in_=l2f_d[tc].transpose([1, 0, 2])),
              wait=[("sl", sd_l2f), ("pe", l2l_guard[buf])],
              inc=f"hl{buf}", n=16)
            hd_l2 = (f"hl{buf}", V(f"hl{buf}"))
            A("sync", lambda s, tc=tc, buf=buf: s.dma_start(
                out=xf32[buf][:], in_=x2_d[tc]),
              wait=[("hs2", hd_wo), ("dve", xld_guard[buf]),
                    ("hout", V("hout"))],
              inc=f"hx{buf}", n=16)
            hd_x2t = (f"hx{buf}", V(f"hx{buf}"))
            cb_bank = f"pf{buf}"
            for cc in range(16):
                MM([hd_l2, ("act", act_wef)] if cc == 0 else [],
                   pf[buf][0:64, 0:128], wefft_t[:, M * cc:M * (cc + 1)],
                   l2fs[buf][:, 128 * cc:128 * (cc + 1)], cb_bank,
                   stop=(cc == 15), inc="pe" if cc == 15 else None,
                   group=f"cb:{tc}")
            pe_comb = V("pe")
            l2l_guard[buf] = pe_comb
            cslot = comb_t[0:64, 128 * buf:128 * (buf + 1)]
            A("scalar", lambda s, cslot=cslot, buf=buf: s.copy(
                cslot, pf[buf][0:64, 0:128]),
              wait=[("pe", pe_comb), ("pe", comb_fin_rd[buf]),
                    ("dve", dve_headend)], inc="act")
            act_comb = V("act")
            if dbg and tc == 0:
                A("gpsimd", lambda g, cslot=cslot: g.dma_start(
                    out=dbg_outs["dbg_comb"], in_=cslot),
                  wait=[("act", act_comb)], inc="sdbg", n=16)
                A("gpsimd", lambda g, buf=buf: g.dma_start(
                    out=dbg_outs["dbg_l2f"], in_=l2fs[buf][:]),
                  wait=[hd_l2], inc="sdbg", n=16)
            bank_read(cb_bank, "act", act_comb)
            bank_cycle("pb0", "pb1", "pb2", "pb3")
            for j in range(4):
                MM([("act", act_comb), ("act", act_wod)] if j == 0 else [],
                   pb[j][:, :], cslot,
                   wodyn_t[0:64, 512 * j:512 * (j + 1)], f"pb{j}", stop=True,
                   inc="pe" if j == 3 else None)
            pe_fin = V("pe")
            comb_fin_rd[buf] = pe_fin
            fbuf = [f32a, f32b][buf]
            for j in range(4):
                A("scalar", lambda s, j=j, fbuf=fbuf: s.copy(
                    fbuf[:, 512 * j:512 * (j + 1)], pb[j][:, :]),
                  wait=[("pe", pe_fin), ("dve", f32ab_rd[buf]),
                        ("pe", pe_aff)] if j == 0 else [],
                  inc="act")
                bank_read(f"pb{j}", "act", V("act"))
            fin_ev = V("act")
            if dbg and tc == 0:
                A("gpsimd", lambda g, fbuf=fbuf: g.dma_start(
                    out=dbg_outs["dbg_ffn"], in_=fbuf[:]),
                  wait=[("act", fin_ev)], inc="sdbg", n=16)

            act_r3 = ln_stats(fbuf[:], [("act", fin_ev)])
            A("vector", lambda v, fbuf=fbuf: v.tensor_scalar(
                fbuf[:], fbuf[:], stat[:, 2:3], stat[:, 7:8],
                ALU.subtract, ALU.mult),
              wait=[("act", act_r3)], inc="dve")
            A("vector", lambda v, buf=buf, fbuf=fbuf: v.tensor_add(
                xf32[buf][:], xf32[buf][:], fbuf[:]),
              wait=[("dve", V("dve")), hd_x2t], inc="dve")
            dve_out = V("dve")
            xld_guard[buf] = dve_out
            f32ab_rd[buf] = dve_out
            A("sync", lambda s, tc=tc, buf=buf: s.dma_start(
                out=out[128 * tc:128 * (tc + 1), :], in_=xf32[buf][:]),
              wait=[("dve", dve_out)], inc="hout", n=16)
            out_dma = V("hout")

        A("sync", lambda s: s.nop(),
          wait=[("hout", out_dma), ("cc", V("cc"))] +
               [(s_, V(s_)) for s_ in SEMS if s_ not in
                ("pe", "act", "dve", "cc", "hout")])

        _DIAG["em"] = em
        block = e(nc.Block())
        em.replay(block, sems)

    return nc


_CACHE = {}


def _host_inputs(inputs):
    def wt(W):
        W = np.ascontiguousarray(np.asarray(W, np.float32))
        return np.ascontiguousarray(
            W.T.reshape(16, 128, 16, 128).transpose(2, 1, 0, 3))

    base = {
        "wqt": wt(inputs["L0_attn_Wq"]),
        "wkt": wt(inputs["L0_attn_Wk"]),
        "wvt": wt(inputs["L0_attn_Wv"]),
        "wgt": wt(inputs["L0_attn_Wg"]),
        "wot": np.ascontiguousarray(
            np.asarray(inputs["L0_attn_Wo"], np.float32).T).reshape(16, 128, D),
        "ws1t": np.ascontiguousarray(
            np.asarray(inputs["L0_attn_Ws1"], np.float32)
            .T.reshape(16, 128, 16).transpose(1, 0, 2)),
        "ws2t": np.ascontiguousarray(np.asarray(inputs["L0_attn_Ws2"],
                                                np.float32).T),
        "bs2": np.asarray(inputs["L0_attn_bs2"], np.float32).reshape(1, -1),
        "gnw": np.asarray(inputs["L0_attn_g_norm_weight"],
                          np.float32).reshape(1, -1),
        "simm": np.ascontiguousarray(
            np.asarray(inputs["L0_ffn_sim_matrix"],
                       np.float32).reshape(16, 128, NE)),
        "gates": np.asarray(inputs["L0_ffn_gates"], np.float32).reshape(1, -1),
        "wv2": np.ascontiguousarray(
            np.asarray(inputs["L0_ffn_w_v"], np.float32).reshape(16, 128, D)),
        "wo2": np.ascontiguousarray(
            np.asarray(inputs["L0_ffn_o_weights"],
                       np.float32).reshape(16, 128, D)),
    }

    x = np.asarray(inputs["L0_attn_hidden_states"], np.float32)
    inv = 1.0 / 10000.0 ** (np.arange(0, DK, 2, dtype=np.float64) / DK)
    maps = []
    for c in range(8):
        b, half = c // 2, c % 2
        pos = np.arange(half * TC, (half + 1) * TC, dtype=np.float64)
        ang = pos[None, :] * inv[:, None]
        cos = np.cos(ang).astype(np.float32)
        sin = np.sin(ang).astype(np.float32)
        m = dict(base)
        m["xin"] = np.ascontiguousarray(x[b, half * TC:(half + 1) * TC])
        m["cosf"] = np.concatenate([cos, cos], 0)
        m["sins"] = np.concatenate([sin, -sin], 0)
        m["spref"] = np.full((128, 1), float(half), np.float32)
        maps.append(m)
    return maps


def kernel(**inputs):
    _ = inputs["L0_ffn_w_q"], inputs["L0_ffn_w_k"]
    if "nc" not in _CACHE:
        _CACHE["nc"] = build_nc()
    nc = _CACHE["nc"]
    maps = _host_inputs(inputs)
    res = run_bass_kernel_spmd(nc, maps, core_ids=list(range(8)))
    out = np.empty((B, T, D), np.float32)
    for c in range(8):
        b, half = c // 2, c % 2
        out[b, half * TC:(half + 1) * TC] = res.results[c]["out"]
    return out


# version 19
# speedup vs baseline: 1.0195x; 1.0007x over previous
import numpy as np
import ml_dtypes
from contextlib import ExitStack

import concourse.bass as bass
import concourse.mybir as mybir
from concourse.bass_utils import run_bass_kernel_spmd

F32 = mybir.dt.float32
BF16 = mybir.dt.bfloat16
AF = mybir.ActivationFunctionType
ALU = mybir.AluOpType
AX = mybir.AxisListType

B, T, D = 4, 4096, 2048
TC = 2048
NT = 16
NH = 16
DK = DV = 128
M = 64
EPS = 1e-5
SCALE = DK ** -0.5
E32 = float(np.exp(32.0))
EM32 = float(np.exp(-32.0))
NE = 32

ENGS = ("sync", "gpsimd", "scalar", "vector", "tensor")
SEMS = ("pe", "act", "dve", "cc",
        "hx0", "hx1", "ho", "ho1", "hl0", "hl1", "hs2", "hout", "hg", "hof",
        "sc", "sw0", "sw1", "so", "sl", "sp", "sr", "sq", "sv", "ss", "sdbg")


class Em:

    def __init__(self):
        self.items = {e: [] for e in ENGS}
        self.cnt = {s: 0 for s in SEMS}

    def v(self, sem):
        return self.cnt[sem]

    def add(self, eng, fn=None, wait=(), inc=None, n=1):
        self.items[eng].append((list(wait), fn, inc, n))
        if inc:
            self.cnt[inc] += n
            return self.cnt[inc]
        return None

    def replay(self, block, sems):
        for ename in ENGS:
            items = self.items[ename]

            def body(eng, items=items):
                seen = {}
                for (waits, fn, inc, n) in items:
                    for (s, v) in waits:
                        if v > 0 and seen.get(s, 0) < v:
                            eng.wait_ge(sems[s], v)
                            seen[s] = v
                    ins = fn(eng) if fn is not None else None
                    if inc:
                        ins.then_inc(sems[inc], n)
            getattr(block, ename)(body)


def _bf(x):
    return np.asarray(x, dtype=ml_dtypes.bfloat16)


_DIAG = {}


def build_nc(dbg=False, n_heads=NH):
    nc = bass.Bass("TRN2", target_bir_lowering=False, debug=False)

    def din(name, shape, dt=F32):
        return nc.dram_tensor(name, list(shape), dt, kind="ExternalInput").ap()

    xin = din("xin", (TC, D))
    wqt = din("wqt", (NH, 128, 16, 128))
    wkt = din("wkt", (NH, 128, 16, 128))
    wvt = din("wvt", (NH, 128, 16, 128))
    wgt = din("wgt", (NH, 128, 16, 128))
    wot = din("wot", (NH, 128, D))
    ws1t = din("ws1t", (128, 16, 16))
    ws2t = din("ws2t", (16, NH * M))
    bs2 = din("bs2", (1, NH * M))
    gnw = din("gnw", (1, DV))
    simm = din("simm", (16, 128, NE))
    gates = din("gates", (1, NE))
    wv2 = din("wv2", (16, 128, D))
    wo2 = din("wo2", (16, 128, D))
    cosf = din("cosf", (128, TC))
    sins = din("sins", (128, TC))
    spref = din("spref", (128, 1))

    out = nc.dram_tensor("out", [TC, D], F32, kind="ExternalOutput").ap()

    ident = nc.inline_tensor(_bf(np.eye(128)), name="c_ident").ap()
    triu = nc.inline_tensor(_bf(np.triu(np.ones((128, 128)))), name="c_triu").ap()
    onec_b = nc.inline_tensor(_bf(np.ones((128, 1))), name="c_onec").ap()
    onec_f = nc.inline_tensor(np.ones((128, 1), np.float32), name="c_onecf").ap()
    oner_b = nc.inline_tensor(_bf(np.ones((1, 128))), name="c_oner").ap()
    oner_f = nc.inline_tensor(np.ones((1, 128), np.float32), name="c_onerf").ap()
    eps_c = nc.inline_tensor(np.full((128, 1), EPS, np.float32), name="c_eps").ap()
    istk = nc.inline_tensor(_bf(np.vstack([np.eye(M), np.eye(M)])),
                            name="c_istk").ap()
    sel_np = np.zeros((NE, 16 * 128), np.float32)
    for kc in range(16):
        sel_np[2 * kc, kc * 128: kc * 128 + 64] = 1.0
        sel_np[2 * kc + 1, kc * 128 + 64: kc * 128 + 128] = 1.0
    sel = nc.inline_tensor(_bf(sel_np), name="c_sel").ap()

    og_d = nc.dram_tensor("og_d", [NH, NT, 128, 128], BF16).ap()
    x2_d = nc.dram_tensor("x2_d", [NT, 128, D], F32).ap()
    l2f_d = nc.dram_tensor("l2f_d", [NT, 16, 128, 128], BF16).ap()
    st_in = [nc.dram_tensor(f"st_in{i}", [1, 128 * 256], BF16).ap()
             for i in range(2)]
    st_out = [nc.dram_tensor(f"st_out{i}", [2, 128 * 256], BF16).ap()
              for i in range(2)]
    sq_in = nc.dram_tensor("sq_in", [1, D], F32).ap()
    sq_out = nc.dram_tensor("sq_out", [2, D], F32).ap()

    dbg_outs = {}
    if dbg:
        for nm, shape in [("dbg_hs", (128, 2048)), ("dbg_q", (128, TC)),
                          ("dbg_k", (128, TC)), ("dbg_es", (128, NT * M)),
                          ("dbg_o", (128, TC)), ("dbg_x2", (128, D)),
                          ("dbg_r", (1, NE)), ("dbg_enz", (128, NT * M)),
                          ("dbg_u", (128, NT * M)), ("dbg_weff", (128, 16 * M)),
                          ("dbg_wod", (64, D)), ("dbg_comb", (64, 128)),
                          ("dbg_l2f", (128, D)), ("dbg_ffn", (128, D))]:
            dbg_outs[nm] = nc.dram_tensor(nm, list(shape), F32,
                                          kind="ExternalOutput").ap()

    em = Em()
    A, V = em.add, em.v
    RG = [[0, 1], [2, 3], [4, 5], [6, 7]]

    with ExitStack() as stk:
        e = stk.enter_context

        def sb(name, shape, dt):
            return e(nc.sbuf_tensor(name, list(shape), dt))

        hs_fm = sb("hs_fm", (128, 16 * TC), BF16)
        raw = sb("raw", (128, TC), BF16)
        rope2 = sb("rope2", (128, TC), BF16)
        q_fm = sb("q_fm", (128, TC), BF16)
        k_fm = sb("k_fm", (128, TC), BF16)
        v_tm = sb("v_tm", (128, TC), BF16)
        sg = sb("sg", (128, TC), BF16)
        o_sb = sb("o_sb", (128, TC), BF16)
        ws1s = o_sb[:, 0:16 * 16]
        istks = o_sb[:, 256:256 + M]
        es_tm = sb("es_tm", (128, NT * M), BF16)
        es_fm = sb("es_fm", (128, 8 * 128), BF16)
        u_tm = sb("u_tm", (128, NT * M), BF16)
        u_fm = sb("u_fm", (128, 8 * 128), BF16)
        rsn = sb("rsn", (128, NT * M), BF16)
        psn = sb("psn", (128, 8 * 128), BF16)
        rf32 = sb("rf32", (128, NT * M), F32)
        pf32u = sb("pf32u", (64, NT * 128), F32)
        ktm = sb("ktm", (128, 2 * 128), BF16)
        stm = sb("stm", (128, 2 * 128), BF16)
        sq_bf = sb("sq_bf", (128, TC), BF16)
        sels = sq_bf[0:32, 0:16 * 128]
        carry = sq_bf[0:1, 0:(NT + 1) * M]
        wbuf = [sb(f"wbuf{p}", (128, 4 * TC), BF16) for p in range(2)]
        cosb = sb("cosb", (128, TC), BF16)
        sinb = sb("sinb", (128, TC), BF16)
        xf32 = [sb(f"xf32{p}", (128, D), F32) for p in range(2)]
        f32a = sb("f32a", (128, D), F32)
        f32b = sb("f32b", (128, D), F32)
        hs_tm = sb("hs_tm", (128, D), BF16)
        t1s = sb("t1s", (16, TC), BF16)
        ws2s = sb("ws2s", (16, NH * M), BF16)
        bs2s = sb("bs2s", (1, NH * M), BF16)
        gnws = sb("gnws", (1, DV), BF16)
        stg = sb("stg", (128, 256), BF16)
        rcv = sb("rcv", (128, 256), BF16)
        cons = sb("cons", (128, 128), BF16)
        trius = sb("trius", (128, 128), BF16)
        onecb = sb("onecb", (128, 1), BF16)
        onecf = sb("onecf", (128, 1), F32)
        onerb = sb("onerb", (1, 128), BF16)
        epsc = sb("epsc", (128, 1), F32)
        sprefs = sb("sprefs", (128, 1), F32)
        stat = sb("stat", (128, 42), F32)
        bn6 = stat[:, 16:40]
        sls = sb("sls", (128, 16), F32)
        srep = sb("srep", (128, 16), F32)
        stat2 = srep
        gcol = sls[0:32, :]
        gcbf = sb("gcbf", (32, 2), BF16)
        sim_sb = None
        grow = None
        rwv_t = v_tm
        wefft_t = k_fm
        wodyn_t = q_fm
        comb_t = sg
        l2fs = [wbuf[p][:, 0:D] for p in range(2)]

        smf1 = f32a[:, 0:NT * M]
        smf2 = f32a[:, NT * M:2 * NT * M]
        enza = f32b[:, 0:NT * M]
        enzb = f32b[:, NT * M:2 * NT * M]
        rowa = f32a[0:1, 0:2048]
        rowb = hs_tm[0:1, 0:2048]
        crow = f32a[0:1, 0:(NT + 1) * M]
        onerf = xf32[0][0:1, 0:128]
        sim_sb = xf32[1][:, 0:16 * NE]
        grow = f32a[0:1, 1024:1536]

        pb = [e(nc.psum_tensor(f"pb{i}", (128, 512), F32)) for i in range(4)]
        pt0 = e(nc.psum_tensor("pt0", (128, 1024), BF16))
        pw = e(nc.psum_tensor("pw", (128, 1024), BF16))
        pf = [e(nc.psum_tensor(f"pf{i}", (128, 512), F32)) for i in range(2)]
        ptq = [pt0, pw]

        def tpb(ping):
            return "pt0" if ping == 0 else "pw"

        ptA = [pt0[:, 0:128], pw[:, 0:128]]
        ptV = [pt0[:, 128:256], pw[:, 128:256]]
        ptE = [pt0[:, 256:384], pw[:, 256:384]]
        ptK = [pt0[:, 384:512], pw[:, 384:512]]
        pwRs = [pf[0][:, 0:64], pf[1][:, 0:64]]
        pwPs = [pf[1][0:64, 64:192], pf[0][0:64, 64:192]]
        pwCB = pf[0][:, 192:256]

        sems = {s: e(nc.semaphore(s)) for s in SEMS}

        fresh = {}
        groups_of = {}
        guards = {}

        def MM(eng_wait, out_ap, lhsT, rhs, bank, stop=True, inc=None,
               group=None):
            g_ = group or bank
            groups_of.setdefault(bank, set()).add(g_)
            w = list(eng_wait) + guards.pop(bank, [])
            st_flag = not fresh.get(g_, False)
            fresh[g_] = True
            return A("tensor",
                     lambda t, o=out_ap, l=lhsT, r=rhs, sf=st_flag, sp=stop:
                     t.matmul(o, l, r, start=sf, stop=sp),
                     wait=w, inc=inc)

        def TP(eng_wait, out_ap, in_ap, bank, inc="pe"):
            w = list(eng_wait) + guards.pop(bank, [])
            groups_of.setdefault(bank, set()).add(bank)
            fresh[bank] = True
            return A("tensor",
                     lambda t, o=out_ap, i=in_ap: t.transpose(o, i, cons[:]),
                     wait=w, inc=inc)

        def bank_read(bank, sem, val):
            guards.setdefault(bank, []).append((sem, val))

        def bank_cycle(*banks):
            for b_ in banks:
                for g_ in groups_of.get(b_, {b_}):
                    fresh[g_] = False
                fresh[b_] = False

        def slot(i):
            if i < 8:
                return pb[2][:, 64 * i:64 * i + 64], "pb2", f"pb2:{i}"
            return pb[3][:, 64 * (i - 8):64 * (i - 8) + 64], "pb3", f"pb3:{i}"

        for dst, src in [(cons, ident), (trius, triu), (onecb, onec_b),
                         (onecf, onec_f), (onerb, oner_b),
                         (sprefs, spref),
                         (cosb, cosf), (sinb, sins),
                         (ws2s, ws2t), (bs2s, bs2), (gnws, gnw),
                         (epsc, eps_c)]:

            A("gpsimd", lambda g, d=dst, s=src: g.dma_start(out=d[:], in_=s),
              inc="sc", n=16)
        A("gpsimd", lambda g: g.dma_start(
            out=ws1s[:], in_=ws1t.rearrange("p a b -> p (a b)")),
          inc="sc", n=16)
        sd_const = V("sc")

        def wload(h, p):
            for j, wt in enumerate((wqt, wkt, wvt, wgt)):
                A("gpsimd",
                  lambda g, wt=wt, h=h, p=p, j=j: g.dma_start(
                      out=wbuf[p][:, TC * j:TC * (j + 1)],
                      in_=wt[h].rearrange("p c o -> p (c o)")),
                  inc=f"sw{p}", n=16)
            return (f"sw{p}", V(f"sw{p}"))

        sd_w = [("sw0", 0)] * (NH + 1)
        sd_w[0] = wload(0, 0)

        xld_guard = [0, 0]
        hd_x = [("hx0", 0)] * NT

        def xload(tc, extra=()):
            buf = tc % 2
            A("sync", lambda s, tc=tc, buf=buf: s.dma_start(
                out=xf32[buf][:], in_=xin[128 * tc:128 * (tc + 1), :]),
              wait=[("dve", xld_guard[buf])] + list(extra),
              inc=f"hx{buf}", n=16)
            return (f"hx{buf}", V(f"hx{buf}"))

        hd_x[0] = xload(0)
        hd_x[1] = xload(1)
        tp_ev = [0, 0]
        for tc in range(NT):
            buf = tc % 2
            w_x = [hd_x[tc], ("sc", sd_const)]
            for q in range(4):
                A("vector", lambda v, buf=buf, q=q: v.bn_stats(
                    bn6[:, 6 * q:6 * (q + 1)],
                    xf32[buf][:, 512 * q:512 * (q + 1)]),
                  wait=(w_x + [("dve", V("dve"))]) if q == 0 else [],
                  inc="dve")
            A("vector", lambda v: v.bn_aggr(stat[:, 2:4], bn6[:]),
              wait=[("dve", V("dve"))], inc="dve")
            A("scalar", lambda s: s.activation(
                stat[:, 6:7], stat[:, 3:4], AF.Ln, bias=epsc[:]),
              wait=[("dve", V("dve"))], inc="act")
            A("scalar", lambda s: s.activation(
                stat[:, 7:8], stat[:, 6:7], AF.Exp, scale=-0.5),
              wait=[("act", V("act"))], inc="act")
            act_rstd = V("act")
            A("vector", lambda v, buf=buf: v.tensor_scalar(
                hs_tm[:], xf32[buf][:], stat[:, 2:3], stat[:, 7:8],
                ALU.subtract, ALU.mult),
              wait=[("act", act_rstd)], inc="dve")
            dve_hs = V("dve")
            xld_guard[buf] = dve_hs
            if tc + 2 < NT:
                hd_x[tc + 2] = xload(tc + 2)
            for cc in range(16):
                ping = cc % 2
                TP([("dve", dve_hs)], ptA[ping],
                   hs_tm[:, 128 * cc:128 * (cc + 1)], tpb(ping))
                pe_tp = V("pe")
                A("vector", lambda v, tc=tc, cc=cc, ping=ping: v.tensor_copy(
                    hs_fm[:, TC * cc + 128 * tc: TC * cc + 128 * (tc + 1)],
                    ptA[ping]),
                  wait=[("pe", pe_tp)], inc="dve")
                tp_ev[ping] = V("dve")
                bank_read(tpb(ping), "dve", tp_ev[ping])
                bank_cycle(tpb(ping))
        dve_hs_all = V("dve")
        act_t1p = [0, 0]
        for tch in range(4):
            ping = tch % 2
            for cc in range(16):
                MM([("dve", dve_hs_all)] if cc == 0 else [],
                   pf[ping][0:16, :], ws1s[:, 16 * cc:16 * (cc + 1)],
                   hs_fm[:, TC * cc + 512 * tch: TC * cc + 512 * (tch + 1)],
                   f"pf{ping}", stop=(cc == 15),
                   inc="pe" if cc == 15 else None)
            pe_t1 = V("pe")
            A("scalar", lambda s, tch=tch, ping=ping: s.copy(
                t1s[:, 512 * tch:512 * (tch + 1)], pf[ping][0:16, :]),
              wait=[("pe", pe_t1)], inc="act")
            act_t1p[ping] = V("act")
            bank_read(f"pf{ping}", "act", act_t1p[ping])
            bank_cycle(f"pf{ping}")
        act_t1 = V("act")

        if dbg:
            A("gpsimd", lambda g: g.dma_start(out=dbg_outs["dbg_hs"],
                                            in_=hs_fm[:, 0:2048]),
              wait=[("dve", dve_hs_all)], inc="sdbg", n=16)

        pe_wread = [0, 0]
        raw_guard = [("dve", 0)]
        rope_guard = [0]
        og_sd = 0
        stg_rd = 0
        rcv_rd = 0
        cc_done = [0] * (NH + 2)
        rowb_rd = [0]
        vt_ev = [0, 0]
        esf_ev = [0, 0]
        ktm_ev = [0, 0]
        uf_ev = [0, 0]
        stm_rd = [0, 0]
        pe_okp = [0, 0]
        pe_Cp = [0, 0]
        pe_Cglob = 0
        dve_hs_all_ = dve_hs_all

        xb0 = xf32[0][:].bitcast(BF16)
        xb1 = xf32[1][:].bitcast(BF16)
        kfm = [k_fm, xb0[:, 0:2048]]
        vtm = [v_tm, xb0[:, 2048:4096]]
        estm = [es_tm, xb1[:, 0:1024]]
        esfm = [es_fm, xb1[:, 1024:2048]]
        carry = xb1[0:1, 2048:2048 + (NT + 1) * M]
        SPb = [[pb[2], pb[3]], [pb[0], pb[1]]]
        SPn = [["pb2", "pb3"], ["pb0", "pb1"]]

        def slot(i, p):
            b = SPn[p][i // 8]
            return (SPb[p][i // 8][:, 64 * (i % 8):64 * (i % 8) + 64], b,
                    f"{b}:s{i % 8}")

        HS = [dict() for _ in range(NH)]

        def projection(kind, h):
            hp = h % 2
            base = {"q": 0, "k": 1, "v": 2, "g": 3}[kind] * TC
            for tch in range(4):
                ping = tch % 2
                w0 = [sd_w[h], ("dve", dve_hs_all_)] if tch == 0 else []
                for cc in range(16):
                    MM(w0 if cc == 0 else [],
                       pf[ping][:, :],
                       wbuf[hp][:, base + 128 * cc:base + 128 * (cc + 1)],
                       hs_fm[:, TC * cc + 512 * tch:
                             TC * cc + 512 * (tch + 1)],
                       f"pf{ping}", stop=(cc == 15),
                       inc="pe" if cc == 15 else None)
                    w0 = []
                pe_pr = V("pe")
                A("scalar", lambda s, tch=tch, ping=ping: s.copy(
                    raw[:, 512 * tch:512 * (tch + 1)], pf[ping][:, :]),
                  wait=[("pe", pe_pr)] + list(raw_guard), inc="act")
                bank_read(f"pf{ping}", "act", V("act"))
                bank_cycle(f"pf{ping}")
            return V("act")

        def rope_into(dst, pe_war):
            act_raw = V("act")
            w = [("act", act_raw), ("sc", sd_const), ("pe", pe_war)]
            A("vector", lambda v, dst=dst: v.tensor_mul(
                dst[:], raw[:], cosb[:]),
              wait=w + [("dve", rope_guard[0])], inc="dve")
            A("vector", lambda v: v.tensor_mul(
                rope2[0:64, :], raw[64:128, :], sinb[64:128, :]),
              inc="dve")
            A("vector", lambda v: v.tensor_mul(
                rope2[64:128, :], raw[0:64, :], sinb[0:64, :]),
              inc="dve")
            dve_m = V("dve")
            A("vector", lambda v, dst=dst: v.tensor_add(
                dst[:], dst[:], rope2[:]),
              wait=[("dve", dve_m)], inc="dve")
            rope_guard[0] = V("dve")
            raw_guard[0] = ("dve", V("dve"))

        def eA1(h):
            st = HS[h]
            p = h % 2
            bank_cycle(*SPn[p])
            for i in range(NT):
                sl, bank, grp = slot(i, p)
                MM([("act", act_t1), ("sc", sd_const)] if i == 0 else [],
                   sl, t1s[:, 128 * i:128 * (i + 1)],
                   ws2s[:, M * h:M * (h + 1)], bank, stop=False, group=grp)
                MM([], sl, onerb[:], bs2s[:, M * h:M * (h + 1)], bank,
                   stop=True, inc="pe" if i == NT - 1 else None, group=grp)
            pe_s = V("pe")
            A("scalar", lambda s, p=p: s.activation(
                estm[p][:, 0:512], SPb[p][0][:, 0:512], AF.Exp),
              wait=[("pe", pe_s), ("pe", pe_okp[p])], inc="act")
            A("scalar", lambda s, p=p: s.activation(
                estm[p][:, 512:1024], SPb[p][1][:, 0:512], AF.Exp),
              inc="act")
            act_es0 = V("act")
            bank_read(SPn[p][0], "act", act_es0)
            bank_read(SPn[p][1], "act", act_es0)
            bank_cycle(*SPn[p])
            A("vector", lambda v, p=p: v.tensor_scalar(
                estm[p][:], estm[p][:], E32, None, ALU.min),
              wait=[("act", act_es0)], inc="dve")
            A("vector", lambda v, p=p: v.tensor_scalar(
                estm[p][:], estm[p][:], EM32, None, ALU.max),
              wait=[("dve", V("dve"))], inc="dve")
            st["dve_es"] = V("dve")

        def eCS(h, dve_f32a_free):
            st = HS[h]
            p = h % 2
            MM([("dve", st["dve_es"])], SPb[p][0][0:1, 0:512], onecb[:],
               estm[p][:, 0:512], SPn[p][0], stop=True, inc="pe",
               group=f"{SPn[p][0]}:cs{h}")
            pe_cs1 = V("pe")
            A("scalar", lambda s, p=p: s.copy(crow[0:1, M:M + 512],
                                              SPb[p][0][0:1, 0:512]),
              wait=[("pe", pe_cs1), ("dve", dve_f32a_free)], inc="act")
            cs_ev1 = V("act")
            bank_read(SPn[p][0], "act", cs_ev1)
            MM([("dve", st["dve_es"])], SPb[p][1][0:1, 0:512], onecb[:],
               estm[p][:, 512:1024], SPn[p][1], stop=True, inc="pe",
               group=f"{SPn[p][1]}:cs{h}")
            pe_cs2 = V("pe")
            A("scalar", lambda s, p=p: s.copy(crow[0:1, M + 512:M + 1024],
                                              SPb[p][1][0:1, 0:512]),
              wait=[("pe", pe_cs2)], inc="act")
            cs_ev2 = V("act")
            bank_read(SPn[p][1], "act", cs_ev2)
            A("vector", lambda v: v.memset(crow[0:1, 0:M], 0.0),
              wait=[("dve", dve_f32a_free)], inc="dve")
            for ci in range(1, NT + 1):
                A("vector", lambda v, ci=ci: v.tensor_add(
                    crow[0:1, M * ci:M * (ci + 1)],
                    crow[0:1, M * ci:M * (ci + 1)],
                    crow[0:1, M * (ci - 1):M * ci]),
                  wait=[("act", cs_ev1 if ci <= 8 else cs_ev2),
                        ("dve", V("dve"))], inc="dve")
            A("vector", lambda v: v.tensor_copy(carry[:], crow[:]),
              wait=[("dve", V("dve"))], inc="dve")
            st["dve_carry"] = V("dve")

        def eKV(h):
            st = HS[h]
            p = h % 2
            projection("k", h)
            rope_into(kfm[p], pe_okp[p])
            st["dve_rope"] = V("dve")
            projection("v", h)
            act_vraw = V("act")
            for i in range(NT):
                ping = i % 2
                TP([("act", act_vraw), ("pe", pe_Cp[p])],
                   ptV[ping], raw[:, 128 * i:128 * (i + 1)], tpb(ping))
                pe_v = V("pe")
                A("vector", lambda v, i=i, ping=ping, p=p: v.tensor_copy(
                    vtm[p][:, 128 * i:128 * (i + 1)], ptV[ping]),
                  wait=[("pe", pe_v)], inc="dve")
                vt_ev[ping] = V("dve")
                bank_read(tpb(ping), "dve", vt_ev[ping])
                bank_cycle(tpb(ping))
            st["dve_vtm"] = V("dve")
            raw_guard[0] = ("pe", V("pe"))

        def eES(h):
            st = HS[h]
            p = h % 2
            for j in range(8):
                ping = j % 2
                TP([("dve", st["dve_es"]), ("pe", pe_Cp[p])],
                   ptE[ping], estm[p][:, 128 * j:128 * (j + 1)], tpb(ping))
                pe_e = V("pe")
                A("vector", lambda v, j=j, ping=ping, p=p: v.tensor_copy(
                    esfm[p][:, 128 * j:128 * (j + 1)], ptE[ping]),
                  wait=[("pe", pe_e)], inc="dve")
                esf_ev[ping] = V("dve")
                bank_read(tpb(ping), "dve", esf_ev[ping])
                bank_cycle(tpb(ping))
            st["dve_esf"] = V("dve")

        def eA3(h):
            st = HS[h]
            p = h % 2
            A("vector", lambda v: v.memset(rf32[:, 0:M], 0.0), inc="dve")
            A("vector", lambda v: v.memset(pf32u[:, 0:128], 0.0), inc="dve")
            chain = V("dve")
            for i in range(NT):
                ping = i % 2
                TP([("dve", st["dve_rope"]), ("dve", ktm_ev[ping])],
                   ptK[ping], kfm[p][:, 128 * i:128 * (i + 1)], tpb(ping))
                pe_kt = V("pe")
                A("vector", lambda v, ping=ping: v.tensor_copy(
                    ktm[:, 128 * ping:128 * (ping + 1)], ptK[ping]),
                  wait=[("pe", pe_kt)], inc="dve")
                ktm_ev[ping] = V("dve")
                bank_read(tpb(ping), "dve", ktm_ev[ping])
                bank_cycle(tpb(ping))
                wR = [("dve", ktm_ev[ping])]
                if i == 0:
                    wR += [("dve", st["dve_es"]), ("dve", st["dve_vtm"])]
                bkR, bkP = ("pf0", "pf1") if ping == 0 else ("pf1", "pf0")
                MM(wR, pwRs[ping], ktm[:, 128 * ping:128 * (ping + 1)],
                   estm[p][:, M * i:M * (i + 1)], bkR, stop=True,
                   inc="pe", group=f"pfR:{h}:{i}")
                pe_R = V("pe")
                MM([], pwPs[ping], estm[p][:, M * i:M * (i + 1)],
                   vtm[p][:, 128 * i:128 * (i + 1)], bkP, stop=True,
                   inc="pe", group=f"pfP:{h}:{i}")
                pe_P = V("pe")
                if i < NT - 1:
                    A("vector", lambda v, i=i, ping=ping: v.tensor_add(
                        rf32[:, M * (i + 1):M * (i + 2)],
                        rf32[:, M * i:M * (i + 1)], pwRs[ping]),
                      wait=[("pe", pe_R), ("dve", chain)], inc="dve")
                    bank_read(bkR, "dve", V("dve"))
                    A("vector", lambda v, i=i, ping=ping: v.tensor_add(
                        pf32u[:, 128 * (i + 1):128 * (i + 2)],
                        pf32u[:, 128 * i:128 * (i + 1)], pwPs[ping]),
                      wait=[("pe", pe_P), ("dve", V("dve"))], inc="dve")
                    chain = V("dve")
                    bank_read(bkP, "dve", chain)
                else:
                    st["pe_R15"], st["pe_P15"] = pe_R, pe_P
                sl, bank, grp = slot(i, p)
                MM([("dve", st["dve_carry"])] if i == 0 else [], sl, trius[:],
                   estm[p][:, M * i:M * (i + 1)], bank, stop=False, group=grp)
                MM([], sl, onerb[:], carry[0:1, M * i:M * (i + 1)], bank,
                   stop=True, inc="pe" if i == NT - 1 else None, group=grp)
            st["pe_state"] = V("pe")
            st["dve_chain"] = chain
            A("vector", lambda v: v.tensor_add(
                stg[:, 0:64], rf32[:, (NT - 1) * M:NT * M], pwRs[1]),
              wait=[("dve", st["dve_chain"]), ("pe", st["pe_R15"]),
                    ("sp", stg_rd)],
              inc="dve")
            bank_read("pf1", "dve", V("dve"))
            A("vector", lambda v: v.tensor_add(
                stg[0:64, 64:192], pf32u[:, (NT - 1) * 128:NT * 128],
                pwPs[1]),
              wait=[("pe", st["pe_P15"])], inc="dve")
            A("vector", lambda v: v.tensor_add(
                stg[64:128, 64:192], pf32u[:, (NT - 1) * 128:NT * 128],
                pwPs[1]),
              inc="dve")
            st["dve_stg"] = V("dve")
            bank_read("pf0", "dve", st["dve_stg"])
            MM([("dve", st["dve_carry"])], pwCB, onerb[:],
               carry[0:1, NT * M:(NT + 1) * M], "pf0", stop=True, inc="pe",
               group=f"pfCB:{h}")
            pe_cb = V("pe")
            A("vector", lambda v: v.tensor_copy(stg[:, 192:256], pwCB),
              wait=[("pe", pe_cb), ("sp", stg_rd)], inc="dve")
            st["dve_stg2"] = V("dve")
            bank_read("pf0", "dve", st["dve_stg2"])

        def ePush(h):
            st = HS[h]
            dve_stg = st["dve_stg"]
            dve_stg2 = st["dve_stg2"]
            A("gpsimd", lambda g, h=h: g.dma_start(
                out=st_in[h % 2].rearrange("o (p f) -> (o p) f", p=128),
                in_=stg[:]),
              wait=[("dve", dve_stg2), ("dve", dve_stg),
                    ("cc", cc_done[h - 2] if h >= 2 else 0)],
              inc="sp", n=16)
            sd_push = V("sp")
            A("gpsimd", lambda g, h=h: g.collective_compute(
                "AllGather", ALU.bypass, replica_groups=RG,
                ins=[st_in[h % 2].opt()], outs=[st_out[h % 2].opt()]),
              wait=[("sp", sd_push)], inc="cc")
            cc_done[h] = V("cc")
            A("gpsimd", lambda g, h=h: g.dma_start(
                out=rcv[:], in_=st_out[h % 2][0:1, :].rearrange(
                    "o (p f) -> (o p) f", p=128)),
              wait=[("cc", cc_done[h]), ("dve", rcv_rd)], inc="sr", n=16)
            st["sd_rcv"] = V("sr")
            return sd_push

        def eQG(h, pe_q_war, dve_sg_war):
            st = HS[h]
            projection("q", h)
            rope_into(q_fm, pe_q_war)
            st["dve_qrope"] = V("dve")
            projection("g", h)
            act_graw = V("act")
            A("scalar", lambda s: s.activation(sg[:], raw[:], AF.Silu),
              wait=[("act", act_graw), ("dve", dve_sg_war)], inc="act")
            st["act_silu"] = V("act")
            raw_guard[0] = ("act", st["act_silu"])
            pe_wread[h % 2] = V("pe")

        def eFold(h):
            st = HS[h]
            p = h % 2
            A("vector", lambda v: v.scalar_tensor_tensor(
                rsn[:].rearrange("p (i m) -> p i m", m=M),
                rcv[:, 0:64].unsqueeze(1).broadcast_to([128, NT, M]),
                sprefs[:],
                rf32[:, 0:NT * M].rearrange("p (i m) -> p i m", m=M),
                ALU.mult, ALU.add),
              wait=[("sr", st["sd_rcv"]), ("dve", st["dve_chain"]),
                    ("pe", pe_okp[1 - p])],
              inc="dve")
            pf32v = pf32u[:, 0:NT * 128].rearrange(
                "p (a two f) -> p a two f", two=2, f=128)
            A("vector", lambda v: v.scalar_tensor_tensor(
                psn[0:64, :].rearrange("p (j one f) -> p j one f",
                                       one=1, f=128),
                rcv[0:64, 64:192].unsqueeze(1).unsqueeze(2)
                .broadcast_to([64, 8, 1, 128]),
                sprefs[0:64, :],
                pf32v[:, :, 0:1, :],
                ALU.mult, ALU.add),
              wait=[("pe", pe_Cglob)], inc="dve")
            A("vector", lambda v: v.scalar_tensor_tensor(
                psn[64:128, :].rearrange("p (j one f) -> p j one f",
                                         one=1, f=128),
                rcv[0:64, 64:192].unsqueeze(1).unsqueeze(2)
                .broadcast_to([64, 8, 1, 128]),
                sprefs[0:64, :],
                pf32v[:, :, 1:2, :],
                ALU.mult, ALU.add),
              inc="dve")
            st["dve_snaps"] = V("dve")
            A("vector", lambda v, p=p: v.scalar_tensor_tensor(
                enza[:, 0:512].rearrange("p (i m) -> p i m", m=M),
                rcv[:, 192:256].unsqueeze(1).broadcast_to([128, 8, M]),
                sprefs[:],
                SPb[p][0][:, 0:512].rearrange("p (i m) -> p i m", m=M),
                ALU.mult, ALU.add),
              wait=[("sr", st["sd_rcv"]), ("pe", st["pe_state"])],
              inc="dve")
            A("vector", lambda v, p=p: v.scalar_tensor_tensor(
                enza[:, 512:1024].rearrange("p (i m) -> p i m", m=M),
                rcv[:, 192:256].unsqueeze(1).broadcast_to([128, 8, M]),
                sprefs[:],
                SPb[p][1][:, 0:512].rearrange("p (i m) -> p i m", m=M),
                ALU.mult, ALU.add),
              inc="dve")
            dve_c = V("dve")
            st["dve_c"] = dve_c
            bank_read(SPn[p][0], "dve", dve_c)
            bank_read(SPn[p][1], "dve", dve_c)
            bank_cycle(*SPn[p])
            A("scalar", lambda s: s.activation(enzb[:], enza[:], AF.Ln),
              wait=[("dve", dve_c)], inc="act")
            A("scalar", lambda s: s.activation(enzb[:], enzb[:], AF.Exp,
                                               scale=-1.0),
              wait=[("act", V("act"))], inc="act")
            st["act_enz"] = V("act")

        def eB2(h):
            st = HS[h]
            p = h % 2
            for i in range(NT):
                ping = i % 2
                sl, bank, grp = slot(i, p)
                MM([("dve", st["dve_snaps"]), ("act", st["act_enz"]),
                    ("dve", st["dve_qrope"])] if i == 0 else [],
                   sl, q_fm[:, 128 * i:128 * (i + 1)],
                   rsn[:, M * i:M * (i + 1)], bank, stop=False, group=grp)
                MM([], pf[ping][:, 0:128], kfm[p][:, 128 * i:128 * (i + 1)],
                   q_fm[:, 128 * i:128 * (i + 1)], f"pf{ping}", stop=True,
                   inc="pe", group=f"kq:{h}:{i}")
                pe_st = V("pe")
                A("vector", lambda v, ping=ping: v.tensor_mul(
                    stm[:, 128 * ping:128 * (ping + 1)], pf[ping][:, 0:128],
                    trius[:]),
                  wait=[("pe", pe_st), ("pe", stm_rd[ping])], inc="dve")
                dve_st = V("dve")
                bank_read(f"pf{ping}", "dve", dve_st)
                MM([("dve", dve_st)], sl,
                   stm[:, 128 * ping:128 * (ping + 1)],
                   estm[p][:, M * i:M * (i + 1)], bank, stop=True,
                   inc="pe", group=grp)
                stm_rd[ping] = V("pe")
            st["pe_ok"] = V("pe")
            pe_okp[p] = st["pe_ok"]

        def eB3(h):
            st = HS[h]
            p = h % 2
            A("vector", lambda v, p=p: v.scalar_tensor_tensor(
                smf1[:, 0:512], SPb[p][0][:, 0:512], SCALE, enzb[:, 0:512],
                ALU.mult, ALU.mult),
              wait=[("pe", st["pe_ok"]), ("act", st["act_enz"])], inc="dve")
            A("vector", lambda v, p=p: v.scalar_tensor_tensor(
                smf1[:, 512:1024], SPb[p][1][:, 0:512],
                SCALE, enzb[:, 512:1024],
                ALU.mult, ALU.mult),
              inc="dve")
            dve_okz = V("dve")
            bank_read(SPn[p][0], "dve", dve_okz)
            bank_read(SPn[p][1], "dve", dve_okz)
            bank_cycle(*SPn[p])
            A("vector", lambda v: v.reduce_max(
                stat[:, 0:NT], smf1.rearrange("p (i m) -> p i m", m=M),
                axis=AX.X, negate=True),
              wait=[("dve", dve_okz)], inc="dve")
            A("vector", lambda v: v.tensor_tensor(
                smf1.rearrange("p (i m) -> p i m", m=M),
                smf1.rearrange("p (i m) -> p i m", m=M),
                stat[:, 0:NT].unsqueeze(2).broadcast_to([128, NT, M]),
                ALU.add),
              wait=[("dve", V("dve"))], inc="dve")
            dve_xm = V("dve")
            A("scalar", lambda s: s.activation(smf2, smf1, AF.Exp),
              wait=[("dve", dve_xm)], inc="act")
            act_ex = V("act")
            A("vector", lambda v: v.reduce_sum(
                stat2[:, 0:NT], smf2.rearrange("p (i m) -> p i m", m=M),
                axis=AX.X),
              wait=[("act", act_ex)], inc="dve")
            A("vector", lambda v: v.reciprocal(stat[:, 0:NT], stat2[:, 0:NT]),
              wait=[("dve", V("dve"))], inc="dve")
            A("vector", lambda v: v.tensor_tensor(
                smf2.rearrange("p (i m) -> p i m", m=M),
                smf2.rearrange("p (i m) -> p i m", m=M),
                stat[:, 0:NT].unsqueeze(2).broadcast_to([128, NT, M]),
                ALU.mult),
              wait=[("dve", V("dve"))], inc="dve")
            A("vector", lambda v: v.tensor_mul(u_tm[:], smf2, enzb),
              wait=[("dve", V("dve")), ("pe", pe_Cglob)], inc="dve")
            st["dve_u"] = V("dve")
            for j in range(8):
                ping = j % 2
                TP([("dve", st["dve_u"]), ("pe", pe_Cglob)],
                   ptE[ping], u_tm[:, 128 * j:128 * (j + 1)], tpb(ping))
                pe_u = V("pe")
                A("vector", lambda v, j=j, ping=ping: v.tensor_copy(
                    u_fm[:, 128 * j:128 * (j + 1)], ptE[ping]),
                  wait=[("pe", pe_u)], inc="dve")
                uf_ev[ping] = V("dve")
                bank_read(tpb(ping), "dve", uf_ev[ping])
                bank_cycle(tpb(ping))
            st["dve_uf"] = V("dve")

        def eC(h):
            st = HS[h]
            p = h % 2
            obank = SPn[p][0]
            opp = SPb[p][0]
            for i in range(NT):
                ping = i % 2
                pslc = psn[64 * (i % 2):64 * (i % 2) + 64,
                           128 * (i // 2):128 * (i // 2) + 128]
                eslc = esfm[p][64 * (i % 2):64 * (i % 2) + 64,
                              128 * (i // 2):128 * (i // 2) + 128]
                uslc = u_fm[64 * (i % 2):64 * (i % 2) + 64,
                            128 * (i // 2):128 * (i // 2) + 128]
                MM([("dve", st["dve_uf"]), ("dve", st["dve_esf"])]
                   if i == 0 else [],
                   pf[ping][:, 0:128], eslc, uslc, f"pf{ping}", stop=True,
                   inc="pe", group=f"w2:{h}:{i}")
                pe_w2 = V("pe")
                A("vector", lambda v, ping=ping: v.tensor_mul(
                    stm[:, 128 * ping:128 * (ping + 1)], pf[ping][:, 0:128],
                    trius[:]),
                  wait=[("pe", pe_w2), ("pe", stm_rd[ping])], inc="dve")
                dve_w2 = V("dve")
                bank_read(f"pf{ping}", "dve", dve_w2)
                MM([("dve", st["dve_snaps"])], opp[:, 0:128], pslc, uslc,
                   obank, stop=False, group=f"o:{h}:{i}")
                MM([("dve", dve_w2)], opp[:, 0:128],
                   vtm[p][:, 128 * i:128 * (i + 1)],
                   stm[:, 128 * ping:128 * (ping + 1)], obank, stop=True,
                   inc="pe", group=f"o:{h}:{i}")
                stm_rd[ping] = V("pe")
                pe_o = V("pe")
                A("scalar", lambda s, i=i, p=p: s.copy(
                    o_sb[:, 128 * i:128 * (i + 1)], opp[:, 0:128]),
                  wait=[("pe", pe_o), ("so", og_sd)], inc="act")
                o_ev = V("act")
                bank_read(obank, "act", o_ev)
            st["act_o"] = V("act")
            st["pe_C"] = V("pe")
            pe_Cp[p] = st["pe_C"]
            return st["pe_C"]

        def eE(h):
            st = HS[h]
            p = h % 2
            A("scalar", lambda s: s.activation(sq_bf[:], o_sb[:], AF.Square),
              wait=[("act", st["act_o"])], inc="act")
            act_sq = V("act")
            sum_banks = [(SPn[p][0], SPb[p][0], 0), (SPn[p][1], SPb[p][1], 2)]
            bank_cycle(*SPn[p])
            for (bk, pp_, j0) in sum_banks:
                MM([("act", act_sq)] if j0 == 0 else [],
                   pp_[0:1, 0:512], onecb[:],
                   sq_bf[:, 512 * j0:512 * (j0 + 1)],
                   bk, stop=True, inc="pe", group=f"E:{h}:{j0}")
            pe_sq = V("pe")
            for (bk, pp_, j0) in sum_banks:
                A("scalar", lambda s, j0=j0, pp_=pp_: s.activation(
                    rowa[0:1, 512 * j0:512 * (j0 + 1)], pp_[0:1, 0:512],
                    AF.Ln, bias=epsc[0:1, 0:1], scale=1.0 / DV),
                  wait=[("pe", pe_sq), ("dve", st["dve_u"]),
                        ("dve", HS[h + 1].get("dve_carry", 0)
                         if h + 1 < len(HS) else 0)] if j0 == 0 else [],
                  inc="act")
                bank_read(bk, "act", V("act"))
            for (bk, pp_, j0) in sum_banks:
                j1 = j0 + 1
                MM([], pp_[0:1, 0:512], onecb[:],
                   sq_bf[:, 512 * j1:512 * (j1 + 1)],
                   bk, stop=True, inc="pe", group=f"E:{h}:{j1}")
            pe_sq2 = V("pe")
            for (bk, pp_, j0) in sum_banks:
                j1 = j0 + 1
                A("scalar", lambda s, j1=j1, pp_=pp_: s.activation(
                    rowa[0:1, 512 * j1:512 * (j1 + 1)], pp_[0:1, 0:512],
                    AF.Ln, bias=epsc[0:1, 0:1], scale=1.0 / DV),
                  wait=[("pe", pe_sq2)] if j1 == 1 else [],
                  inc="act")
                bank_read(bk, "act", V("act"))
            A("scalar", lambda s: s.activation(
                rowb[0:1, :], rowa[0:1, :], AF.Exp, scale=-0.5),
              wait=[("act", V("act")), ("pe", rowb_rd[0])], inc="act")
            act_rstd = V("act")
            bank_cycle(*SPn[p])
            for j in range(4):
                gb = SPn[p][j % 2]
                gp = SPb[p][j % 2]
                MM([("act", act_rstd)] if j == 0 else [],
                   gp[:, :], gnws[:], rowb[0:1, 512 * j:512 * (j + 1)],
                   gb, stop=True, inc="pe", group=f"g:{h}:{j}")
                pe_g = V("pe")
                rowb_rd[0] = pe_g
                A("scalar", lambda s, j=j, gp=gp: s.copy(
                    sq_bf[:, 512 * j:512 * (j + 1)], gp[:, :]),
                  wait=[("pe", pe_g)], inc="act")
                bank_read(gb, "act", V("act"))
            act_grbc = V("act")
            A("vector", lambda v: v.tensor_mul(o_sb[:], o_sb[:], sg[:]),
              wait=[("act", act_grbc)], inc="dve")
            A("vector", lambda v: v.tensor_mul(o_sb[:], o_sb[:], sq_bf[:]),
              wait=[("dve", V("dve"))], inc="dve")
            dve_og = V("dve")
            st["dve_og"] = dve_og
            A("gpsimd", lambda g, h=h: g.dma_start(
                out=og_d[h].transpose([1, 0, 2]),
                in_=o_sb[:].rearrange("p (c t) -> p c t", t=128)),
              wait=[("dve", dve_og)], inc="so", n=16)
            return V("so")

        for h in range(n_heads + 1):
            live = h < n_heads
            if live:
                if h + 1 < n_heads:
                    A("gpsimd", lambda g, h=h: g.nop(),
                      wait=[("pe", pe_wread[(h + 1) % 2])])
                    sd_w[h + 1] = wload(h + 1, (h + 1) % 2)
                eA1(h)
            if h >= 1:
                eB2(h - 1)
            if live:
                eKV(h)
            if h >= 1:
                eB3(h - 1)
            if live:
                eCS(h, HS[h - 1]["dve_u"] if h >= 1 else 0)
                eES(h)
                eA3(h)
                stg_rd = ePush(h)
            if h >= 1:
                eC(h - 1)
                pe_Cglob = HS[h - 1]["pe_C"]
                og_sd = eE(h - 1)
            if live:
                eQG(h, HS[h - 1]["pe_ok"] if h >= 1 else 0,
                    HS[h - 1]["dve_og"] if h >= 1 else 0)
                eFold(h)
                rcv_rd = HS[h]["dve_c"]

        act_headend = V("act")
        dve_headend = V("dve")
        pe_headend = V("pe")
        sd_og = V("so")
        pe_headloop = V("pe")

        for hh in range(NH):
            A("gpsimd", lambda g, hh=hh: g.dma_start(
                out=hs_fm[:, TC * hh:TC * (hh + 1)], in_=wot[hh]),
              wait=[("pe", pe_wread[(n_heads - 1) % 2])] if hh == 0 else [],
              inc="sv", n=16)
        sd_wo = V("sv")
        A("gpsimd", lambda g: g.nop(),
          wait=[("pe", pe_headloop), ("dve", V("dve")), ("act", V("act"))])
        bank_cycle("pf0")
        A("vector", lambda v: v.memset(sls[:], 0.0),
          wait=[("dve", V("dve"))], inc="dve")
        sls_dve = V("dve")
        attn_ev = 0
        l2t_ev = [0, 0]
        pe_sln = 0
        hstm_sd = 0
        sqbf_rd = pe_headloop
        f32a_rd = dve_headend
        f32b_rd = [dve_headend]
        ogb = [raw, rope2]
        ogb_rd = [pe_headloop, pe_headloop]
        hd_ogs = [0, 0]
        hd_xs = [0, 0]

        def p3_load(tc):
            b = tc % 2
            hosem = "ho" if b == 0 else "ho1"
            A("sync", lambda s, tc=tc, b=b: s.dma_start(
                out=ogb[b][:].rearrange("p (h t) -> p h t", h=NH),
                in_=og_d[:, tc].transpose([1, 0, 2])),
              wait=[("so", sd_og), ("pe", ogb_rd[b]),
                    ("act", act_headend), ("dve", dve_headend)],
              inc=hosem, n=16)
            hd_ogs[b] = (hosem, V(hosem))
            hd_xs[b] = xload(tc, [("pe", pe_headend), ("act", act_headend),
                                  ("hs2", V("hs2"))])

        def p3_wo(tc):
            b = tc % 2
            bank_cycle("pb0", "pb1", "pb2", "pb3")
            for j in range(4):
                for hh in range(NH):
                    MM([hd_ogs[b], ("sv", sd_wo)]
                       if (j == 0 and hh == 0) else [],
                       pb[j][:, :], ogb[b][:, 128 * hh:128 * (hh + 1)],
                       hs_fm[:, TC * hh + 512 * j:TC * hh + 512 * (j + 1)],
                       f"pb{j}", stop=(hh == NH - 1),
                       inc="pe" if (j == 3 and hh == NH - 1) else None)
            ogb_rd[b] = V("pe")
            return ogb_rd[b]

        p3_load(0)
        pe_attns = {0: p3_wo(0)}
        for tc in range(NT):
            buf = tc % 2
            pe_attn = pe_attns[tc]
            hd_xr = hd_xs[buf]
            for j in range(4):
                A("scalar", lambda s, j=j: s.copy(
                    f32a[:, 512 * j:512 * (j + 1)], pb[j][:, :]),
                  wait=[("pe", pe_attn), ("dve", f32a_rd)] if j == 0 else [],
                  inc="act")
                bank_read(f"pb{j}", "act", V("act"))
            attn_ev = V("act")
            if tc + 1 < NT:
                p3_load(tc + 1)
                pe_attns[tc + 1] = p3_wo(tc + 1)

            def ln_stats(src_ap, wait0):
                for q in range(4):
                    A("vector", lambda v, q=q: v.bn_stats(
                        bn6[:, 6 * q:6 * (q + 1)],
                        src_ap[:, 512 * q:512 * (q + 1)]),
                      wait=(wait0 + [("dve", V("dve"))]) if q == 0 else [],
                      inc="dve")
                A("vector", lambda v: v.bn_aggr(stat[:, 2:4], bn6[:]),
                  wait=[("dve", V("dve"))], inc="dve")
                A("scalar", lambda s: s.activation(
                    stat[:, 6:7], stat[:, 3:4], AF.Ln, bias=epsc[:]),
                  wait=[("dve", V("dve"))], inc="act")
                A("scalar", lambda s: s.activation(
                    stat[:, 7:8], stat[:, 6:7], AF.Exp, scale=-0.5),
                  wait=[("act", V("act"))], inc="act")
                return V("act")

            act_r1 = ln_stats(f32a[:], [("act", attn_ev)])
            A("vector", lambda v: v.tensor_scalar(
                f32a[:], f32a[:], stat[:, 2:3], stat[:, 7:8],
                ALU.subtract, ALU.mult),
              wait=[("act", act_r1)], inc="dve")
            A("vector", lambda v, buf=buf: v.tensor_add(
                xf32[buf][:], xf32[buf][:], f32a[:]),
              wait=[("dve", V("dve")), hd_xr], inc="dve")
            dve_x2 = V("dve")
            f32a_rd = dve_x2
            A("sync", lambda s, tc=tc, buf=buf: s.dma_start(
                out=x2_d[tc], in_=xf32[buf][:]),
              wait=[("dve", dve_x2)], inc="hs2", n=16)
            hd_x2st = V("hs2")
            act_r2 = ln_stats(xf32[buf][:], [("dve", dve_x2)])
            A("vector", lambda v, buf=buf: v.tensor_scalar(
                sq_bf[:], xf32[buf][:], stat[:, 2:3], stat[:, 7:8],
                ALU.subtract, ALU.mult),
              wait=[("act", act_r2), ("pe", sqbf_rd)], inc="dve")
            dve_l2 = V("dve")
            xld_guard[buf] = dve_l2
            bank_cycle("pf0")
            for cc in range(16):
                MM([("dve", dve_l2), ("dve", sls_dve)] if cc == 0 else [],
                   pf[0][:, cc:cc + 1], sq_bf[:, 128 * cc:128 * (cc + 1)],
                   onecb[:], "pf0", stop=True,
                   inc="pe" if cc == 15 else None, group=f"pf0:c{cc}")
            pe_sln = V("pe")
            A("vector", lambda v: v.tensor_add(sls[:], sls[:],
                                               pf[0][:, 0:16]),
              wait=[("pe", pe_sln), ("dve", sls_dve)], inc="dve")
            sls_dve = V("dve")
            bank_read("pf0", "dve", sls_dve)
            for cc in range(16):
                ping = cc % 2
                TP([("dve", dve_l2)],
                   ptA[ping], sq_bf[:, 128 * cc:128 * (cc + 1)],
                   tpb(ping))
                pe_t = V("pe")
                A("vector", lambda v, cc=cc, ping=ping: v.tensor_copy(
                    hs_tm[:, 128 * cc:128 * (cc + 1)], ptA[ping]),
                  wait=[("pe", pe_t), ("sl", hstm_sd)], inc="dve")
                l2t_ev[ping] = V("dve")
                bank_read(tpb(ping), "dve", l2t_ev[ping])
                bank_cycle(tpb(ping))
            sqbf_rd = V("pe")
            hstm_act = V("dve")
            A("gpsimd", lambda g, tc=tc: g.dma_start(
                out=l2f_d[tc].transpose([1, 0, 2]),
                in_=hs_tm[:].rearrange("p (c t) -> p c t", t=128)),
              wait=[("dve", hstm_act)], inc="sl", n=16)
            hstm_sd = V("sl")
            if dbg and tc == 0:
                A("sync", lambda s, buf=buf: s.dma_start(
                    out=dbg_outs["dbg_x2"], in_=xf32[buf][:]),
                  wait=[("dve", dve_x2)], inc="sdbg", n=16)
        hd_wo = V("hs2")
        sd_l2f = V("sl")

        dve_sls = sls_dve
        bank_cycle("pf0")
        A("gpsimd", lambda g: g.dma_start(out=sels, in_=sel),
          wait=[("pe", V("pe")), ("act", V("act")), ("dve", V("dve"))],
          inc="ss", n=16)
        A("gpsimd", lambda g: g.dma_start(out=istks, in_=istk),
          inc="ss", n=16)
        sd_sels = V("ss")
        A("gpsimd", lambda g: g.dma_start(out=onerf, in_=oner_f),
          wait=[("dve", xld_guard[0]), ("hs2", hd_wo)], inc="hof", n=16)
        hd_onerf = V("hof")
        A("gpsimd", lambda g: g.dma_start(
            out=sq_in.rearrange("o (p f) -> (o p) f", p=128), in_=sls[:]),
          wait=[("dve", dve_sls)], inc="sq", n=16)
        A("gpsimd", lambda g: g.collective_compute(
            "AllGather", ALU.bypass, replica_groups=RG,
            ins=[sq_in.opt()], outs=[sq_out.opt()]),
          wait=[("sq", V("sq"))], inc="cc")
        cc_sq = V("cc")
        A("gpsimd", lambda g: g.dma_start(
            out=srep[:], in_=sq_out[0:1, :].rearrange("o (p f) -> (o p) f",
                                                      p=128)),
          wait=[("cc", cc_sq)], inc="sq", n=16)
        A("gpsimd", lambda g: g.dma_start(
            out=sls[:], in_=sq_out[1:2, :].rearrange("o (p f) -> (o p) f",
                                                     p=128)),
          wait=[("dve", dve_sls)], inc="sq", n=16)
        sd_sq = V("sq")
        A("vector", lambda v: v.tensor_add(srep[:], srep[:], sls[:]),
          wait=[("sq", sd_sq)], inc="dve")
        A("vector", lambda v: v.tensor_scalar(
            srep[:], srep[:], 1.0 / T, None, ALU.mult),
          wait=[("dve", V("dve"))], inc="dve")
        dve_srep = V("dve")
        A("gpsimd", lambda g: g.dma_start(
            out=sim_sb.rearrange("p (c e) -> p c e", e=NE),
            in_=simm.transpose([1, 0, 2])),
          wait=[("dve", xld_guard[1])], inc="ss", n=16)
        sd_sim = V("ss")
        A("scalar", lambda s: s.activation(f32a[:, 0:16 * NE], sim_sb,
                                           AF.Square),
          wait=[("ss", sd_sim), ("dve", f32a_rd)], inc="act")
        act_simsq = V("act")
        for cc in range(16):
            A("vector", lambda v, cc=cc: v.tensor_scalar(
                f32b[:, NE * cc:NE * (cc + 1)],
                sim_sb[:, NE * cc:NE * (cc + 1)],
                srep[:, cc:cc + 1], None, ALU.mult),
              wait=[("dve", dve_srep), ("ss", sd_sim),
                    ("dve", f32b_rd[0])] if cc == 0 else [],
              inc="dve")
        dve_ws = V("dve")
        bank_cycle("pf0", "pf1", "pb0", "pb1")
        for cc in range(16):
            MM([("dve", dve_ws)] if cc == 0 else [],
               pf[0][0:1, 0:NE], onecf[:], f32b[:, NE * cc:NE * (cc + 1)],
               "pf0", stop=(cc == 15), inc="pe" if cc == 15 else None)
            MM([("act", act_simsq)] if cc == 0 else [],
               pf[1][0:1, 0:NE], onecf[:], f32a[:, NE * cc:NE * (cc + 1)],
               "pf1", stop=(cc == 15), inc="pe" if cc == 15 else None)
            MM([("dve", dve_ws)] if cc == 0 else [],
               pb[0][0:NE, 0:1], f32b[:, NE * cc:NE * (cc + 1)], onecf[:],
               "pb0", stop=(cc == 15), inc="pe" if cc == 15 else None)
            MM([("act", act_simsq)] if cc == 0 else [],
               pb[1][0:NE, 0:1], f32a[:, NE * cc:NE * (cc + 1)], onecf[:],
               "pb1", stop=(cc == 15), inc="pe" if cc == 15 else None)
        pe_aff = V("pe")
        A("scalar", lambda s: s.activation(stat[:, 0:16], srep[:], AF.Square),
          wait=[("dve", dve_srep)], inc="act")
        act_ss = V("act")
        bank_cycle("pb2")
        MM([("act", act_ss)], pb[2][0:1, 0:16], onecf[:], stat[:, 0:16],
           "pb2", stop=True, inc="pe")
        pe_sn = V("pe")
        A("vector", lambda v: v.tensor_copy(grow[0:1, 0:16], pb[2][0:1, 0:16]),
          wait=[("pe", pe_sn)], inc="dve")
        bank_read("pb2", "dve", V("dve"))
        A("vector", lambda v: v.reduce_sum(gcol[0:1, 0:1], grow[0:1, 0:16],
                                           axis=AX.X),
          wait=[("dve", V("dve"))], inc="dve")
        A("scalar", lambda s: s.activation(gcol[0:1, 1:2], gcol[0:1, 0:1],
                                           AF.Ln),
          wait=[("dve", V("dve"))], inc="act")
        A("scalar", lambda s: s.activation(gcol[0:1, 2:3], gcol[0:1, 1:2],
                                           AF.Exp, scale=-0.5),
          wait=[("act", V("act"))], inc="act")
        act_rn = V("act")
        def gr(j):
            return grow[0:1, NE * j + 32:NE * (j + 1) + 32]
        A("scalar", lambda s: s.activation(grow[0:1, 0:NE], pf[1][0:1, 0:NE],
                                           AF.Ln),
          wait=[("pe", pe_aff)], inc="act")
        A("scalar", lambda s: s.activation(grow[0:1, 0:NE], grow[0:1, 0:NE],
                                           AF.Exp, scale=-0.5),
          wait=[("act", V("act"))], inc="act")
        act_cnr = V("act")
        bank_read("pf1", "act", act_cnr)
        A("scalar", lambda s: s.activation(gcol[:, 3:4], pb[1][0:NE, 0:1],
                                           AF.Ln),
          wait=[("pe", pe_aff)], inc="act")
        A("scalar", lambda s: s.activation(gcol[:, 3:4], gcol[:, 3:4],
                                           AF.Exp, scale=-0.5),
          wait=[("act", V("act"))], inc="act")
        act_cnc = V("act")
        bank_read("pb1", "act", act_cnc)
        A("vector", lambda v: v.tensor_mul(gr(0), pf[0][0:1, 0:NE],
                                           grow[0:1, 0:NE]),
          wait=[("act", act_cnr)], inc="dve")
        bank_read("pf0", "dve", V("dve"))
        A("vector", lambda v: v.tensor_scalar(
            gr(0), gr(0), gcol[0:1, 2:3], None, ALU.mult),
          wait=[("dve", V("dve")), ("act", act_rn)], inc="dve")
        dve_affr = V("dve")
        A("vector", lambda v: v.tensor_mul(gcol[:, 0:1], pb[0][0:NE, 0:1],
                                           gcol[:, 3:4]),
          wait=[("act", act_cnc)], inc="dve")
        bank_read("pb0", "dve", V("dve"))
        A("vector", lambda v: v.tensor_copy(gcbf[:, 0:1], gcol[:, 0:1]),
          wait=[("dve", V("dve"))], inc="dve")
        dve_acbf = V("dve")
        bank_cycle("pb3")
        MM([("dve", dve_acbf), ("act", act_rn), ("hof", hd_onerf)],
           pb[3][0:NE, 0:1],
           onerf[0:1, 0:NE], gcol[0:1, 2:3], "pb3", stop=True, inc="pe")
        pe_rnb = V("pe")
        A("vector", lambda v: v.tensor_mul(gcol[:, 0:1], gcol[:, 0:1],
                                           pb[3][0:NE, 0:1]),
          wait=[("pe", pe_rnb)], inc="dve")
        dve_affc = V("dve")
        bank_read("pb3", "dve", dve_affc)
        A("sync", lambda s: s.dma_start(out=gr(1), in_=gates),
          inc="hg", n=16)
        hd_g = V("hg")
        A("scalar", lambda s: s.activation(gr(1), gr(1), AF.Tanh, scale=0.5),
          wait=[("hg", hd_g)], inc="act")
        A("vector", lambda v: v.tensor_scalar(
            gr(1), gr(1), 0.5, 0.5, ALU.mult, ALU.add),
          wait=[("act", V("act"))], inc="dve")
        dve_sigr = V("dve")
        bank_cycle("pb3")
        MM([("dve", dve_sigr)], pb[3][0:NE, 0:1], gr(1),
           onecf[0:1, 0:1], "pb3", stop=True, inc="pe")
        pe_sigc = V("pe")
        A("vector", lambda v: v.tensor_copy(gcol[:, 1:2], pb[3][0:NE, 0:1]),
          wait=[("pe", pe_sigc)], inc="dve")
        bank_read("pb3", "dve", V("dve"))
        A("vector", lambda v: v.tensor_sub(gr(2), gr(0), gr(1)),
          wait=[("dve", dve_affr), ("dve", V("dve"))], inc="dve")
        A("vector", lambda v: v.tensor_relu(gr(3), gr(2)),
          wait=[("dve", V("dve"))], inc="dve")
        A("vector", lambda v: v.tensor_single_scalar(
            gr(4), gr(3), 0.0, ALU.is_gt),
          wait=[("dve", V("dve"))], inc="dve")
        dve_maskr = V("dve")
        A("vector", lambda v: v.tensor_sub(gcol[:, 2:3], gcol[:, 0:1],
                                           gcol[:, 1:2]),
          wait=[("dve", dve_affc)], inc="dve")
        dve_logc = V("dve")
        A("vector", lambda v: v.memset(sim_sb[0:32, 0:NE], 1.0),
          wait=[("act", act_simsq), ("dve", dve_ws)], inc="dve")
        A("vector", lambda v: v.tensor_scalar(
            sim_sb[0:32, 0:NE], sim_sb[0:32, 0:NE], gcol[0:32, 2:3], None,
            ALU.mult),
          wait=[("dve", V("dve")), ("dve", dve_logc)], inc="dve")
        dve_cbc = V("dve")
        bank_cycle("pb3")
        MM([("dve", dve_logc)], pb[3][0:NE, 0:NE], onerf[0:1, 0:NE],
           gr(2), "pb3", stop=True, inc="pe")
        pe_rbc = V("pe")
        A("vector", lambda v: v.tensor_tensor(
            sim_sb[0:32, NE:2 * NE], pb[3][0:NE, 0:NE], sim_sb[0:32, 0:NE],
            ALU.is_gt),
          wait=[("pe", pe_rbc), ("dve", dve_cbc)], inc="dve")
        bank_read("pb3", "dve", V("dve"))
        A("vector", lambda v: v.reduce_sum(
            gcol[:, 4:5], sim_sb[0:32, NE:2 * NE], axis=AX.X),
          wait=[("dve", V("dve"))], inc="dve")
        xld_guard[1] = V("dve")
        A("vector", lambda v: v.tensor_single_scalar(
            gcol[:, 5:6], gcol[:, 4:5], float(NE // 2) - 0.5, ALU.is_le),
          wait=[("dve", V("dve"))], inc="dve")
        dve_fbc = V("dve")
        A("vector", lambda v: v.reduce_sum(gcol[0:1, 6:7], gr(4), axis=AX.X),
          wait=[("dve", dve_maskr)], inc="dve")
        A("vector", lambda v: v.tensor_single_scalar(
            gcol[0:1, 7:8], gcol[0:1, 6:7], 0.5, ALU.is_lt),
          wait=[("dve", V("dve"))], inc="dve")
        dve_flag = V("dve")
        bank_cycle("pb3")
        MM([("dve", dve_flag)], pb[3][0:NE, 0:1], onerf[0:1, 0:NE],
           gcol[0:1, 7:8], "pb3", stop=True, inc="pe")
        pe_flb = V("pe")
        A("vector", lambda v: v.tensor_mul(gcol[:, 5:6], gcol[:, 5:6],
                                           pb[3][0:NE, 0:1]),
          wait=[("pe", pe_flb), ("dve", dve_fbc)], inc="dve")
        bank_read("pb3", "dve", V("dve"))
        xld_guard[0] = V("dve")
        A("vector", lambda v: v.tensor_single_scalar(
            gcol[:, 8:9], gcol[:, 2:3], 0.0, ALU.is_gt),
          wait=[("dve", dve_logc)], inc="dve")
        A("vector", lambda v: v.tensor_max(gcol[:, 8:9], gcol[:, 8:9],
                                           gcol[:, 5:6]),
          wait=[("dve", V("dve"))], inc="dve")
        dve_mfc = V("dve")
        A("vector", lambda v: v.tensor_copy(gcbf[:, 1:2], gcol[:, 8:9]),
          wait=[("dve", dve_mfc)], inc="dve")
        dve_mfb = V("dve")
        bank_cycle("pb3")
        MM([("dve", dve_mfb)], pb[3][0:1, 0:NE], gcbf[:, 1:2],
           cons[0:32, 0:NE], "pb3", stop=True, inc="pe")
        pe_mfr = V("pe")
        A("vector", lambda v: v.tensor_copy(gr(5), pb[3][0:1, 0:NE]),
          wait=[("pe", pe_mfr)], inc="dve")
        bank_read("pb3", "dve", V("dve"))
        A("vector", lambda v: v.tensor_mul(gr(6), gr(3), gr(5)),
          wait=[("dve", V("dve"))], inc="dve")
        A("vector", lambda v: v.tensor_scalar(
            gr(7), gr(5), -1.0, 1e30, ALU.add, ALU.mult),
          wait=[("dve", V("dve"))], inc="dve")
        A("vector", lambda v: v.tensor_add(gr(6), gr(6), gr(7)),
          wait=[("dve", V("dve"))], inc="dve")
        A("vector", lambda v: v.reduce_max(gcol[0:1, 9:10], gr(6),
                                           axis=AX.X, negate=True),
          wait=[("dve", V("dve"))], inc="dve")
        dve_ml = V("dve")
        A("scalar", lambda s: s.activation(gr(7), gr(6), AF.Exp,
                                           bias=gcol[0:1, 9:10]),
          wait=[("dve", dve_ml)], inc="act")
        A("vector", lambda v: v.reduce_sum(gcol[0:1, 10:11], gr(7), axis=AX.X),
          wait=[("act", V("act"))], inc="dve")
        A("vector", lambda v: v.reciprocal(gcol[0:1, 11:12],
                                           gcol[0:1, 10:11]),
          wait=[("dve", V("dve"))], inc="dve")
        A("vector", lambda v: v.tensor_scalar(
            gr(7), gr(7), gcol[0:1, 11:12], None, ALU.mult),
          wait=[("dve", V("dve"))], inc="dve")
        A("vector", lambda v: v.tensor_mul(gr(7), gr(7), gr(5)),
          wait=[("dve", V("dve"))], inc="dve")
        dve_r = V("dve")
        if dbg:
            A("gpsimd", lambda g: g.dma_start(out=dbg_outs["dbg_r"],
                                              in_=gr(7)),
              wait=[("dve", dve_r)], inc="sdbg", n=16)
        A("vector", lambda v: v.tensor_copy(rowb[0:1, 0:NE], gr(7)),
          wait=[("dve", dve_r), ("act", V("act")), ("sl", sd_l2f)], inc="dve")
        dve_rbf = V("dve")
        f32a_rd = dve_rbf
        bank_cycle("pb3")
        MM([("dve", dve_rbf)], pb[3][0:NE, 0:1], rowb[0:1, 0:NE],
           onecb[0:1, 0:1], "pb3", stop=True, inc="pe")
        pe_rc = V("pe")
        A("vector", lambda v: v.tensor_copy(gcbf[:, 0:1], pb[3][0:NE, 0:1]),
          wait=[("pe", pe_rc)], inc="dve")
        dve_rcol = V("dve")
        bank_read("pb3", "dve", dve_rcol)
        bank_cycle("pb1")
        for kc in range(16):
            MM([("dve", dve_rcol), ("ss", sd_sim)] if kc == 0 else [],
               pb[1][:, kc:kc + 1], sels[:, 128 * kc:128 * (kc + 1)],
               gcbf[:, 0:1], "pb1", stop=True,
               inc="pe" if kc == 15 else None, group=f"pb1:rt{kc}")
        pe_rt = V("pe")
        A("vector", lambda v: v.tensor_copy(stat[:, 0:16], pb[1][:, 0:16]),
          wait=[("pe", pe_rt)], inc="dve")
        dve_rt = V("dve")
        bank_read("pb1", "dve", dve_rt)
        for kc in range(16):
            A("vector", lambda v, kc=kc: v.tensor_scalar(
                rwv_t[:, M * kc:M * (kc + 1)], istks[:], stat[:, kc:kc + 1],
                None, ALU.mult),
              wait=[("dve", dve_rt), ("pe", pe_headloop),
                    ("ss", sd_sim)] if kc == 0 else [],
              inc="dve")
        dve_rwv = V("dve")
        pe_wo_done = V("pe")
        for kc in range(16):
            A("gpsimd", lambda g, kc=kc: g.dma_start(
                out=hs_fm[:, TC * kc:TC * (kc + 1)], in_=wv2[kc]),
              wait=[("pe", pe_wo_done)] if kc == 0 else [], inc="sv", n=16)
        sd_wv = V("sv")
        bank_cycle("pf0", "pf1")
        wef_pv = [0, 0]
        for cc in range(16):
            ping = cc % 2
            for kc in range(16):
                MM([("sv", sd_wv), ("dve", dve_rwv)] if kc == 0 else [],
                   pf[ping][:, 0:64],
                   hs_fm[:, TC * kc + 128 * cc:TC * kc + 128 * (cc + 1)],
                   rwv_t[:, M * kc:M * (kc + 1)], f"pf{ping}", stop=(kc == 15),
                   inc="pe" if kc == 15 else None)
            pe_wef = V("pe")
            A("scalar", lambda s, cc=cc, ping=ping: s.copy(
                wefft_t[:, M * cc:M * (cc + 1)], pf[ping][:, 0:64]),
              wait=[("pe", pe_wef)], inc="act")
            wef_pv[ping] = V("act")
            bank_read(f"pf{ping}", "act", wef_pv[ping])
            bank_cycle(f"pf{ping}")
        act_wef = V("act")
        pe_wef_all = V("pe")
        if dbg:
            A("gpsimd", lambda g: g.dma_start(out=dbg_outs["dbg_weff"],
                                              in_=wefft_t[:, 0:16 * M]),
              wait=[("act", act_wef)], inc="sdbg", n=16)
        for kc in range(16):
            A("gpsimd", lambda g, kc=kc: g.dma_start(
                out=hs_fm[:, TC * kc:TC * (kc + 1)], in_=wo2[kc]),
              wait=[("pe", pe_wef_all)] if kc == 0 else [], inc="sv", n=16)
        sd_wo2 = V("sv")
        for dc in range(4):
            ping = dc % 2
            for kc in range(16):
                MM([("sv", sd_wo2)] if kc == 0 else [],
                   pf[ping][0:64, :], rwv_t[:, M * kc:M * (kc + 1)],
                   hs_fm[:, TC * kc + 512 * dc:TC * kc + 512 * (dc + 1)],
                   f"pf{ping}", stop=(kc == 15),
                   inc="pe" if kc == 15 else None)
            pe_wod = V("pe")
            A("scalar", lambda s, dc=dc, ping=ping: s.copy(
                wodyn_t[0:64, 512 * dc:512 * (dc + 1)], pf[ping][0:64, :]),
              wait=[("pe", pe_wod)], inc="act")
            bank_read(f"pf{ping}", "act", V("act"))
            bank_cycle(f"pf{ping}")
        act_wod = V("act")
        if dbg:
            A("gpsimd", lambda g: g.dma_start(out=dbg_outs["dbg_wod"],
                                              in_=wodyn_t[0:64, 0:D]),
              wait=[("act", act_wod)], inc="sdbg", n=16)

        l2l_guard = [pe_headloop, pe_headloop]
        f32ab_rd = [f32a_rd, f32a_rd]
        comb_fin_rd = [0, 0]
        fin_ev = 0
        out_dma = 0
        for tc in range(NT):
            buf = tc % 2
            A("sync", lambda s, tc=tc, buf=buf: s.dma_start(
                out=l2fs[buf][:].rearrange("p (c t) -> p c t", t=128),
                in_=l2f_d[tc].transpose([1, 0, 2])),
              wait=[("sl", sd_l2f), ("pe", l2l_guard[buf])],
              inc=f"hl{buf}", n=16)
            hd_l2 = (f"hl{buf}", V(f"hl{buf}"))
            A("sync", lambda s, tc=tc, buf=buf: s.dma_start(
                out=xf32[buf][:], in_=x2_d[tc]),
              wait=[("hs2", hd_wo), ("dve", xld_guard[buf]),
                    ("hout", V("hout"))],
              inc=f"hx{buf}", n=16)
            hd_x2t = (f"hx{buf}", V(f"hx{buf}"))
            cb_bank = f"pf{buf}"
            for cc in range(16):
                MM([hd_l2, ("act", act_wef)] if cc == 0 else [],
                   pf[buf][0:64, 0:128], wefft_t[:, M * cc:M * (cc + 1)],
                   l2fs[buf][:, 128 * cc:128 * (cc + 1)], cb_bank,
                   stop=(cc == 15), inc="pe" if cc == 15 else None,
                   group=f"cb:{tc}")
            pe_comb = V("pe")
            l2l_guard[buf] = pe_comb
            cslot = comb_t[0:64, 128 * buf:128 * (buf + 1)]
            A("scalar", lambda s, cslot=cslot, buf=buf: s.copy(
                cslot, pf[buf][0:64, 0:128]),
              wait=[("pe", pe_comb), ("pe", comb_fin_rd[buf]),
                    ("dve", dve_headend)], inc="act")
            act_comb = V("act")
            if dbg and tc == 0:
                A("gpsimd", lambda g, cslot=cslot: g.dma_start(
                    out=dbg_outs["dbg_comb"], in_=cslot),
                  wait=[("act", act_comb)], inc="sdbg", n=16)
                A("gpsimd", lambda g, buf=buf: g.dma_start(
                    out=dbg_outs["dbg_l2f"], in_=l2fs[buf][:]),
                  wait=[hd_l2], inc="sdbg", n=16)
            bank_read(cb_bank, "act", act_comb)
            bank_cycle("pb0", "pb1", "pb2", "pb3")
            for j in range(4):
                MM([("act", act_comb), ("act", act_wod)] if j == 0 else [],
                   pb[j][:, :], cslot,
                   wodyn_t[0:64, 512 * j:512 * (j + 1)], f"pb{j}", stop=True,
                   inc="pe" if j == 3 else None)
            pe_fin = V("pe")
            comb_fin_rd[buf] = pe_fin
            fbuf = [f32a, f32b][buf]
            for j in range(4):
                A("scalar", lambda s, j=j, fbuf=fbuf: s.copy(
                    fbuf[:, 512 * j:512 * (j + 1)], pb[j][:, :]),
                  wait=[("pe", pe_fin), ("dve", f32ab_rd[buf]),
                        ("pe", pe_aff)] if j == 0 else [],
                  inc="act")
                bank_read(f"pb{j}", "act", V("act"))
            fin_ev = V("act")
            if dbg and tc == 0:
                A("gpsimd", lambda g, fbuf=fbuf: g.dma_start(
                    out=dbg_outs["dbg_ffn"], in_=fbuf[:]),
                  wait=[("act", fin_ev)], inc="sdbg", n=16)

            act_r3 = ln_stats(fbuf[:], [("act", fin_ev)])
            A("vector", lambda v, fbuf=fbuf: v.tensor_scalar(
                fbuf[:], fbuf[:], stat[:, 2:3], stat[:, 7:8],
                ALU.subtract, ALU.mult),
              wait=[("act", act_r3)], inc="dve")
            A("vector", lambda v, buf=buf, fbuf=fbuf: v.tensor_add(
                xf32[buf][:], xf32[buf][:], fbuf[:]),
              wait=[("dve", V("dve")), hd_x2t], inc="dve")
            dve_out = V("dve")
            xld_guard[buf] = dve_out
            f32ab_rd[buf] = dve_out
            A("sync", lambda s, tc=tc, buf=buf: s.dma_start(
                out=out[128 * tc:128 * (tc + 1), :], in_=xf32[buf][:]),
              wait=[("dve", dve_out)], inc="hout", n=16)
            out_dma = V("hout")

        A("sync", lambda s: s.nop(),
          wait=[("hout", out_dma), ("cc", V("cc"))] +
               [(s_, V(s_)) for s_ in SEMS if s_ not in
                ("pe", "act", "dve", "cc", "hout")])

        _DIAG["em"] = em
        block = e(nc.Block())
        em.replay(block, sems)

    return nc


_CACHE = {}


def _host_inputs(inputs):
    def wt(W):
        W = np.ascontiguousarray(np.asarray(W, np.float32))
        return np.ascontiguousarray(
            W.T.reshape(16, 128, 16, 128).transpose(2, 1, 0, 3))

    base = {
        "wqt": wt(inputs["L0_attn_Wq"]),
        "wkt": wt(inputs["L0_attn_Wk"]),
        "wvt": wt(inputs["L0_attn_Wv"]),
        "wgt": wt(inputs["L0_attn_Wg"]),
        "wot": np.ascontiguousarray(
            np.asarray(inputs["L0_attn_Wo"], np.float32).T).reshape(16, 128, D),
        "ws1t": np.ascontiguousarray(
            np.asarray(inputs["L0_attn_Ws1"], np.float32)
            .T.reshape(16, 128, 16).transpose(1, 0, 2)),
        "ws2t": np.ascontiguousarray(np.asarray(inputs["L0_attn_Ws2"],
                                                np.float32).T),
        "bs2": np.asarray(inputs["L0_attn_bs2"], np.float32).reshape(1, -1),
        "gnw": np.asarray(inputs["L0_attn_g_norm_weight"],
                          np.float32).reshape(1, -1),
        "simm": np.ascontiguousarray(
            np.asarray(inputs["L0_ffn_sim_matrix"],
                       np.float32).reshape(16, 128, NE)),
        "gates": np.asarray(inputs["L0_ffn_gates"], np.float32).reshape(1, -1),
        "wv2": np.ascontiguousarray(
            np.asarray(inputs["L0_ffn_w_v"], np.float32).reshape(16, 128, D)),
        "wo2": np.ascontiguousarray(
            np.asarray(inputs["L0_ffn_o_weights"],
                       np.float32).reshape(16, 128, D)),
    }

    x = np.asarray(inputs["L0_attn_hidden_states"], np.float32)
    inv = 1.0 / 10000.0 ** (np.arange(0, DK, 2, dtype=np.float64) / DK)
    maps = []
    for c in range(8):
        b, half = c // 2, c % 2
        pos = np.arange(half * TC, (half + 1) * TC, dtype=np.float64)
        ang = pos[None, :] * inv[:, None]
        cos = np.cos(ang).astype(np.float32)
        sin = np.sin(ang).astype(np.float32)
        m = dict(base)
        m["xin"] = np.ascontiguousarray(x[b, half * TC:(half + 1) * TC])
        m["cosf"] = np.concatenate([cos, cos], 0)
        m["sins"] = np.concatenate([sin, -sin], 0)
        m["spref"] = np.full((128, 1), float(half), np.float32)
        maps.append(m)
    return maps


def kernel(**inputs):
    _ = inputs["L0_ffn_w_q"], inputs["L0_ffn_w_k"]
    if "nc" not in _CACHE:
        _CACHE["nc"] = build_nc()
    nc = _CACHE["nc"]
    maps = _host_inputs(inputs)
    res = run_bass_kernel_spmd(nc, maps, core_ids=list(range(8)))
    out = np.empty((B, T, D), np.float32)
    for c in range(8):
        b, half = c // 2, c % 2
        out[b, half * TC:(half + 1) * TC] = res.results[c]["out"]
    return out
